# version 24
# baseline (speedup 1.0000x reference)
import math
import numpy as np
import ml_dtypes
from contextlib import ExitStack

import concourse.bass as bass
import concourse.mybir as mybir
from concourse.bass_utils import run_bass_kernel_spmd

F32 = mybir.dt.float32
BF16 = mybir.dt.bfloat16
AF = mybir.ActivationFunctionType
ALU = mybir.AluOpType
AX = mybir.AxisListType

L = 2048
D = 1024
H = 8
DK = 128
F = 4096
E = 8
NT = L // 128
ND = D // 128
NF = F // 128
W = 2176
SOFT_SHIFT = -20.0
QSCALE = DK ** -0.5


def positions_np():
    from scipy.special import gammaln
    d = np.arange(-L + 1, L, dtype=np.float64)
    n = 192 // 6
    absd = np.abs(d)[:, None]
    half_life = 2.0 ** np.linspace(3.0, math.log(L) / math.log(2.0), n)
    feat_exp = np.exp(-math.log(2.0) / half_life[None, :] * absd)
    center_widths = 2.0 ** np.arange(1, n + 1, dtype=np.float64) - 1.0
    feat_cm = (center_widths[None, :] > absd).astype(np.float64)
    stddev = L / (2.0 * n)
    mean = np.linspace(L / n, float(L), n)[None, :]
    conc = (mean / stddev) ** 2
    rate = mean / stddev ** 2
    with np.errstate(divide="ignore"):
        log_absd = np.where(absd > 0, np.log(np.where(absd > 0, absd, 1.0)), -np.inf)
    log_unnorm = (conc - 1.0) * log_absd - rate * absd
    log_norm = gammaln(conc) - conc * np.log(rate)
    probs = np.exp(log_unnorm - log_norm) + 1e-08
    feat_gamma = probs / np.max(probs, axis=-1, keepdims=True)
    emb = np.concatenate([feat_exp, feat_cm, feat_gamma], axis=-1)
    emb = np.concatenate([emb, np.sign(d)[:, None] * emb], axis=-1)
    return emb.astype(np.float32)


CDBG = set()


def build(max_phase=99, debug_dumps=False, cnit=None):
    nc = bass.Bass("TRN2", target_bir_lowering=False, detect_race_conditions=False)

    x_d = nc.dram_tensor("x", [L, D], F32, kind="ExternalInput")
    wqs_d = nc.dram_tensor("w_qs", [D, D], F32, kind="ExternalInput")
    wks_d = nc.dram_tensor("w_ks", [D, D], F32, kind="ExternalInput")
    wvs_d = nc.dram_tensor("w_vs", [D, D], F32, kind="ExternalInput")
    wrel_d = nc.dram_tensor("w_rel", [192, D], F32, kind="ExternalInput")
    reb_d = nc.dram_tensor("reb", [H, DK], F32, kind="ExternalInput")
    rpb_d = nc.dram_tensor("rpb", [H, DK], F32, kind="ExternalInput")
    wout_d = nc.dram_tensor("w_out", [D, D], F32, kind="ExternalInput")
    bout_d = nc.dram_tensor("b_out", [1, D], F32, kind="ExternalInput")
    wi_d = nc.dram_tensor("wi", [F, D], F32, kind="ExternalInput")
    bi_d = nc.dram_tensor("bi", [1, F], F32, kind="ExternalInput")
    wo_d = nc.dram_tensor("wo", [D, F], F32, kind="ExternalInput")
    bo_d = nc.dram_tensor("bo", [1, D], F32, kind="ExternalInput")
    lA_d = nc.dram_tensor("lora_A", [E * 16, D], F32, kind="ExternalInput")
    lB_d = nc.dram_tensor("lora_B", [E * F, 16], F32, kind="ExternalInput")
    gw_d = nc.dram_tensor("gate_w", [E, D], F32, kind="ExternalInput")
    gb_d = nc.dram_tensor("gate_b", [1, E], F32, kind="ExternalInput")
    posT_d = nc.dram_tensor("posT", [192, 4096], BF16, kind="ExternalInput")
    out_d = nc.dram_tensor("out", [L, D], F32, kind="ExternalOutput")

    bounce_d = [nc.dram_tensor(f"bounce{i}", [128, W], BF16, kind="Internal") for i in range(2)]
    rkt_d = nc.dram_tensor("rkt", [H * 128, 4096], BF16, kind="Internal")
    intpm_d = nc.dram_tensor("intpm", [E * 32, L], BF16, kind="Internal")
    btp_d = nc.dram_tensor("btp", [E * 32, F], BF16, kind="Internal")
    s_dram = nc.dram_tensor("s_dram", [L, D], BF16, kind="Internal")
    wiT_d = nc.dram_tensor("wiT_d", [ND * 128, F], BF16, kind="Internal")
    woT_d = nc.dram_tensor("woT_d", [NF * 128, D], BF16, kind="Internal")

    ident_bf_d = nc.inline_tensor(np.eye(128, dtype=ml_dtypes.bfloat16), "ident_bf_d")
    ident_f_d = nc.inline_tensor(np.eye(128, dtype=np.float32), "ident_f_d")
    ones_row_d = nc.inline_tensor(np.ones((1, 128), dtype=ml_dtypes.bfloat16), "ones_row_d")

    es_persist = ExitStack()
    P = es_persist.enter_context

    ident_bf = P(nc.sbuf_tensor("ident_bf", [128, 128], BF16))
    ident_f = P(nc.sbuf_tensor("ident_f", [128, 128], F32))
    ones_row = P(nc.sbuf_tensor("ones_row", [1, 128], BF16))
    rebT = P(nc.sbuf_tensor("rebT", [128, H], F32))
    rpbT = P(nc.sbuf_tensor("rpbT", [128, H], F32))
    gwT = P(nc.sbuf_tensor("gwT", [128, ND * E], BF16))
    Apm = P(nc.sbuf_tensor("Apm", [128, ND * 256], BF16))
    bout_bf = P(nc.sbuf_tensor("bout_bf", [1, D], BF16))
    gb_bf = P(nc.sbuf_tensor("gb_bf", [1, E], BF16))
    G_sb = P(nc.sbuf_tensor("G_sb", [128, NT * E], F32))
    gsumrow = P(nc.sbuf_tensor("gsumrow", [1, L], BF16))
    stats = P(nc.sbuf_tensor("stats", [128, 2 * NT], F32))
    tiny = P(nc.sbuf_tensor("tiny", [128, 16], F32))
    cm20 = P(nc.sbuf_tensor("cm20", [128, 1], F32))
    cqs = P(nc.sbuf_tensor("cqs", [128, 1], F32))

    SEMN = ["sA", "sB", "sC", "sD", "sE", "sF", "sG", "sH", "sI", "sJ",
            "sK", "sL", "sM", "sN", "sO", "sP", "sQ", "sR", "sS", "sT",
            "sU", "sV", "sW", "sX"]
    CHN = ["vch", "sch"]
    DSEMN = ["A1s", "A1g", "A2r", "A2b", "A3l0", "A3l1", "A3s0", "A3s1",
             "Bx0", "Bx1", "Bwv", "Bwqk", "Bint",
             "Cbo0", "Cbo1", "Cpi0", "Cpi1", "Crk0", "Crk1",
             "Swo", "Sst0", "Sst1",
             "DwiT", "DwoT", "Dbt0", "Dbt1", "Dbt2", "Dbt3", "Dxs0", "Dxs1",
             "Dip0", "Dip1", "Dout0", "Dout1"]
    es_sem = ExitStack()
    sm = {n: es_sem.enter_context(nc.semaphore(n)) for n in SEMN + CHN}
    dm = {n: es_sem.enter_context(nc.semaphore(n)) for n in DSEMN}

    def dump_block(pairs):
        out_flat = out_d
        with nc.Block() as blk:
            @blk.gpsimd
            def _(gp):
                k = [0]
                for off, srcap, n in pairs:
                    row0, col0 = divmod(off, D)
                    assert col0 == 0 and n % D == 0
                    dst = out_flat[row0:row0 + n // D, :]
                    gp.dma_start(dst, srcap).then_inc(dm["Dout0"], 16)
                    k[0] += 16
                gp.wait_ge(dm["Dout0"], k[0])
    def clear_block():
        with nc.Block() as blk:
            @blk.sync
            def _(sy):
                for n in SEMN + CHN:
                    sy.sem_clear(sm[n])

    esA = ExitStack()
    posTA = esA.enter_context(nc.sbuf_tensor("posTA", [128, 4096], BF16))
    posTB = esA.enter_context(nc.sbuf_tensor("posTB", [64, 4096], BF16))
    wrelA = esA.enter_context(nc.sbuf_tensor("wrelA", [128, D], BF16))
    wrelB = esA.enter_context(nc.sbuf_tensor("wrelB", [64, D], BF16))
    rkt_stage = esA.enter_context(nc.sbuf_tensor("rkt_stage", [128, 4096], BF16))
    An_sb = esA.enter_context(nc.sbuf_tensor("An_sb", [128, D], BF16))
    AT_sb = esA.enter_context(nc.sbuf_tensor("AT_sb", [128, ND * 128], BF16))
    Bn_sb = esA.enter_context(nc.sbuf_tensor("Bn_sb", [128, E * 512], BF16))
    BT_sb = esA.enter_context(nc.sbuf_tensor("BT_sb", [16, F], BF16))
    BT_z = esA.enter_context(nc.sbuf_tensor("BT_z", [16, F], BF16))
    gwn = esA.enter_context(nc.sbuf_tensor("gwn", [E, D], BF16))
    rebn = esA.enter_context(nc.sbuf_tensor("rebn", [H, DK], F32))
    rpbn = esA.enter_context(nc.sbuf_tensor("rpbn", [H, DK], F32))
    brow_f = esA.enter_context(nc.sbuf_tensor("brow_f", [1, D], F32))

    pA1 = esA.enter_context(nc.psum_tensor("pA1", [128, 512], F32))
    pA2 = esA.enter_context(nc.psum_tensor("pA2", [128, 512], F32))
    pAt = esA.enter_context(nc.psum_tensor("pAt", [128, 128], F32))
    pAtb = esA.enter_context(nc.psum_tensor("pAtb", [16, 2048], BF16))
    pAab = esA.enter_context(nc.psum_tensor("pAab", [128, 128], BF16))
    pAg = esA.enter_context(nc.psum_tensor("pAg", [128, E], BF16))

    with nc.Block() as blk:
        @blk.sync
        def _(sy):
            sy.dma_start(posTA[:, :], posT_d[0:128, :]).then_inc(dm["A1s"], 16)
            sy.dma_start(posTB[:, :], posT_d[128:192, :]).then_inc(dm["A1s"], 16)
            sy.dma_start(rebn[:, :], reb_d[:, :]).then_inc(dm["A1s"], 16)
            sy.dma_start(rpbn[:, :], rpb_d[:, :]).then_inc(dm["A1s"], 16)
            sy.dma_start(brow_f[:, :], bout_d[:, :]).then_inc(dm["A1s"], 16)

        @blk.gpsimd
        def _(gp):
            gp.dma_start(wrelA[:, :], wrel_d[0:128, :]).then_inc(dm["A1g"], 16)
            gp.dma_start(wrelB[:, :], wrel_d[128:192, :]).then_inc(dm["A1g"], 16)
            gp.dma_start(An_sb[:, :], lA_d[:, :]).then_inc(dm["A1g"], 16)
            gp.dma_start(gwn[:, :], gw_d[:, :]).then_inc(dm["A1g"], 16)
            gp.dma_start(gb_bf[:, :], gb_d[:, :]).then_inc(dm["A1g"], 16)
            for e in range(E):
                src = bass.AP(lB_d, e * F * 16, [[16, 128], [128 * 16, 32], [1, 16]])
                dstap = bass.AP(Bn_sb, e * 512, [[E * 512, 128], [16, 32], [1, 16]])
                gp.dma_start(dstap, src).then_inc(dm["A1g"], 16)
            gp.dma_start(ident_bf[:, :], ident_bf_d[:, :]).then_inc(dm["A1g"], 16)
            gp.dma_start(ident_f[:, :], ident_f_d[:, :]).then_inc(dm["A1g"], 16)
            gp.dma_start(ones_row[:, :], ones_row_d[:, :]).then_inc(dm["A1g"], 16)

    with nc.Block() as blk:
        @blk.tensor
        def _(t):
            t.wait_ge(dm["A1s"], 16 * 5)
            t.wait_ge(dm["A1g"], 16 * 16)
            mmv = 0
            for h in range(H):
                for ch in range(8):
                    pb = [pA1, pA2][mmv % 2]
                    if mmv >= 2:
                        t.wait_ge(sm["sD"], mmv - 1)
                    t.matmul(pb[:, :], wrelA[:, h * 128:(h + 1) * 128],
                             posTA[:, ch * 512:(ch + 1) * 512], start=True, stop=False)
                    t.matmul(pb[:, :], wrelB[:, h * 128:(h + 1) * 128],
                             posTB[:, ch * 512:(ch + 1) * 512],
                             start=False, stop=True).then_inc(sm["sC"], 1)
                    mmv += 1
            nc.tensor.transpose(pAt[:, 0:8], rebn[:, :], ident_f[0:8, 0:8])
            nc.tensor.transpose(pAt[:, 8:16], rpbn[:, :],
                                ident_f[0:8, 0:8]).then_inc(sm["sE"], 1)
            for dt in range(ND):
                nc.tensor.transpose(pAg[:, :], gwn[:, dt * 128:(dt + 1) * 128],
                                    ident_bf[0:8, 0:8]).then_inc(sm["sF"], 1)
                t.wait_ge(sm["sG"], dt + 1)
            for dt in range(ND):
                nc.tensor.transpose(pAab[:, :], An_sb[:, dt * 128:(dt + 1) * 128],
                                    ident_bf[:, :]).then_inc(sm["sH"], 1)
                t.wait_ge(sm["sI"], dt + 1)
            for e in range(E):
                for half in range(2):
                    for tt in range(16):
                        tg = half * 16 + tt
                        ins = nc.tensor.transpose(
                            pAtb[:, tt * 128:(tt + 1) * 128],
                            Bn_sb[:, e * 512 + tg * 16: e * 512 + tg * 16 + 16],
                            ident_bf[:, :])
                        if tt == 15:
                            ins.then_inc(sm["sJ"], 1)
                    t.wait_ge(sm["sK"], e * 2 + half + 1)

        @blk.vector
        def _(v):
            for h in range(H):
                if h >= 1:
                    v.wait_ge(dm["A2r"], 16 * h)
                for ch in range(8):
                    mmv = h * 8 + ch
                    v.wait_ge(sm["sC"], mmv + 1)
                    v.tensor_copy(rkt_stage[:, ch * 512:(ch + 1) * 512],
                                  [pA1, pA2][mmv % 2][:, :]).then_inc(sm["sD"], 1)
            v.wait_ge(sm["sE"], 1)
            v.tensor_copy(rebT[:, :], pAt[:, 0:8])
            v.tensor_copy(rpbT[:, :], pAt[:, 8:16])
            for dt in range(ND):
                v.wait_ge(sm["sF"], dt + 1)
                v.tensor_copy(gwT[:, dt * E:(dt + 1) * E], pAg[:, :]).then_inc(sm["sG"], 1)
            for dt in range(ND):
                v.wait_ge(sm["sH"], dt + 1)
                v.tensor_copy(AT_sb[:, dt * 128:(dt + 1) * 128],
                              pAab[:, :]).then_inc(sm["sI"], 1)
            v.memset(BT_z[:, :], 0.0).then_inc(sm["sL"], 1)
            for e in range(E):
                if e >= 1:
                    v.wait_ge(dm["A2b"], 16 * (2 * e + 1))
                for half in range(2):
                    v.wait_ge(sm["sJ"], e * 2 + half + 1)
                    v.tensor_copy(BT_sb[:, half * 2048:(half + 1) * 2048],
                                  pAtb[:, :]).then_inc(sm["sK"], 1)
            for dt in range(ND):
                for e in range(E):
                    base = dt * 256 + e * 32
                    v.tensor_copy(Apm[:, base:base + 16],
                                  AT_sb[:, dt * 128 + e * 16: dt * 128 + e * 16 + 16])
                    if e == 0:
                        v.memset(Apm[:, base + 16:base + 32], 0.0)
                    else:
                        v.tensor_scalar(Apm[:, base + 16:base + 32],
                                        AT_sb[:, dt * 128 + (e - 1) * 16: dt * 128 + e * 16],
                                        -1.0, None, ALU.mult)
            v.wait_ge(dm["A1s"], 16 * 5)
            v.tensor_copy(bout_bf[:, :], brow_f[:, :])
            v.memset(cm20[:, :], SOFT_SHIFT)
            v.memset(cqs[:, :], QSCALE)

        @blk.gpsimd
        def _(gp):
            for h in range(H):
                gp.wait_ge(sm["sD"], h * 8 + 8)
                gp.dma_start(rkt_d[h * 128:(h + 1) * 128, :],
                             rkt_stage[:, :]).then_inc(dm["A2r"], 16)
            gp.wait_ge(sm["sL"], 1)
            gp.dma_start(btp_d[16:32, :], BT_z[:, :]).then_inc(dm["A2b"], 16)
            for e in range(E):
                gp.wait_ge(sm["sK"], 2 * (e + 1))
                gp.dma_start(btp_d[e * 32:e * 32 + 16, :],
                             BT_sb[:, :]).then_inc(dm["A2b"], 16)
                if e < E - 1:
                    gp.dma_start(btp_d[(e + 1) * 32 + 16:(e + 2) * 32, :],
                                 BT_sb[:, :]).then_inc(dm["A2b"], 16)
            gp.wait_ge(dm["A2b"], 16 * 16)
            gp.wait_ge(dm["A2r"], 16 * H)

    esA.close()
    clear_block()
    if max_phase <= 1:
        if debug_dumps:
            dump_block([(0, rkt_d[0:128, :], 128 * 4096),
                        (512 * 1024, btp_d[0:64, :], 64 * F),
                        (768 * 1024, intpm_d[0:32, :], 32 * L)])
        es_persist.close()
        es_sem.close()
        return nc


    esA3 = ExitStack()
    natst = [esA3.enter_context(nc.sbuf_tensor(f"natst{i}", [128, 4096], BF16)) for i in range(2)]
    tost = [esA3.enter_context(nc.sbuf_tensor(f"tost{i}", [128, 1024], BF16)) for i in range(2)]
    pW = [esA3.enter_context(nc.psum_tensor(f"pW{i}", [128, 1024], BF16)) for i in range(2)]

    with nc.Block() as blk:
        @blk.gpsimd
        def _(gp):
            nld = 0
            for r in range(8):
                if nld >= 2:
                    gp.wait_ge(sm["sC"], 4 * (nld - 1))
                src = bass.AP(wi_d, r * 512 * D, [[D, 128], [128 * D, 4], [1, D]])
                dstap = bass.AP(natst[nld % 2], 0, [[4096, 128], [1024, 4], [1, D]])
                gp.dma_start(dstap, src).then_inc(dm["A3l0" if nld % 2 == 0 else "A3l1"], 16)
                nld += 1
            for dt in range(8):
                if nld >= 2:
                    gp.wait_ge(sm["sC"], 4 * (nld - 1))
                src = bass.AP(wo_d, dt * 128 * F, [[F, 128], [1, F]])
                gp.dma_start(natst[nld % 2][:, :],
                             src).then_inc(dm["A3l0" if nld % 2 == 0 else "A3l1"], 16)
                nld += 1

        @blk.tensor
        def _(t):
            nb = 0
            for r in range(8):
                t.wait_ge(dm["A3l0" if r % 2 == 0 else "A3l1"], 16 * (r // 2 + 1))
                for ftl in range(4):
                    if nb >= 2:
                        t.wait_ge(sm["sC"], nb - 1)
                    for dt in range(8):
                        ins = nc.tensor.transpose(
                            pW[nb % 2][:, dt * 128:(dt + 1) * 128],
                            natst[r % 2][:, ftl * 1024 + dt * 128: ftl * 1024 + (dt + 1) * 128],
                            ident_bf[:, :])
                        if dt == 7:
                            ins.then_inc(sm["sD"], 1)
                    nb += 1
            for dtw in range(8):
                nw = 8 + dtw
                t.wait_ge(dm["A3l0" if nw % 2 == 0 else "A3l1"], 16 * (nw // 2 + 1))
                for b4 in range(4):
                    if nb >= 2:
                        t.wait_ge(sm["sC"], nb - 1)
                    for ftl in range(8):
                        ins = nc.tensor.transpose(
                            pW[nb % 2][:, ftl * 128:(ftl + 1) * 128],
                            natst[nw % 2][:, b4 * 1024 + ftl * 128: b4 * 1024 + (ftl + 1) * 128],
                            ident_bf[:, :])
                        if ftl == 7:
                            ins.then_inc(sm["sD"], 1)
                    nb += 1

        @blk.vector
        def _(v):
            for nb in range(64):
                v.wait_ge(sm["sD"], nb + 1)
                if nb >= 2:
                    v.wait_ge(dm["A3s0" if nb % 2 == 0 else "A3s1"],
                              16 * (nb // 2))
                v.tensor_copy(tost[nb % 2][:, :], pW[nb % 2][:, :]).then_inc(sm["sC"], 1)

        @blk.sync
        def _(sy):
            nb = 0
            for r in range(8):
                for ftl in range(4):
                    sy.wait_ge(sm["sC"], nb + 1)
                    dst = bass.AP(wiT_d, (r * 4 + ftl) * 128,
                                  [[F, 128], [128 * F, 8], [1, 128]])
                    sy.dma_start(dst, bass.AP(tost[nb % 2], 0,
                                              [[1024, 128], [128, 8], [1, 128]])
                                 ).then_inc(dm["A3s0" if nb % 2 == 0 else "A3s1"], 16)
                    nb += 1
            for dtw in range(8):
                for b4 in range(4):
                    sy.wait_ge(sm["sC"], nb + 1)
                    dst = bass.AP(woT_d, (b4 * 8) * 128 * D + dtw * 128,
                                  [[D, 128], [128 * D, 8], [1, 128]])
                    sy.dma_start(dst, bass.AP(tost[nb % 2], 0,
                                              [[1024, 128], [128, 8], [1, 128]])
                                 ).then_inc(dm["A3s0" if nb % 2 == 0 else "A3s1"], 16)
                    nb += 1
            sy.wait_ge(dm["A3s0"], 16 * 32)
            sy.wait_ge(dm["A3s1"], 16 * 32)

    esA3.close()
    clear_block()
    if max_phase <= 2:
        if debug_dumps:
            dump_block([(0, bass.AP(wiT_d, 0, [[F, 256], [1, F]]), 256 * F),
                        (1024 * 1024, bass.AP(woT_d, 0, [[D, 1024], [1, D]]), 1024 * D)])
        es_persist.close()
        es_sem.close()
        return nc


    es_bc = ExitStack()
    QsT = es_bc.enter_context(nc.sbuf_tensor("QsT", [128, ND * L], BF16))
    KT = es_bc.enter_context(nc.sbuf_tensor("KT", [128, ND * L], BF16))
    V_sb = es_bc.enter_context(nc.sbuf_tensor("V_sb", [128, NT * D], BF16))
    coT = es_bc.enter_context(nc.sbuf_tensor("coT", [128, ND * L], BF16))

    esB = ExitStack()
    xnT = esB.enter_context(nc.sbuf_tensor("xnT", [128, ND * L], BF16))

    esB1 = ExitStack()
    x_t = [esB1.enter_context(nc.sbuf_tensor(f"x_t{i}", [128, D], F32)) for i in range(2)]
    xn_nat = [esB1.enter_context(nc.sbuf_tensor(f"xn_nat{i}", [128, D], BF16)) for i in range(2)]
    sq_sc = [esB1.enter_context(nc.sbuf_tensor(f"sq_sc{i}", [128, D], F32)) for i in range(2)]
    stg = esB1.enter_context(nc.sbuf_tensor("stg", [128, 32], F32))
    Lg = esB1.enter_context(nc.sbuf_tensor("Lg", [128, NT * E], F32))
    g8b = esB1.enter_context(nc.sbuf_tensor("g8b", [128, 4 * NT * E], F32))
    gsc = esB1.enter_context(nc.sbuf_tensor("gsc", [128, 6 * NT], F32))
    pB_xn = esB1.enter_context(nc.psum_tensor("pB_xn", [128, ND * 128], BF16))
    pB_g = esB1.enter_context(nc.psum_tensor("pB_g", [128, E], F32))
    pB_gs = esB1.enter_context(nc.psum_tensor("pB_gs", [1, 128], F32))

    def b3(tens, off, n2, n3):
        return bass.AP(tens, off, [[tens.shape[1], 128], [n3, n2], [1, n3]])

    with nc.Block() as blk:
        @blk.sync
        def _(sy):
            for p in range(2):
                for tt in range(NT):
                    k = p * NT + tt
                    if 2 <= k and k - 2 < NT:
                        sy.wait_ge(sm["sD"], k - 1)
                    elif k - 2 >= NT:
                        sy.wait_ge(sm["sH"], k - NT - 1)
                    sy.dma_start(x_t[k % 2][:, :],
                                 x_d[tt * 128:(tt + 1) * 128, :]
                                 ).then_inc(dm["Bx0" if k % 2 == 0 else "Bx1"], 16)

        @blk.vector
        def _(v):
            vc = [0]
            def vfence(ins):
                ins.then_inc(sm["vch"], 1)
                vc[0] += 1
                v.wait_ge(sm["vch"], vc[0])
            for tt in range(NT):
                v.wait_ge(dm["Bx0" if tt % 2 == 0 else "Bx1"], 16 * (tt // 2 + 1))
                xt = x_t[tt % 2]
                v.reduce_sum(stg[:, tt:tt + 1], xt[:, :], axis=AX.X)
                vfence(v.tensor_mul(sq_sc[tt % 2][:, :], xt[:, :], xt[:, :]))
                ins = v.reduce_sum(stg[:, NT + tt:NT + tt + 1], sq_sc[tt % 2][:, :], axis=AX.X)
                ins.then_inc(sm["sD"], 1)
            v.tensor_scalar(gsc[:, 0:NT], stg[:, 0:NT], 1.0 / D, None, ALU.mult)
            vfence(v.tensor_scalar(gsc[:, NT:2 * NT], stg[:, NT:2 * NT], 1.0 / D, None, ALU.mult))
            vfence(v.tensor_mul(gsc[:, 2 * NT:3 * NT], gsc[:, 0:NT], gsc[:, 0:NT]))
            vfence(v.tensor_sub(gsc[:, NT:2 * NT], gsc[:, NT:2 * NT], gsc[:, 2 * NT:3 * NT]))
            vfence(v.tensor_scalar(gsc[:, NT:2 * NT], gsc[:, NT:2 * NT], 1e-5, None, ALU.add))
            ins = v.reciprocal(gsc[:, 2 * NT:3 * NT], gsc[:, NT:2 * NT])
            ins.then_inc(sm["sE"], 1)
            v.wait_ge(sm["sF"], 1)
            v.tensor_mul(stats[:, NT:2 * NT], gsc[:, 0:NT],
                         stats[:, 0:NT]).then_inc(sm["vch"], 1)
            vc[0] += 1
            v.wait_ge(sm["vch"], vc[0])
            v.tensor_scalar(stats[:, NT:2 * NT], stats[:, NT:2 * NT], -1.0,
                            None, ALU.mult).then_inc(sm["sG"], 1)
            for tt in range(NT):
                v.wait_ge(sm["sI"], tt + 1)
                dst = bass.AP(xnT, tt * 128, [[ND * L, 128], [L, ND], [1, 128]])
                srcp = bass.AP(pB_xn, 0, [[ND * 128, 128], [128, ND], [1, 128]])
                v.tensor_copy(dst, srcp).then_inc(sm["sJ"], 1)
                v.wait_ge(sm["sK"], tt + 1)
                v.tensor_copy(Lg[:, tt * E:(tt + 1) * E],
                              pB_g[:, :]).then_inc(sm["sL"], 1)
            E8, SC, MK, CT = 0, NT * E, 2 * NT * E, 3 * NT * E
            m1c, m2c, vsc, s8c = 3 * NT, 4 * NT, 5 * NT, NT
            vfence(v.reduce_max(gsc[:, m1c:m1c + NT], b3(Lg, 0, NT, E), axis=AX.X))
            mb = bass.AP(gsc, m1c, [[6 * NT, 128], [1, NT], [0, E]])
            ins = v.tensor_sub(b3(g8b, E8, NT, E), b3(Lg, 0, NT, E), mb)
            ins.then_inc(sm["sU"], 1)
            v.wait_ge(sm["sM"], 1)
            vfence(v.reduce_sum(gsc[:, s8c:s8c + NT], b3(g8b, E8, NT, E), axis=AX.X))
            vfence(v.reciprocal(gsc[:, 2 * NT:3 * NT], gsc[:, s8c:s8c + NT]))
            rb = bass.AP(gsc, 2 * NT, [[6 * NT, 128], [1, NT], [0, E]])
            vfence(v.tensor_mul(b3(g8b, SC, NT, E), b3(g8b, E8, NT, E), rb))
            vfence(v.reduce_max(gsc[:, m1c:m1c + NT], b3(g8b, SC, NT, E), axis=AX.X))
            m1b = bass.AP(gsc, m1c, [[6 * NT, 128], [1, NT], [0, E]])
            vfence(v.tensor_tensor(b3(g8b, MK, NT, E), b3(g8b, SC, NT, E), m1b, ALU.is_equal))
            vfence(v.scalar_tensor_tensor(b3(g8b, MK, NT, E), b3(g8b, MK, NT, E), -1e30,
                                          b3(g8b, SC, NT, E), ALU.mult, ALU.add))
            vfence(v.reduce_max(gsc[:, m2c:m2c + NT], b3(g8b, MK, NT, E), axis=AX.X))
            vfence(v.tensor_add(gsc[:, vsc:vsc + NT], gsc[:, m1c:m1c + NT],
                                gsc[:, m2c:m2c + NT]))
            m2b = bass.AP(gsc, m2c, [[6 * NT, 128], [1, NT], [0, E]])
            vfence(v.tensor_tensor(b3(g8b, CT, NT, E), b3(g8b, SC, NT, E), m2b, ALU.is_ge))
            vsb = bass.AP(gsc, vsc, [[6 * NT, 128], [1, NT], [0, E]])
            vfence(v.tensor_tensor(b3(G_sb, 0, NT, E), b3(g8b, CT, NT, E), vsb, ALU.mult))
            v.tensor_scalar(gsc[:, vsc:vsc + NT], gsc[:, vsc:vsc + NT], 2.0,
                            None, ALU.mult).then_inc(sm["sN"], 1)
            for tt in range(NT):
                v.wait_ge(sm["sO"], tt + 1)
                v.tensor_copy(gsumrow[0:1, tt * 128:(tt + 1) * 128],
                              pB_gs[0:1, :]).then_inc(sm["sP"], 1)

        @blk.scalar
        def _(s):
            s.wait_ge(sm["sE"], 1)
            s.activation(stats[:, 0:NT], gsc[:, 2 * NT:3 * NT],
                         AF.Sqrt).then_inc(sm["sF"], 1)
            for tt in range(NT):
                k = NT + tt
                s.wait_ge(dm["Bx0" if k % 2 == 0 else "Bx1"], 16 * (k // 2 + 1))
                if tt == 0:
                    s.wait_ge(sm["sG"], 1)
                if tt >= 2:
                    s.wait_ge(sm["sI"], tt - 1)
                nc.scalar.activation(xn_nat[k % 2][:, :], x_t[k % 2][:, :], AF.Identity,
                                     bias=stats[:, NT + tt:NT + tt + 1],
                                     scale=stats[:, tt:tt + 1]).then_inc(sm["sH"], 1)
            s.wait_ge(sm["sL"], NT)
            s.wait_ge(sm["sU"], 1)
            nc.scalar.activation(g8b[:, 0:NT * E], g8b[:, 0:NT * E],
                                 AF.Exp).then_inc(sm["sM"], 1)

        @blk.tensor
        def _(t):
            for tt in range(NT):
                k = NT + tt
                t.wait_ge(sm["sH"], tt + 1)
                if tt >= 1:
                    t.wait_ge(sm["sJ"], tt)
                for ct in range(ND):
                    ins = nc.tensor.transpose(pB_xn[:, ct * 128:(ct + 1) * 128],
                                              xn_nat[k % 2][:, ct * 128:(ct + 1) * 128],
                                              ident_bf[:, :])
                    if ct == ND - 1:
                        ins.then_inc(sm["sI"], 1)
                if tt >= 1:
                    t.wait_ge(sm["sL"], tt)
                t.wait_ge(sm["sJ"], tt + 1)
                for ct in range(ND):
                    t.matmul(pB_g[:, :], xnT[:, ct * L + tt * 128: ct * L + (tt + 1) * 128],
                             gwT[:, ct * E:(ct + 1) * E], start=(ct == 0), stop=False)
                t.matmul(pB_g[:, :], ones_row[:, :], gb_bf[:, :],
                         start=False, stop=True).then_inc(sm["sK"], 1)
            t.wait_ge(sm["sN"], 1)
            for tt in range(NT):
                if tt >= 1:
                    t.wait_ge(sm["sP"], tt)
                nc.tensor.transpose(pB_gs[:, :], gsc[:, 5 * NT + tt:5 * NT + tt + 1],
                                    ident_f[:, :]).then_inc(sm["sO"], 1)

    esB1.close()
    clear_block()

    esB2v = ExitStack()
    wv_sb = esB2v.enter_context(nc.sbuf_tensor("wv_sb", [128, ND * D], BF16))
    pB_v = esB2v.enter_context(nc.psum_tensor("pB_v", [128, D], F32))

    with nc.Block() as blk:
        @blk.gpsimd
        def _(gp):
            srcap = bass.AP(wvs_d, 0, [[D, 128], [128 * D, ND], [1, D]])
            dstap = bass.AP(wv_sb, 0, [[ND * D, 128], [D, ND], [1, D]])
            gp.dma_start(dstap, srcap).then_inc(dm["Bwv"], 16)

        @blk.tensor
        def _(t):
            t.wait_ge(dm["Bwv"], 16)
            for tt in range(NT):
                for ch in range(2):
                    for ct in range(ND):
                        ins = t.matmul(pB_v[:, ch * 512:(ch + 1) * 512],
                                       xnT[:, ct * L + tt * 128: ct * L + (tt + 1) * 128],
                                       wv_sb[:, ct * D + ch * 512: ct * D + ch * 512 + 512],
                                       start=(ct == 0), stop=(ct == ND - 1))
                        if ch == 1 and ct == ND - 1:
                            ins.then_inc(sm["sP"], 1)
                t.wait_ge(sm["sQ"], tt + 1)

        @blk.vector
        def _(v):
            for tt in range(NT):
                v.wait_ge(sm["sP"], tt + 1)
                v.tensor_copy(V_sb[:, tt * D:(tt + 1) * D], pB_v[:, :]).then_inc(sm["sQ"], 1)

    esB2v.close()
    clear_block()
    if max_phase <= 5:
        esB.close()
        es_bc.close()
        es_persist.close()
        es_sem.close()
        return nc


    esB2 = ExitStack()
    wq_sb = esB2.enter_context(nc.sbuf_tensor("wq_sb", [128, ND * D], BF16))
    wk_sb = esB2.enter_context(nc.sbuf_tensor("wk_sb", [128, ND * D], BF16))
    pB_qk = esB2.enter_context(nc.psum_tensor("pB_qk", [128, L], F32))

    with nc.Block() as blk:
        @blk.gpsimd
        def _(gp):
            for (dst, srcd) in ((wq_sb, wqs_d), (wk_sb, wks_d)):
                srcap = bass.AP(srcd, 0, [[D, 128], [128 * D, ND], [1, D]])
                dstap = bass.AP(dst, 0, [[ND * D, 128], [D, ND], [1, D]])
                gp.dma_start(dstap, srcap).then_inc(dm["Bwqk"], 16)

        @blk.tensor
        def _(t):
            t.wait_ge(dm["Bwqk"], 16 * 2)
            cnt = 0
            for wsb in (wq_sb, wk_sb):
                for dt in range(ND):
                    for ch in range(4):
                        for ct in range(ND):
                            ins = t.matmul(pB_qk[:, ch * 512:(ch + 1) * 512],
                                           wsb[:, ct * D + dt * 128: ct * D + (dt + 1) * 128],
                                           xnT[:, ct * L + ch * 512: ct * L + (ch + 1) * 512],
                                           start=(ct == 0), stop=(ct == ND - 1))
                            if ch == 3 and ct == ND - 1:
                                ins.then_inc(sm["sO"], 1)
                    cnt += 1
                    t.wait_ge(sm["sN"], cnt)

        @blk.scalar
        def _(s):
            for dt in range(ND):
                s.wait_ge(sm["sO"], dt + 1)
                nc.scalar.activation(QsT[:, dt * L:(dt + 1) * L], pB_qk[:, :], AF.Identity,
                                     bias=0.0, scale=cqs[:, :]).then_inc(sm["sN"], 1)

        @blk.vector
        def _(v):
            for dt in range(ND):
                v.wait_ge(sm["sO"], ND + dt + 1)
                v.tensor_copy(KT[:, dt * L:(dt + 1) * L], pB_qk[:, :]).then_inc(sm["sN"], 1)

    esB2.close()
    clear_block()
    if max_phase <= 6:
        esB.close()
        es_bc.close()
        es_persist.close()
        es_sem.close()
        return nc


    esB3 = ExitStack()
    pB_i = esB3.enter_context(nc.psum_tensor("pB_i", [128, L], F32))
    int_st = esB3.enter_context(nc.sbuf_tensor("int_st", [128, L], BF16))

    with nc.Block() as blk:
        @blk.tensor
        def _(t):
            for p in range(2):
                for ch in range(4):
                    for ct in range(ND):
                        ins = t.matmul(pB_i[:, ch * 512:(ch + 1) * 512],
                                       Apm[:, ct * 256 + p * 128: ct * 256 + (p + 1) * 128],
                                       xnT[:, ct * L + ch * 512: ct * L + (ch + 1) * 512],
                                       start=(ct == 0), stop=(ct == ND - 1))
                        if ch == 3 and ct == ND - 1:
                            ins.then_inc(sm["sR"], 1)
                t.wait_ge(sm["sS"], p + 1)

        @blk.vector
        def _(v):
            for p in range(2):
                v.wait_ge(sm["sR"], p + 1)
                if p >= 1:
                    v.wait_ge(dm["Bint"], 16 * p)
                v.tensor_copy(int_st[:, :], pB_i[:, :]).then_inc(sm["sS"], 1)

        @blk.gpsimd
        def _(gp):
            for p in range(2):
                gp.wait_ge(sm["sS"], p + 1)
                gp.dma_start(intpm_d[p * 128:(p + 1) * 128, :],
                             int_st[:, :]).then_inc(dm["Bint"], 16)
            gp.wait_ge(dm["Bint"], 16 * 2)

    esB3.close()
    esB.close()
    clear_block()
    if max_phase <= 7:
        if debug_dumps:
            dump_block([(0, intpm_d[:, :], 256 * L),
                        (512 * 1024, QsT[:, 0:2048], 128 * 2048),
                        (768 * 1024, KT[:, 0:2048], 128 * 2048),
                        (1024 * 1024, V_sb[:, 0:1024], 128 * 1024),
                        (1152 * 1024, bass.AP(G_sb, 0, [[NT * E, 128], [1, 128]]), 128 * 128),
                        (1168 * 1024, gsumrow[0:1, :], 2048)])
        es_bc.close()
        es_persist.close()
        es_sem.close()
        return nc


    esC = ExitStack()
    rkt_sb = [esC.enter_context(nc.sbuf_tensor(f"rkt_sb{i}", [128, 4096], BF16)) for i in range(2)]
    qeT = esC.enter_context(nc.sbuf_tensor("qeT", [128, L], BF16))
    qpT = esC.enter_context(nc.sbuf_tensor("qpT", [128, L], BF16))
    band_sb = [esC.enter_context(nc.sbuf_tensor(f"band_sb{i}", [128, W], BF16)) for i in range(2)]
    P_sb = [esC.enter_context(nc.sbuf_tensor(f"P_sb{i}", [128, L], BF16)) for i in range(2)]
    A_sb = [esC.enter_context(nc.sbuf_tensor(f"A_sb{i}", [128, L], BF16)) for i in range(2)]
    AT4 = esC.enter_context(nc.sbuf_tensor("AT4", [128, 16 * 512], BF16))
    sums_sb = esC.enter_context(nc.sbuf_tensor("sums_sb", [128, 4], F32))

    pC_S = esC.enter_context(nc.psum_tensor("pC_S", [128, L], F32))
    pC_b = [esC.enter_context(nc.psum_tensor(f"pC_b{i}", [128, 512], F32)) for i in range(2)]
    pC_at = esC.enter_context(nc.psum_tensor("pC_at", [128, 1024], BF16))
    pC_av = esC.enter_context(nc.psum_tensor("pC_av", [128, 512], F32))

    CH_W = [512, 512, 512, 512, 128]
    CH_OFF = [0, 512, 1024, 1536, 2048]
    NIT = H * NT
    if cnit is not None:
        NIT = cnit
    VCH = (0, 1)
    SCH = (2, 3, 4)

    def bevac_req(n):
        b, c = divmod(n, 5)
        if c in VCH:
            return ("sB", 2 * b + c + 1)
        return ("sC", 3 * b + c - 1)

    with nc.Block() as blk:
        @blk.tensor
        def _(t):
            nchunk = [0]

            def emit_band():
                n = nchunk[0]
                if n >= 5 * NIT:
                    return
                bi, c = divmod(n, 5)
                hb, jb = divmod(bi, NT)
                if c == 0 and jb == 0:
                    t.wait_ge(sm["sA"], hb + 1)
                    t.wait_ge(dm["Crk0" if hb % 2 == 0 else "Crk1"],
                              16 * (hb // 2 + 1))
                if n >= 2:
                    sname, cntv = bevac_req(n - 2)
                    t.wait_ge(sm[sname], cntv)
                j0 = 1920 - 128 * jb
                t.matmul(pC_b[n % 2][:, 0:CH_W[c]], qpT[:, jb * 128:(jb + 1) * 128],
                         rkt_sb[hb % 2][:, j0 + CH_OFF[c]: j0 + CH_OFF[c] + CH_W[c]],
                         start=True, stop=True).then_inc(sm["sD"], 1)
                nchunk[0] += 1

            def trans_batch(jj, half):
                if half == 0:
                    t.wait_ge(sm["sJ"], jj + 1)
                t.wait_ge(sm["sK"], 2 * jj + half)
                for kt in range(8):
                    ktg = half * 8 + kt
                    ins = nc.tensor.transpose(pC_at[:, kt * 128:(kt + 1) * 128],
                                              A_sb[jj % 2][:, ktg * 128:(ktg + 1) * 128],
                                              ident_bf[:, :])
                    if kt == 7:
                        ins.then_inc(sm["sL"], 1)

            def av_group(j):
                g = j // 4
                hh = j // NT
                t.wait_ge(sm["sK"], 2 * (j + 1))
                if g >= 1:
                    t.wait_ge(sm["sM"], g)
                for kt in range(16):
                    ins = t.matmul(pC_av[:, :],
                                   V_sb[:, kt * D + hh * 128: kt * D + (hh + 1) * 128],
                                   AT4[:, kt * 512:(kt + 1) * 512],
                                   start=(kt == 0), stop=(kt == 15))
                    if kt == 15:
                        ins.then_inc(sm["sP"], 1)

            for _n in range(15):
                emit_band()
            for i in range(NIT):
                h, j = divmod(i, NT)
                t.wait_ge(dm["Cpi0" if i % 2 == 0 else "Cpi1"],
                          16 * (i // 2 + 1))
                if j == 0:
                    t.wait_ge(sm["sF"], h + 1)
                for ch in range(4):
                    if i >= 1 and ch == 0:
                        t.wait_ge(sm["sH"], 2 * i - 1)
                    if i >= 1 and ch == 2:
                        t.wait_ge(sm["sH"], 2 * i)
                    t.matmul(pC_S[:, ch * 512:(ch + 1) * 512],
                             qeT[:, j * 128:(j + 1) * 128],
                             KT[:, h * L + ch * 512: h * L + (ch + 1) * 512],
                             start=True, stop=False)
                for ch in range(4):
                    ins = t.matmul(pC_S[:, ch * 512:(ch + 1) * 512], ident_bf[:, :],
                                   P_sb[i % 2][:, ch * 512:(ch + 1) * 512],
                                   start=False, stop=True)
                    if ch == 1:
                        ins.then_inc(sm["sT"], 1)
                    if ch == 3:
                        ins.then_inc(sm["sI"], 1)
                if i >= 1:
                    trans_batch(i - 1, 0)
                emit_band()
                emit_band()
                if i >= 1:
                    trans_batch(i - 1, 1)
                if i >= 1 and (i - 1) % 4 == 3:
                    av_group(i - 1)
                emit_band()
                emit_band()
                emit_band()
            trans_batch(NIT - 1, 0)
            trans_batch(NIT - 1, 1)
            av_group(NIT - 1)

        @blk.scalar
        def _(s):
            s.dma_start(rkt_sb[0][:, :], rkt_d[0:128, :]).then_inc(dm["Crk0"], 16)
            s.dma_start(rkt_sb[1][:, :], rkt_d[128:256, :]).then_inc(dm["Crk1"], 16)
            s.activation(qpT[:, :], QsT[:, 0:L], AF.Identity,
                         bias=rpbT[:, 0:1], scale=1.0).then_inc(sm["sA"], 1)
            s.activation(qeT[:, :], QsT[:, 0:L], AF.Identity,
                         bias=rebT[:, 0:1], scale=1.0).then_inc(sm["sF"], 1)
            for bi0 in (0, 1, 2):
                for c in SCH:
                    n = bi0 * 5 + c
                    s.wait_ge(sm["sD"], n + 1)
                    s.activation(band_sb[bi0 % 2][:, CH_OFF[c]:CH_OFF[c] + CH_W[c]],
                                 pC_b[n % 2][:, 0:CH_W[c]], AF.Copy).then_inc(sm["sC"], 1)
            for i in range(NIT):
                h, j = divmod(i, NT)
                s.wait_ge(sm["sT"], i + 1)
                if i >= 2:
                    s.wait_ge(sm["sJ"], i - 1)
                    s.wait_ge(sm["sL"], 2 * (i - 1))
                for hf in range(2):
                    if hf == 1:
                        s.wait_ge(sm["sI"], i + 1)
                    nc.scalar.activation(A_sb[i % 2][:, hf * 1024:(hf + 1) * 1024],
                                         pC_S[:, hf * 1024:(hf + 1) * 1024], AF.Exp,
                                         bias=cm20[:, :], scale=1.0,
                                         accum_out=sums_sb[:, 2 * (i % 2) + hf:
                                                           2 * (i % 2) + hf + 1]
                                         ).then_inc(sm["sH"], 1)
                if j == NT - 1 and h < H - 1:
                    s.activation(qeT[:, :], QsT[:, (h + 1) * L:(h + 2) * L], AF.Identity,
                                 bias=rebT[:, h + 1:h + 2], scale=1.0).then_inc(sm["sF"], 1)
                bi = i + 3
                if bi < NIT:
                    for c in SCH:
                        n = bi * 5 + c
                        s.wait_ge(sm["sD"], n + 1)
                        if bi >= 2 and c == SCH[0]:
                            s.wait_ge(dm["Cbo0" if bi % 2 == 0 else "Cbo1"],
                                      16 * (bi // 2))
                        s.activation(band_sb[bi % 2][:, CH_OFF[c]:CH_OFF[c] + CH_W[c]],
                                     pC_b[n % 2][:, 0:CH_W[c]],
                                     AF.Copy).then_inc(sm["sC"], 1)
                if j == NT - 4 and h < H - 1:
                    s.wait_ge(sm["sD"], 5 * NT * (h + 1))
                    s.activation(qpT[:, :], QsT[:, (h + 1) * L:(h + 2) * L], AF.Identity,
                                 bias=rpbT[:, h + 1:h + 2], scale=1.0).then_inc(sm["sA"], 1)
                    if h + 2 < H:
                        s.dma_start(rkt_sb[(h + 2) % 2][:, :],
                                    rkt_d[(h + 2) * 128:(h + 3) * 128, :]
                                    ).then_inc(dm["Crk0" if h % 2 == 0 else "Crk1"], 16)

        @blk.vector
        def _(v):
            for bi0 in (0, 1, 2):
                for c in VCH:
                    n = bi0 * 5 + c
                    v.wait_ge(sm["sD"], n + 1)
                    v.tensor_copy(band_sb[bi0 % 2][:, CH_OFF[c]:CH_OFF[c] + CH_W[c]],
                                  pC_b[n % 2][:, 0:CH_W[c]]).then_inc(sm["sB"], 1)

            def at_evac(jj, half):
                v.wait_ge(sm["sL"], 2 * jj + half + 1)
                if half == 0 and jj >= 4:
                    v.wait_ge(sm["sP"], jj // 4)
                dst = bass.AP(AT4, half * 8 * 512 + (jj % 4) * 128,
                              [[16 * 512, 128], [512, 8], [1, 128]])
                srcp = bass.AP(pC_at, 0, [[1024, 128], [128, 8], [1, 128]])
                v.tensor_copy(dst, srcp).then_inc(sm["sK"], 1)

            for i in range(NIT):
                jj = i - 1
                if jj >= 0:
                    at_evac(jj, 0)
                bi = i + 3
                if bi < NIT:
                    for c in VCH:
                        n = bi * 5 + c
                        v.wait_ge(sm["sD"], n + 1)
                        if bi >= 2 and c == 0:
                            v.wait_ge(dm["Cbo0" if bi % 2 == 0 else "Cbo1"],
                                      16 * (bi // 2))
                        v.tensor_copy(band_sb[bi % 2][:, CH_OFF[c]:CH_OFF[c] + CH_W[c]],
                                      pC_b[n % 2][:, 0:CH_W[c]]).then_inc(sm["sB"], 1)
                if jj >= 0:
                    at_evac(jj, 1)
                jj2 = i - 2
                if jj2 >= 3 and jj2 % 4 == 3:
                    g = jj2 // 4
                    v.wait_ge(sm["sP"], g + 1)
                    v.tensor_copy(coT[:, (g // 4) * L + (g % 4) * 512:
                                      (g // 4) * L + (g % 4) * 512 + 512],
                                  pC_av[:, :]).then_inc(sm["sM"], 1)
                v.wait_ge(sm["sH"], 2 * (i + 1))
                ins = v.tensor_add(tiny[:, 14:15], sums_sb[:, 2 * (i % 2):2 * (i % 2) + 1],
                                   sums_sb[:, 2 * (i % 2) + 1:2 * (i % 2) + 2])
                ins.then_inc(sm["vch"], 1)
                v.wait_ge(sm["vch"], 2 * i + 1)
                ins = v.reciprocal(tiny[:, 12 + i % 2:13 + i % 2], tiny[:, 14:15])
                ins.then_inc(sm["vch"], 1)
                v.wait_ge(sm["vch"], 2 * i + 2)
                v.tensor_scalar(A_sb[i % 2][:, :], A_sb[i % 2][:, :],
                                tiny[:, 12 + i % 2:13 + i % 2], None,
                                ALU.mult).then_inc(sm["sJ"], 1)
            jj = NIT - 1
            at_evac(jj, 0)
            at_evac(jj, 1)
            g = NIT // 4 - 1
            v.wait_ge(sm["sP"], g + 1)
            v.tensor_copy(coT[:, (g // 4) * L + (g % 4) * 512:
                              (g // 4) * L + (g % 4) * 512 + 512],
                          pC_av[:, :]).then_inc(sm["sM"], 1)

        @blk.gpsimd
        def _(gp):
            for bi in (0, 1, 2):
                gp.wait_ge(sm["sB"], 2 * (bi + 1))
                gp.wait_ge(sm["sC"], 3 * (bi + 1))
                if bi >= 2:
                    gp.wait_ge(dm["Cpi0" if bi % 2 == 0 else "Cpi1"],
                               16 * (bi // 2))
                gp.dma_start(bounce_d[bi % 2][:, :],
                             band_sb[bi % 2][:, :]
                             ).then_inc(dm["Cbo0" if bi % 2 == 0 else "Cbo1"], 16)
            for i in range(NIT):
                bi = i + 3
                if bi < NIT:
                    gp.wait_ge(sm["sB"], 2 * (bi + 1))
                    gp.wait_ge(sm["sC"], 3 * (bi + 1))
                    gp.wait_ge(dm["Cpi0" if bi % 2 == 0 else "Cpi1"],
                               16 * (bi // 2))
                    gp.dma_start(bounce_d[bi % 2][:, :],
                                 band_sb[bi % 2][:, :]
                                 ).then_inc(dm["Cbo0" if bi % 2 == 0 else "Cbo1"], 16)

        @blk.sync
        def _(sy):
            for bi in range(NIT):
                sy.wait_ge(dm["Cbo0" if bi % 2 == 0 else "Cbo1"], 16 * (bi // 2 + 1))
                if bi >= 2:
                    sy.wait_ge(sm["sI"], bi - 1)
                sy.dma_start(P_sb[bi % 2][:, :],
                             bass.AP(bounce_d[bi % 2], 127, [[W - 1, 128], [1, L]])
                             ).then_inc(dm["Cpi0" if bi % 2 == 0 else "Cpi1"], 16)

    esC.close()
    if max_phase <= 8:
        es_bc.close()
        es_persist.close()
        es_sem.close()
        return nc


    esS = ExitStack()
    woutn = esS.enter_context(nc.sbuf_tensor("woutn", [128, ND * D], BF16))
    s_st = [esS.enter_context(nc.sbuf_tensor(f"s_st{i}", [128, D], BF16)) for i in range(2)]
    pS = esS.enter_context(nc.psum_tensor("pS", [128, D], F32))

    with nc.Block() as blk:
        @blk.gpsimd
        def _(gp):
            src = bass.AP(wout_d, 0, [[D, 128], [128 * D, ND], [1, D]])
            gp.dma_start(bass.AP(woutn, 0, [[ND * D, 128], [D, ND], [1, D]]),
                         src).then_inc(dm["Swo"], 16)
            for tt in range(NT):
                gp.wait_ge(sm["sW"], tt + 1)
                gp.dma_start(s_dram[tt * 128:(tt + 1) * 128, :],
                             s_st[tt % 2][:, :]).then_inc(dm["Sst0" if tt % 2 == 0 else "Sst1"], 16)
            gp.wait_ge(dm["Sst0"], 16 * (NT // 2))
            gp.wait_ge(dm["Sst1"], 16 * (NT // 2))

        @blk.tensor
        def _(t):
            t.wait_ge(dm["Swo"], 16)
            for tt in range(NT):
                if tt >= 1:
                    t.wait_ge(sm["sW"], tt)
                for ch in range(2):
                    for hv in range(ND):
                        t.matmul(pS[:, ch * 512:(ch + 1) * 512],
                                 coT[:, hv * L + tt * 128: hv * L + (tt + 1) * 128],
                                 woutn[:, hv * D + ch * 512: hv * D + ch * 512 + 512],
                                 start=(hv == 0), stop=False)
                    ins = t.matmul(pS[:, ch * 512:(ch + 1) * 512], ones_row[:, :],
                                   bout_bf[0:1, ch * 512:(ch + 1) * 512],
                                   start=False, stop=True)
                    if ch == 1:
                        ins.then_inc(sm["sR"], 1)

        @blk.scalar
        def _(s):
            for tt in range(NT):
                s.wait_ge(sm["sR"], tt + 1)
                if tt >= 2:
                    s.wait_ge(dm["Sst0" if tt % 2 == 0 else "Sst1"], 16 * (tt // 2))
                s.activation(s_st[tt % 2][:, :], pS[:, :], AF.Copy).then_inc(sm["sW"], 1)

    esS.close()
    es_bc.close()
    clear_block()
    if max_phase <= 9:
        if debug_dumps:
            dump_block([(0, s_dram[:, :], L * D)])
        es_persist.close()
        es_sem.close()
        return nc


    esD = ExitStack()
    wiT = esD.enter_context(nc.sbuf_tensor("wiT", [128, ND * F], BF16))
    woT = esD.enter_context(nc.sbuf_tensor("woT", [128, NF * D], BF16))
    xD_t = [esD.enter_context(nc.sbuf_tensor(f"xD_t{i}", [128, D], F32)) for i in range(2)]
    xnD = [esD.enter_context(nc.sbuf_tensor(f"xnD{i}", [128, D], BF16)) for i in range(2)]
    xnT_t = [esD.enter_context(nc.sbuf_tensor(f"xnT_t{i}", [128, D], BF16)) for i in range(2)]
    sD_t = [esD.enter_context(nc.sbuf_tensor(f"sD_t{i}", [128, D], BF16)) for i in range(2)]
    ipm_t = [esD.enter_context(nc.sbuf_tensor(f"ipm_t{i}", [32, E * 128], BF16)) for i in range(2)]
    btp_sb = [esD.enter_context(nc.sbuf_tensor(f"btp_sb{i}", [32, 1024], BF16)) for i in range(4)]
    rA = [esD.enter_context(nc.sbuf_tensor(f"rA{i}", [128, 512], BF16)) for i in range(2)]
    rB = [esD.enter_context(nc.sbuf_tensor(f"rB{i}", [128, 512], BF16)) for i in range(2)]
    mix_sb = [esD.enter_context(nc.sbuf_tensor(f"mix_sb{i}", [128, 1024], BF16))
              for i in range(2)]
    mixT_sb = esD.enter_context(nc.sbuf_tensor("mixT_sb", [128, 1024], BF16))
    gD = esD.enter_context(nc.sbuf_tensor("gD", [128, D], F32))
    outD = [esD.enter_context(nc.sbuf_tensor(f"outD{i}", [128, D], F32)) for i in range(2)]

    bi_bf = esD.enter_context(nc.sbuf_tensor("bi_bf", [1, F], BF16))
    bo_bf = esD.enter_context(nc.sbuf_tensor("bo_bf", [1, D], BF16))
    pD_z = esD.enter_context(nc.psum_tensor("pD_z", [128, 1024], F32))
    pD_mt = esD.enter_context(nc.psum_tensor("pD_mt", [128, 1024], BF16))
    pD_f = esD.enter_context(nc.psum_tensor("pD_f", [128, D], F32))
    pD_xn = esD.enter_context(nc.psum_tensor("pD_xn", [128, 1024], BF16))

    NQ = 4

    with nc.Block() as blk:
        @blk.gpsimd
        def _(gp):
            src = bass.AP(wiT_d, 0, [[F, 128], [128 * F, ND], [1, F]])
            gp.dma_start(bass.AP(wiT, 0, [[ND * F, 128], [F, ND], [1, F]]),
                         src).then_inc(dm["DwiT"], 16)
            gp.dma_start(bi_bf[:, :], bi_d[:, :]).then_inc(dm["DwiT"], 16)
            gp.dma_start(bo_bf[:, :], bo_d[:, :]).then_inc(dm["DwiT"], 16)
            nb = 0
            BTS = ["Dbt0", "Dbt1", "Dbt2", "Dbt3"]
            for tt in range(NT):
                for qtr in range(NQ):
                    for e in range(E):
                        if nb >= 4:
                            gp.wait_ge(sm["sM"], 2 * (nb - 3))
                        gp.dma_start(btp_sb[nb % 4][:, :],
                                     bass.AP(btp_d, e * 32 * F + qtr * 1024,
                                             [[F, 32], [1, 1024]])
                                     ).then_inc(dm[BTS[nb % 4]], 16)
                        nb += 1


        @blk.sync
        def _(sy):
            src = bass.AP(woT_d, 0, [[D, 128], [128 * D, NF], [1, D]])
            sy.dma_start(bass.AP(woT, 0, [[NF * D, 128], [D, NF], [1, D]]),
                         src).then_inc(dm["DwoT"], 16)
            for tt in range(NT):
                if tt >= 2:
                    sy.wait_ge(sm["sD"], tt - 1)
                sy.dma_start(xD_t[tt % 2][:, :],
                             x_d[tt * 128:(tt + 1) * 128, :]
                             ).then_inc(dm["Dxs0" if tt % 2 == 0 else "Dxs1"], 16)
                sy.dma_start(sD_t[tt % 2][:, :],
                             s_dram[tt * 128:(tt + 1) * 128, :]
                             ).then_inc(dm["Dxs0" if tt % 2 == 0 else "Dxs1"], 16)
                if tt >= 2:
                    sy.wait_ge(sm["sP"], tt - 1)
                sy.dma_start(bass.AP(ipm_t[tt % 2], 0, [[E * 128, 32], [128, E], [1, 128]]),
                             bass.AP(intpm_d, tt * 128,
                                     [[L, 32], [32 * L, E], [1, 128]])
                             ).then_inc(dm["Dip0" if tt % 2 == 0 else "Dip1"], 16)
                if tt >= 1:
                    sy.wait_ge(sm["sD"], tt)
                    sy.dma_start(out_d[(tt - 1) * 128:tt * 128, :],
                                 outD[(tt - 1) % 2][:, :]
                                 ).then_inc(dm["Dout0" if (tt - 1) % 2 == 0 else "Dout1"], 16)
            sy.wait_ge(sm["sD"], NT)
            sy.dma_start(out_d[(NT - 1) * 128:NT * 128, :],
                         outD[(NT - 1) % 2][:, :]
                         ).then_inc(dm["Dout0" if (NT - 1) % 2 == 0 else "Dout1"], 16)
            sy.wait_ge(dm["Dout0"], 16 * (NT // 2))
            sy.wait_ge(dm["Dout1"], 16 * (NT // 2))

        @blk.scalar
        def _(s):
            for tt in range(NT):
                s.wait_ge(dm["Dxs0" if tt % 2 == 0 else "Dxs1"], 32 * (tt // 2 + 1))
                if tt >= 2:
                    s.wait_ge(sm["sK"], tt - 1)
                nc.scalar.activation(xnD[tt % 2][:, :], xD_t[tt % 2][:, :], AF.Identity,
                                     bias=stats[:, NT + tt:NT + tt + 1],
                                     scale=stats[:, tt:tt + 1]).then_inc(sm["sL"], 1)
                if tt >= 1:
                    if tt >= 2:
                        s.wait_ge(sm["sD"], tt - 1)
                    s.wait_ge(sm["sP"], tt)
                    nc.scalar.activation(gD[:, :], pD_f[:, :], AF.Silu).then_inc(sm["sQ"], 1)
                for qtr in range(NQ):
                    for e in range(E):
                        idx = (tt * NQ + qtr) * E + e
                        s.wait_ge(sm["sM"], 2 * idx + 1)
                        if idx >= 2:
                            s.wait_ge(sm["sN"], idx - 1)
                        nc.scalar.activation(rA[idx % 2][:, :], pD_z[:, 0:512], AF.Relu,
                                             bias=0.0,
                                             scale=G_sb[:, tt * E + e: tt * E + e + 1]
                                             ).then_inc(sm["sO"], 1)
                if tt == NT - 1:
                    s.wait_ge(sm["sD"], tt)
                    s.wait_ge(sm["sP"], tt + 1)
                    nc.scalar.activation(gD[:, :], pD_f[:, :], AF.Silu).then_inc(sm["sQ"], 1)

        @blk.vector
        def _(v):
            for tt in range(NT):
                v.wait_ge(sm["sK"], tt + 1)
                if tt >= 2:
                    v.wait_ge(sm["sP"], tt - 1)
                v.tensor_copy(xnT_t[tt % 2][:, :], pD_xn[:, :]).then_inc(sm["sR"], 1)
                if tt >= 1:
                    ttp = tt - 1
                    v.wait_ge(sm["sQ"], ttp + 1)
                    v.wait_ge(dm["Dxs0" if ttp % 2 == 0 else "Dxs1"], 32 * (ttp // 2 + 1))
                    if ttp >= 2:
                        v.wait_ge(dm["Dout0" if ttp % 2 == 0 else "Dout1"], 16 * (ttp // 2))
                    ins = v.tensor_mul(gD[:, :], sD_t[ttp % 2][:, :], gD[:, :])
                    ins.then_inc(sm["vch"], 1)
                    v.wait_ge(sm["vch"], ttp + 1)
                    v.tensor_add(outD[ttp % 2][:, :], gD[:, :],
                                 xD_t[ttp % 2][:, :]).then_inc(sm["sD"], 1)
                def addAB(k):
                    qi_k = k // E
                    v.wait_ge(sm["sO"], k + 1)
                    if k % E == 0:
                        if qi_k >= 2:
                            v.wait_ge(sm["sV"], qi_k - 1)
                        insa = v.tensor_copy(mix_sb[qi_k % 2][:, 0:512],
                                             rA[k % 2][:, :])
                    else:
                        insa = v.tensor_add(mix_sb[qi_k % 2][:, 0:512],
                                            mix_sb[qi_k % 2][:, 0:512],
                                            rA[k % 2][:, :])
                    insa.then_inc(sm["sN"], 1)
                    v.wait_ge(sm["sS"], k + 1)
                    if k % E == 0:
                        insb = v.tensor_copy(mix_sb[qi_k % 2][:, 512:1024],
                                             rB[k % 2][:, :])
                    else:
                        insb = v.tensor_add(mix_sb[qi_k % 2][:, 512:1024],
                                            mix_sb[qi_k % 2][:, 512:1024],
                                            rB[k % 2][:, :])
                    insb.then_inc(sm["sX"], 1)

                for qtr in range(NQ):
                    qi = tt * NQ + qtr
                    for e in range(E):
                        idx = qi * E + e
                        v.wait_ge(sm["sM"], 2 * idx + 2)
                        if idx >= 2:
                            v.wait_ge(sm["sX"], idx - 1)
                        v.tensor_scalar(rB[idx % 2][:, :], pD_z[:, 512:1024],
                                        G_sb[:, tt * E + e: tt * E + e + 1], 0.0,
                                        ALU.mult, ALU.max).then_inc(sm["sS"], 1)
                        if e >= 1:
                            addAB(idx - 1)
                    addAB(qi * E + E - 1)
                    v.wait_ge(sm["sV"], qi + 1)
                    v.tensor_copy(mixT_sb[:, :], pD_mt[:, :]).then_inc(sm["sW"], 1)
                if tt == NT - 1:
                    v.wait_ge(sm["sQ"], tt + 1)
                    v.wait_ge(dm["Dxs0" if tt % 2 == 0 else "Dxs1"], 32 * (tt // 2 + 1))
                    v.wait_ge(dm["Dout0" if tt % 2 == 0 else "Dout1"], 16 * (tt // 2))
                    ins = v.tensor_mul(gD[:, :], sD_t[tt % 2][:, :], gD[:, :])
                    ins.then_inc(sm["vch"], 1)
                    v.wait_ge(sm["vch"], tt + 1)
                    v.tensor_add(outD[tt % 2][:, :], gD[:, :],
                                 xD_t[tt % 2][:, :]).then_inc(sm["sD"], 1)

        @blk.tensor
        def _(t):
            t.wait_ge(dm["DwiT"], 48)
            t.wait_ge(dm["DwoT"], 16)
            pend = []

            def emit_down(k):
                for _ in range(k):
                    if not pend:
                        return
                    pend.pop(0)()

            def push_down(tt, qtr):
                qi = tt * NQ + qtr

                def first_waits(tt=tt, qtr=qtr, qi=qi):
                    t.wait_ge(sm["sW"], qi + 1)
                    if qtr == 0:
                        t.wait_ge(sm["sQ"], tt)
                first = [True]
                for ft in range(8):
                    for ch in range(2):
                        def job(tt=tt, qtr=qtr, ft=ft, ch=ch, first=first):
                            if first[0]:
                                first_waits()
                                first[0] = False
                            ftg = qtr * 8 + ft
                            t.matmul(pD_f[:, ch * 512:(ch + 1) * 512],
                                     mixT_sb[:, ft * 128:(ft + 1) * 128],
                                     woT[:, ftg * D + ch * 512: ftg * D + ch * 512 + 512],
                                     start=(qtr == 0 and ft == 0), stop=False)
                        pend.append(job)
                if qtr == NQ - 1:
                    def bo_job(tt=tt):
                        for ch in range(2):
                            ins = t.matmul(pD_f[:, ch * 512:(ch + 1) * 512],
                                           gsumrow[0:1, tt * 128:(tt + 1) * 128],
                                           bo_bf[0:1, ch * 512:(ch + 1) * 512],
                                           start=False, stop=True)
                            if ch == 1:
                                ins.then_inc(sm["sP"], 1)
                    pend.append(bo_job)

            def mix_transposes(tt, qtr):
                qi = tt * NQ + qtr
                t.wait_ge(sm["sN"], 8 * (qi + 1))
                t.wait_ge(sm["sX"], 8 * (qi + 1))
                if qi >= 1:
                    t.wait_ge(sm["sW"], qi)
                for ft in range(8):
                    ins = nc.tensor.transpose(pD_mt[:, ft * 128:(ft + 1) * 128],
                                              mix_sb[qi % 2][:, ft * 128:(ft + 1) * 128],
                                              ident_bf[:, :])
                    if ft == 7:
                        ins.then_inc(sm["sV"], 1)
                push_down(tt, qtr)

            prev_q = [None]
            for tt in range(NT):
                t.wait_ge(sm["sL"], tt + 1)
                if tt >= 2:
                    t.wait_ge(sm["sR"], tt - 1)
                for ct in range(ND):
                    ins = nc.tensor.transpose(pD_xn[:, ct * 128:(ct + 1) * 128],
                                              xnD[tt % 2][:, ct * 128:(ct + 1) * 128],
                                              ident_bf[:, :])
                    if ct == ND - 1:
                        ins.then_inc(sm["sK"], 1)
                if prev_q[0] is not None:
                    mix_transposes(*prev_q[0])
                    prev_q[0] = None
                    emit_down(len(pend))
                t.wait_ge(sm["sR"], tt + 1)
                t.wait_ge(dm["Dip0" if tt % 2 == 0 else "Dip1"], 16 * (tt // 2 + 1))
                for qtr in range(NQ):
                    qi = tt * NQ + qtr
                    if qi >= 1:
                        t.wait_ge(sm["sO"], 8 * qi)
                        t.wait_ge(sm["sS"], 8 * qi)
                    for hf in range(2):
                        for dt in range(ND):
                            t.matmul(pD_z[:, hf * 512:(hf + 1) * 512],
                                     xnT_t[tt % 2][:, dt * 128:(dt + 1) * 128],
                                     wiT[:, dt * F + qtr * 1024 + hf * 512:
                                         dt * F + qtr * 1024 + hf * 512 + 512],
                                     start=(dt == 0), stop=False)
                        t.matmul(pD_z[:, hf * 512:(hf + 1) * 512], ones_row[:, :],
                                 bi_bf[0:1, qtr * 1024 + hf * 512: qtr * 1024 + hf * 512 + 512],
                                 start=False, stop=True)
                    if prev_q[0] is not None:
                        mix_transposes(*prev_q[0])
                        prev_q[0] = None
                    for e in range(E):
                        idx = qi * E + e
                        t.wait_ge(dm[["Dbt0", "Dbt1", "Dbt2", "Dbt3"][idx % 4]],
                                  16 * (idx // 4 + 1))
                        for hf in range(2):
                            if e >= 1:
                                t.wait_ge(sm["sO" if hf == 0 else "sS"], idx)
                            t.matmul(pD_z[:, hf * 512:(hf + 1) * 512],
                                     ipm_t[tt % 2][:, e * 128:(e + 1) * 128],
                                     btp_sb[idx % 4][:, hf * 512:(hf + 1) * 512],
                                     start=False, stop=False, skip_group_check=True
                                     ).then_inc(sm["sM"], 1)
                        emit_down(2)
                    emit_down(3)
                    prev_q[0] = (tt, qtr)
            mix_transposes(*prev_q[0])
            emit_down(len(pend))

    esD.close()
    es_persist.close()
    es_sem.close()
    return nc


def kernel(**inputs):
    i = inputs
    x = np.ascontiguousarray(i["L0_attn_q"], dtype=np.float32)
    pos = positions_np()
    posT = np.zeros((192, 4096), dtype=ml_dtypes.bfloat16)
    posT[:, :4095] = pos.T.astype(ml_dtypes.bfloat16)

    common = {
        "w_qs": np.ascontiguousarray(i["L0_attn_w_qs"], np.float32),
        "w_ks": np.ascontiguousarray(i["L0_attn_w_ks"], np.float32),
        "w_vs": np.ascontiguousarray(i["L0_attn_w_vs"], np.float32),
        "w_rel": np.ascontiguousarray(i["L0_attn_w_rel_pos_k"], np.float32),
        "reb": np.ascontiguousarray(np.asarray(i["L0_attn_rel_embed_bias"]).reshape(H, DK), np.float32),
        "rpb": np.ascontiguousarray(np.asarray(i["L0_attn_rel_pos_bias"]).reshape(H, DK), np.float32),
        "w_out": np.ascontiguousarray(i["L0_attn_w_out"], np.float32),
        "b_out": np.ascontiguousarray(np.asarray(i["L0_attn_b_out"]).reshape(1, D), np.float32),
        "wi": np.ascontiguousarray(i["L0_ffn_wi"], np.float32),
        "bi": np.ascontiguousarray(np.asarray(i["L0_ffn_bi"]).reshape(1, F), np.float32),
        "wo": np.ascontiguousarray(i["L0_ffn_wo"], np.float32),
        "bo": np.ascontiguousarray(np.asarray(i["L0_ffn_bo"]).reshape(1, D), np.float32),
        "lora_A": np.ascontiguousarray(np.asarray(i["L0_ffn_lora_A"]).reshape(E * 16, D), np.float32),
        "lora_B": np.ascontiguousarray(np.asarray(i["L0_ffn_lora_B"]).reshape(E * F, 16), np.float32),
        "gate_w": np.ascontiguousarray(i["L0_ffn_gate_w"], np.float32),
        "gate_b": np.ascontiguousarray(np.asarray(i["L0_ffn_gate_b"]).reshape(1, E), np.float32),
        "posT": posT,
    }
    nc = build()
    in_maps = [dict(common, x=np.ascontiguousarray(x[b])) for b in range(8)]
    res = run_bass_kernel_spmd(nc, in_maps, core_ids=list(range(8)))
    return np.stack([np.asarray(res.results[b]["out"], np.float32) for b in range(8)], axis=0)




# version 25
# speedup vs baseline: 1.0415x; 1.0415x over previous
import math
import numpy as np
import ml_dtypes
from contextlib import ExitStack

import concourse.bass as bass
import concourse.mybir as mybir
from concourse.bass_utils import run_bass_kernel_spmd

F32 = mybir.dt.float32
BF16 = mybir.dt.bfloat16
AF = mybir.ActivationFunctionType
ALU = mybir.AluOpType
AX = mybir.AxisListType

L = 2048
D = 1024
H = 8
DK = 128
F = 4096
E = 8
NT = L // 128
ND = D // 128
NF = F // 128
W = 2176
SOFT_SHIFT = -20.0
QSCALE = DK ** -0.5


def positions_np():
    from scipy.special import gammaln
    d = np.arange(-L + 1, L, dtype=np.float64)
    n = 192 // 6
    absd = np.abs(d)[:, None]
    half_life = 2.0 ** np.linspace(3.0, math.log(L) / math.log(2.0), n)
    feat_exp = np.exp(-math.log(2.0) / half_life[None, :] * absd)
    center_widths = 2.0 ** np.arange(1, n + 1, dtype=np.float64) - 1.0
    feat_cm = (center_widths[None, :] > absd).astype(np.float64)
    stddev = L / (2.0 * n)
    mean = np.linspace(L / n, float(L), n)[None, :]
    conc = (mean / stddev) ** 2
    rate = mean / stddev ** 2
    with np.errstate(divide="ignore"):
        log_absd = np.where(absd > 0, np.log(np.where(absd > 0, absd, 1.0)), -np.inf)
    log_unnorm = (conc - 1.0) * log_absd - rate * absd
    log_norm = gammaln(conc) - conc * np.log(rate)
    probs = np.exp(log_unnorm - log_norm) + 1e-08
    feat_gamma = probs / np.max(probs, axis=-1, keepdims=True)
    emb = np.concatenate([feat_exp, feat_cm, feat_gamma], axis=-1)
    emb = np.concatenate([emb, np.sign(d)[:, None] * emb], axis=-1)
    return emb.astype(np.float32)


CDBG = set()


def build(max_phase=99, debug_dumps=False, cnit=None):
    nc = bass.Bass("TRN2", target_bir_lowering=False, detect_race_conditions=False)

    x_d = nc.dram_tensor("x", [L, D], F32, kind="ExternalInput")
    wqs_d = nc.dram_tensor("w_qs", [D, D], F32, kind="ExternalInput")
    wks_d = nc.dram_tensor("w_ks", [D, D], F32, kind="ExternalInput")
    wvs_d = nc.dram_tensor("w_vs", [D, D], F32, kind="ExternalInput")
    wrel_d = nc.dram_tensor("w_rel", [192, D], F32, kind="ExternalInput")
    reb_d = nc.dram_tensor("reb", [H, DK], F32, kind="ExternalInput")
    rpb_d = nc.dram_tensor("rpb", [H, DK], F32, kind="ExternalInput")
    wout_d = nc.dram_tensor("w_out", [D, D], F32, kind="ExternalInput")
    bout_d = nc.dram_tensor("b_out", [1, D], F32, kind="ExternalInput")
    wi_d = nc.dram_tensor("wi", [F, D], F32, kind="ExternalInput")
    bi_d = nc.dram_tensor("bi", [1, F], F32, kind="ExternalInput")
    wo_d = nc.dram_tensor("wo", [D, F], F32, kind="ExternalInput")
    bo_d = nc.dram_tensor("bo", [1, D], F32, kind="ExternalInput")
    lA_d = nc.dram_tensor("lora_A", [E * 16, D], F32, kind="ExternalInput")
    lB_d = nc.dram_tensor("lora_B", [E * F, 16], F32, kind="ExternalInput")
    gw_d = nc.dram_tensor("gate_w", [E, D], F32, kind="ExternalInput")
    gb_d = nc.dram_tensor("gate_b", [1, E], F32, kind="ExternalInput")
    posT_d = nc.dram_tensor("posT", [192, 4096], BF16, kind="ExternalInput")
    out_d = nc.dram_tensor("out", [L, D], F32, kind="ExternalOutput")

    bounce_d = [nc.dram_tensor(f"bounce{i}", [128, W], BF16, kind="Internal") for i in range(2)]
    rkt_d = nc.dram_tensor("rkt", [H * 128, 4096], BF16, kind="Internal")
    intpm_d = nc.dram_tensor("intpm", [E * 32, L], BF16, kind="Internal")
    btp_d = nc.dram_tensor("btp", [E * 32, F], BF16, kind="Internal")
    s_dram = nc.dram_tensor("s_dram", [L, D], BF16, kind="Internal")
    wiT_d = nc.dram_tensor("wiT_d", [ND * 128, F], BF16, kind="Internal")
    woT_d = nc.dram_tensor("woT_d", [NF * 128, D], BF16, kind="Internal")

    ident_bf_d = nc.inline_tensor(np.eye(128, dtype=ml_dtypes.bfloat16), "ident_bf_d")
    ident_f_d = nc.inline_tensor(np.eye(128, dtype=np.float32), "ident_f_d")
    ones_row_d = nc.inline_tensor(np.ones((1, 128), dtype=ml_dtypes.bfloat16), "ones_row_d")

    es_persist = ExitStack()
    P = es_persist.enter_context

    ident_bf = P(nc.sbuf_tensor("ident_bf", [128, 128], BF16))
    ident_f = P(nc.sbuf_tensor("ident_f", [128, 128], F32))
    ones_row = P(nc.sbuf_tensor("ones_row", [1, 128], BF16))
    rebT = P(nc.sbuf_tensor("rebT", [128, H], F32))
    rpbT = P(nc.sbuf_tensor("rpbT", [128, H], F32))
    gwT = P(nc.sbuf_tensor("gwT", [128, ND * E], BF16))
    Apm = P(nc.sbuf_tensor("Apm", [128, ND * 256], BF16))
    bout_bf = P(nc.sbuf_tensor("bout_bf", [1, D], BF16))
    gb_bf = P(nc.sbuf_tensor("gb_bf", [1, E], BF16))
    G_sb = P(nc.sbuf_tensor("G_sb", [128, NT * E], F32))
    gsumrow = P(nc.sbuf_tensor("gsumrow", [1, L], BF16))
    stats = P(nc.sbuf_tensor("stats", [128, 2 * NT], F32))
    tiny = P(nc.sbuf_tensor("tiny", [128, 16], F32))
    cm20 = P(nc.sbuf_tensor("cm20", [128, 1], F32))
    cqs = P(nc.sbuf_tensor("cqs", [128, 1], F32))

    SEMN = ["sA", "sB", "sC", "sD", "sE", "sF", "sG", "sH", "sI", "sJ",
            "sK", "sL", "sM", "sN", "sO", "sP", "sQ", "sR", "sS", "sT",
            "sU", "sV", "sW", "sX"]
    CHN = ["vch", "sch"]
    DSEMN = ["A1s", "A1g", "A2r", "A2b", "A3l0", "A3l1", "A3s0", "A3s1",
             "Bx0", "Bx1", "Bwv", "Bwqk", "Bint",
             "Cbo0", "Cbo1", "Cpi0", "Cpi1", "Crk0", "Crk1",
             "Swo", "Sst0", "Sst1",
             "DwiT", "DwoT", "Dbt0", "Dbt1", "Dbt2", "Dbt3", "Dxs0", "Dxs1",
             "Dip0", "Dip1", "Dout0", "Dout1"]
    es_sem = ExitStack()
    sm = {n: es_sem.enter_context(nc.semaphore(n)) for n in SEMN + CHN}
    dm = {n: es_sem.enter_context(nc.semaphore(n)) for n in DSEMN}

    def dump_block(pairs):
        out_flat = out_d
        with nc.Block() as blk:
            @blk.gpsimd
            def _(gp):
                k = [0]
                for off, srcap, n in pairs:
                    row0, col0 = divmod(off, D)
                    assert col0 == 0 and n % D == 0
                    dst = out_flat[row0:row0 + n // D, :]
                    gp.dma_start(dst, srcap).then_inc(dm["Dout0"], 16)
                    k[0] += 16
                gp.wait_ge(dm["Dout0"], k[0])
    def clear_block():
        with nc.Block() as blk:
            @blk.sync
            def _(sy):
                for n in SEMN + CHN:
                    sy.sem_clear(sm[n])

    esA = ExitStack()
    posTA = esA.enter_context(nc.sbuf_tensor("posTA", [128, 4096], BF16))
    posTB = esA.enter_context(nc.sbuf_tensor("posTB", [64, 4096], BF16))
    wrelA = esA.enter_context(nc.sbuf_tensor("wrelA", [128, D], BF16))
    wrelB = esA.enter_context(nc.sbuf_tensor("wrelB", [64, D], BF16))
    rkt_stage = esA.enter_context(nc.sbuf_tensor("rkt_stage", [128, 4096], BF16))
    An_sb = esA.enter_context(nc.sbuf_tensor("An_sb", [128, D], BF16))
    AT_sb = esA.enter_context(nc.sbuf_tensor("AT_sb", [128, ND * 128], BF16))
    Bn_sb = esA.enter_context(nc.sbuf_tensor("Bn_sb", [128, E * 512], BF16))
    BT_sb = esA.enter_context(nc.sbuf_tensor("BT_sb", [16, F], BF16))
    BT_z = esA.enter_context(nc.sbuf_tensor("BT_z", [16, F], BF16))
    gwn = esA.enter_context(nc.sbuf_tensor("gwn", [E, D], BF16))
    rebn = esA.enter_context(nc.sbuf_tensor("rebn", [H, DK], F32))
    rpbn = esA.enter_context(nc.sbuf_tensor("rpbn", [H, DK], F32))
    brow_f = esA.enter_context(nc.sbuf_tensor("brow_f", [1, D], F32))

    pA1 = esA.enter_context(nc.psum_tensor("pA1", [128, 512], F32))
    pA2 = esA.enter_context(nc.psum_tensor("pA2", [128, 512], F32))
    pAt = esA.enter_context(nc.psum_tensor("pAt", [128, 128], F32))
    pAtb = esA.enter_context(nc.psum_tensor("pAtb", [16, 2048], BF16))
    pAab = esA.enter_context(nc.psum_tensor("pAab", [128, 128], BF16))
    pAg = esA.enter_context(nc.psum_tensor("pAg", [128, E], BF16))

    with nc.Block() as blk:
        @blk.sync
        def _(sy):
            sy.dma_start(posTA[:, :], posT_d[0:128, :]).then_inc(dm["A1s"], 16)
            sy.dma_start(posTB[:, :], posT_d[128:192, :]).then_inc(dm["A1s"], 16)
            sy.dma_start(rebn[:, :], reb_d[:, :]).then_inc(dm["A1s"], 16)
            sy.dma_start(rpbn[:, :], rpb_d[:, :]).then_inc(dm["A1s"], 16)
            sy.dma_start(brow_f[:, :], bout_d[:, :]).then_inc(dm["A1s"], 16)

        @blk.gpsimd
        def _(gp):
            gp.dma_start(wrelA[:, :], wrel_d[0:128, :]).then_inc(dm["A1g"], 16)
            gp.dma_start(wrelB[:, :], wrel_d[128:192, :]).then_inc(dm["A1g"], 16)
            gp.dma_start(An_sb[:, :], lA_d[:, :]).then_inc(dm["A1g"], 16)
            gp.dma_start(gwn[:, :], gw_d[:, :]).then_inc(dm["A1g"], 16)
            gp.dma_start(gb_bf[:, :], gb_d[:, :]).then_inc(dm["A1g"], 16)
            for e in range(E):
                src = bass.AP(lB_d, e * F * 16, [[16, 128], [128 * 16, 32], [1, 16]])
                dstap = bass.AP(Bn_sb, e * 512, [[E * 512, 128], [16, 32], [1, 16]])
                gp.dma_start(dstap, src).then_inc(dm["A1g"], 16)
            gp.dma_start(ident_bf[:, :], ident_bf_d[:, :]).then_inc(dm["A1g"], 16)
            gp.dma_start(ident_f[:, :], ident_f_d[:, :]).then_inc(dm["A1g"], 16)
            gp.dma_start(ones_row[:, :], ones_row_d[:, :]).then_inc(dm["A1g"], 16)

    with nc.Block() as blk:
        @blk.tensor
        def _(t):
            t.wait_ge(dm["A1s"], 16 * 5)
            t.wait_ge(dm["A1g"], 16 * 16)
            mmv = 0
            for h in range(H):
                for ch in range(8):
                    pb = [pA1, pA2][mmv % 2]
                    if mmv >= 2:
                        t.wait_ge(sm["sD"], mmv - 1)
                    t.matmul(pb[:, :], wrelA[:, h * 128:(h + 1) * 128],
                             posTA[:, ch * 512:(ch + 1) * 512], start=True, stop=False)
                    t.matmul(pb[:, :], wrelB[:, h * 128:(h + 1) * 128],
                             posTB[:, ch * 512:(ch + 1) * 512],
                             start=False, stop=True).then_inc(sm["sC"], 1)
                    mmv += 1
            nc.tensor.transpose(pAt[:, 0:8], rebn[:, :], ident_f[0:8, 0:8])
            nc.tensor.transpose(pAt[:, 8:16], rpbn[:, :],
                                ident_f[0:8, 0:8]).then_inc(sm["sE"], 1)
            for dt in range(ND):
                nc.tensor.transpose(pAg[:, :], gwn[:, dt * 128:(dt + 1) * 128],
                                    ident_bf[0:8, 0:8]).then_inc(sm["sF"], 1)
                t.wait_ge(sm["sG"], dt + 1)
            for dt in range(ND):
                nc.tensor.transpose(pAab[:, :], An_sb[:, dt * 128:(dt + 1) * 128],
                                    ident_bf[:, :]).then_inc(sm["sH"], 1)
                t.wait_ge(sm["sI"], dt + 1)
            for e in range(E):
                for half in range(2):
                    for tt in range(16):
                        tg = half * 16 + tt
                        ins = nc.tensor.transpose(
                            pAtb[:, tt * 128:(tt + 1) * 128],
                            Bn_sb[:, e * 512 + tg * 16: e * 512 + tg * 16 + 16],
                            ident_bf[:, :])
                        if tt == 15:
                            ins.then_inc(sm["sJ"], 1)
                    t.wait_ge(sm["sK"], e * 2 + half + 1)

        @blk.vector
        def _(v):
            for h in range(H):
                if h >= 1:
                    v.wait_ge(dm["A2r"], 16 * h)
                for ch in range(8):
                    mmv = h * 8 + ch
                    v.wait_ge(sm["sC"], mmv + 1)
                    v.tensor_copy(rkt_stage[:, ch * 512:(ch + 1) * 512],
                                  [pA1, pA2][mmv % 2][:, :]).then_inc(sm["sD"], 1)
            v.wait_ge(sm["sE"], 1)
            v.tensor_copy(rebT[:, :], pAt[:, 0:8])
            v.tensor_copy(rpbT[:, :], pAt[:, 8:16])
            for dt in range(ND):
                v.wait_ge(sm["sF"], dt + 1)
                v.tensor_copy(gwT[:, dt * E:(dt + 1) * E], pAg[:, :]).then_inc(sm["sG"], 1)
            for dt in range(ND):
                v.wait_ge(sm["sH"], dt + 1)
                v.tensor_copy(AT_sb[:, dt * 128:(dt + 1) * 128],
                              pAab[:, :]).then_inc(sm["sI"], 1)
            v.memset(BT_z[:, :], 0.0).then_inc(sm["sL"], 1)
            for e in range(E):
                if e >= 1:
                    v.wait_ge(dm["A2b"], 16 * (2 * e + 1))
                for half in range(2):
                    v.wait_ge(sm["sJ"], e * 2 + half + 1)
                    v.tensor_copy(BT_sb[:, half * 2048:(half + 1) * 2048],
                                  pAtb[:, :]).then_inc(sm["sK"], 1)
            for dt in range(ND):
                for e in range(E):
                    base = dt * 256 + e * 32
                    v.tensor_copy(Apm[:, base:base + 16],
                                  AT_sb[:, dt * 128 + e * 16: dt * 128 + e * 16 + 16])
                    if e == 0:
                        v.memset(Apm[:, base + 16:base + 32], 0.0)
                    else:
                        v.tensor_scalar(Apm[:, base + 16:base + 32],
                                        AT_sb[:, dt * 128 + (e - 1) * 16: dt * 128 + e * 16],
                                        -1.0, None, ALU.mult)
            v.wait_ge(dm["A1s"], 16 * 5)
            v.tensor_copy(bout_bf[:, :], brow_f[:, :])
            v.memset(cm20[:, :], SOFT_SHIFT)
            v.memset(cqs[:, :], QSCALE)

        @blk.gpsimd
        def _(gp):
            for h in range(H):
                gp.wait_ge(sm["sD"], h * 8 + 8)
                gp.dma_start(rkt_d[h * 128:(h + 1) * 128, :],
                             rkt_stage[:, :]).then_inc(dm["A2r"], 16)
            gp.wait_ge(sm["sL"], 1)
            gp.dma_start(btp_d[16:32, :], BT_z[:, :]).then_inc(dm["A2b"], 16)
            for e in range(E):
                gp.wait_ge(sm["sK"], 2 * (e + 1))
                gp.dma_start(btp_d[e * 32:e * 32 + 16, :],
                             BT_sb[:, :]).then_inc(dm["A2b"], 16)
                if e < E - 1:
                    gp.dma_start(btp_d[(e + 1) * 32 + 16:(e + 2) * 32, :],
                                 BT_sb[:, :]).then_inc(dm["A2b"], 16)
            gp.wait_ge(dm["A2b"], 16 * 16)
            gp.wait_ge(dm["A2r"], 16 * H)

    esA.close()
    clear_block()
    if max_phase <= 1:
        if debug_dumps:
            dump_block([(0, rkt_d[0:128, :], 128 * 4096),
                        (512 * 1024, btp_d[0:64, :], 64 * F),
                        (768 * 1024, intpm_d[0:32, :], 32 * L)])
        es_persist.close()
        es_sem.close()
        return nc


    esA3 = ExitStack()
    natst = [esA3.enter_context(nc.sbuf_tensor(f"natst{i}", [128, 4096], BF16)) for i in range(2)]
    tost = [esA3.enter_context(nc.sbuf_tensor(f"tost{i}", [128, 1024], BF16)) for i in range(2)]
    pW = [esA3.enter_context(nc.psum_tensor(f"pW{i}", [128, 1024], BF16)) for i in range(2)]

    with nc.Block() as blk:
        @blk.gpsimd
        def _(gp):
            nld = 0
            for r in range(8):
                if nld >= 2:
                    gp.wait_ge(sm["sC"], 4 * (nld - 1))
                src = bass.AP(wi_d, r * 512 * D, [[D, 128], [128 * D, 4], [1, D]])
                dstap = bass.AP(natst[nld % 2], 0, [[4096, 128], [1024, 4], [1, D]])
                gp.dma_start(dstap, src).then_inc(dm["A3l0" if nld % 2 == 0 else "A3l1"], 16)
                nld += 1
            for dt in range(8):
                if nld >= 2:
                    gp.wait_ge(sm["sC"], 4 * (nld - 1))
                src = bass.AP(wo_d, dt * 128 * F, [[F, 128], [1, F]])
                gp.dma_start(natst[nld % 2][:, :],
                             src).then_inc(dm["A3l0" if nld % 2 == 0 else "A3l1"], 16)
                nld += 1

        @blk.tensor
        def _(t):
            nb = 0
            for r in range(8):
                t.wait_ge(dm["A3l0" if r % 2 == 0 else "A3l1"], 16 * (r // 2 + 1))
                for ftl in range(4):
                    if nb >= 2:
                        t.wait_ge(sm["sC"], nb - 1)
                    for dt in range(8):
                        ins = nc.tensor.transpose(
                            pW[nb % 2][:, dt * 128:(dt + 1) * 128],
                            natst[r % 2][:, ftl * 1024 + dt * 128: ftl * 1024 + (dt + 1) * 128],
                            ident_bf[:, :])
                        if dt == 7:
                            ins.then_inc(sm["sD"], 1)
                    nb += 1
            for dtw in range(8):
                nw = 8 + dtw
                t.wait_ge(dm["A3l0" if nw % 2 == 0 else "A3l1"], 16 * (nw // 2 + 1))
                for b4 in range(4):
                    if nb >= 2:
                        t.wait_ge(sm["sC"], nb - 1)
                    for ftl in range(8):
                        ins = nc.tensor.transpose(
                            pW[nb % 2][:, ftl * 128:(ftl + 1) * 128],
                            natst[nw % 2][:, b4 * 1024 + ftl * 128: b4 * 1024 + (ftl + 1) * 128],
                            ident_bf[:, :])
                        if ftl == 7:
                            ins.then_inc(sm["sD"], 1)
                    nb += 1

        @blk.vector
        def _(v):
            for nb in range(64):
                v.wait_ge(sm["sD"], nb + 1)
                if nb >= 2:
                    v.wait_ge(dm["A3s0" if nb % 2 == 0 else "A3s1"],
                              16 * (nb // 2))
                v.tensor_copy(tost[nb % 2][:, :], pW[nb % 2][:, :]).then_inc(sm["sC"], 1)

        @blk.sync
        def _(sy):
            nb = 0
            for r in range(8):
                for ftl in range(4):
                    sy.wait_ge(sm["sC"], nb + 1)
                    dst = bass.AP(wiT_d, (r * 4 + ftl) * 128,
                                  [[F, 128], [128 * F, 8], [1, 128]])
                    sy.dma_start(dst, bass.AP(tost[nb % 2], 0,
                                              [[1024, 128], [128, 8], [1, 128]])
                                 ).then_inc(dm["A3s0" if nb % 2 == 0 else "A3s1"], 16)
                    nb += 1
            for dtw in range(8):
                for b4 in range(4):
                    sy.wait_ge(sm["sC"], nb + 1)
                    dst = bass.AP(woT_d, (b4 * 8) * 128 * D + dtw * 128,
                                  [[D, 128], [128 * D, 8], [1, 128]])
                    sy.dma_start(dst, bass.AP(tost[nb % 2], 0,
                                              [[1024, 128], [128, 8], [1, 128]])
                                 ).then_inc(dm["A3s0" if nb % 2 == 0 else "A3s1"], 16)
                    nb += 1
            sy.wait_ge(dm["A3s0"], 16 * 32)
            sy.wait_ge(dm["A3s1"], 16 * 32)

    esA3.close()
    clear_block()
    if max_phase <= 2:
        if debug_dumps:
            dump_block([(0, bass.AP(wiT_d, 0, [[F, 256], [1, F]]), 256 * F),
                        (1024 * 1024, bass.AP(woT_d, 0, [[D, 1024], [1, D]]), 1024 * D)])
        es_persist.close()
        es_sem.close()
        return nc


    es_bc = ExitStack()
    QsT = es_bc.enter_context(nc.sbuf_tensor("QsT", [128, ND * L], BF16))
    KT = es_bc.enter_context(nc.sbuf_tensor("KT", [128, ND * L], BF16))
    V_sb = es_bc.enter_context(nc.sbuf_tensor("V_sb", [128, NT * D], BF16))
    coT = es_bc.enter_context(nc.sbuf_tensor("coT", [128, ND * L], BF16))

    esB = ExitStack()
    xnT = esB.enter_context(nc.sbuf_tensor("xnT", [128, ND * L], BF16))

    esB1 = ExitStack()
    x_t = [esB1.enter_context(nc.sbuf_tensor(f"x_t{i}", [128, D], F32)) for i in range(2)]
    xn_nat = [esB1.enter_context(nc.sbuf_tensor(f"xn_nat{i}", [128, D], BF16)) for i in range(2)]
    sq_sc = [esB1.enter_context(nc.sbuf_tensor(f"sq_sc{i}", [128, D], F32)) for i in range(2)]
    stg = esB1.enter_context(nc.sbuf_tensor("stg", [128, 32], F32))
    Lg = esB1.enter_context(nc.sbuf_tensor("Lg", [128, NT * E], F32))
    g8b = esB1.enter_context(nc.sbuf_tensor("g8b", [128, 4 * NT * E], F32))
    gsc = esB1.enter_context(nc.sbuf_tensor("gsc", [128, 6 * NT], F32))
    pB_xn = esB1.enter_context(nc.psum_tensor("pB_xn", [128, ND * 128], BF16))
    pB_g = esB1.enter_context(nc.psum_tensor("pB_g", [128, E], F32))
    pB_gs = esB1.enter_context(nc.psum_tensor("pB_gs", [1, 128], F32))

    def b3(tens, off, n2, n3):
        return bass.AP(tens, off, [[tens.shape[1], 128], [n3, n2], [1, n3]])

    with nc.Block() as blk:
        @blk.sync
        def _(sy):
            for p in range(2):
                for tt in range(NT):
                    k = p * NT + tt
                    if 2 <= k and k - 2 < NT:
                        sy.wait_ge(sm["sD"], k - 1)
                    elif k - 2 >= NT:
                        sy.wait_ge(sm["sH"], k - NT - 1)
                    sy.dma_start(x_t[k % 2][:, :],
                                 x_d[tt * 128:(tt + 1) * 128, :]
                                 ).then_inc(dm["Bx0" if k % 2 == 0 else "Bx1"], 16)

        @blk.vector
        def _(v):
            vc = [0]
            def vfence(ins):
                ins.then_inc(sm["vch"], 1)
                vc[0] += 1
                v.wait_ge(sm["vch"], vc[0])
            for tt in range(NT):
                v.wait_ge(dm["Bx0" if tt % 2 == 0 else "Bx1"], 16 * (tt // 2 + 1))
                xt = x_t[tt % 2]
                v.reduce_sum(stg[:, tt:tt + 1], xt[:, :], axis=AX.X)
                vfence(v.tensor_mul(sq_sc[tt % 2][:, :], xt[:, :], xt[:, :]))
                ins = v.reduce_sum(stg[:, NT + tt:NT + tt + 1], sq_sc[tt % 2][:, :], axis=AX.X)
                ins.then_inc(sm["sD"], 1)
            v.tensor_scalar(gsc[:, 0:NT], stg[:, 0:NT], 1.0 / D, None, ALU.mult)
            vfence(v.tensor_scalar(gsc[:, NT:2 * NT], stg[:, NT:2 * NT], 1.0 / D, None, ALU.mult))
            vfence(v.tensor_mul(gsc[:, 2 * NT:3 * NT], gsc[:, 0:NT], gsc[:, 0:NT]))
            vfence(v.tensor_sub(gsc[:, NT:2 * NT], gsc[:, NT:2 * NT], gsc[:, 2 * NT:3 * NT]))
            vfence(v.tensor_scalar(gsc[:, NT:2 * NT], gsc[:, NT:2 * NT], 1e-5, None, ALU.add))
            ins = v.reciprocal(gsc[:, 2 * NT:3 * NT], gsc[:, NT:2 * NT])
            ins.then_inc(sm["sE"], 1)
            v.wait_ge(sm["sF"], 1)
            v.tensor_mul(stats[:, NT:2 * NT], gsc[:, 0:NT],
                         stats[:, 0:NT]).then_inc(sm["vch"], 1)
            vc[0] += 1
            v.wait_ge(sm["vch"], vc[0])
            v.tensor_scalar(stats[:, NT:2 * NT], stats[:, NT:2 * NT], -1.0,
                            None, ALU.mult).then_inc(sm["sG"], 1)
            for tt in range(NT):
                v.wait_ge(sm["sI"], tt + 1)
                dst = bass.AP(xnT, tt * 128, [[ND * L, 128], [L, ND], [1, 128]])
                srcp = bass.AP(pB_xn, 0, [[ND * 128, 128], [128, ND], [1, 128]])
                v.tensor_copy(dst, srcp).then_inc(sm["sJ"], 1)
                v.wait_ge(sm["sK"], tt + 1)
                v.tensor_copy(Lg[:, tt * E:(tt + 1) * E],
                              pB_g[:, :]).then_inc(sm["sL"], 1)
            E8, SC, MK, CT = 0, NT * E, 2 * NT * E, 3 * NT * E
            m1c, m2c, vsc, s8c = 3 * NT, 4 * NT, 5 * NT, NT
            vfence(v.reduce_max(gsc[:, m1c:m1c + NT], b3(Lg, 0, NT, E), axis=AX.X))
            mb = bass.AP(gsc, m1c, [[6 * NT, 128], [1, NT], [0, E]])
            ins = v.tensor_sub(b3(g8b, E8, NT, E), b3(Lg, 0, NT, E), mb)
            ins.then_inc(sm["sU"], 1)
            v.wait_ge(sm["sM"], 1)
            vfence(v.reduce_sum(gsc[:, s8c:s8c + NT], b3(g8b, E8, NT, E), axis=AX.X))
            vfence(v.reciprocal(gsc[:, 2 * NT:3 * NT], gsc[:, s8c:s8c + NT]))
            rb = bass.AP(gsc, 2 * NT, [[6 * NT, 128], [1, NT], [0, E]])
            vfence(v.tensor_mul(b3(g8b, SC, NT, E), b3(g8b, E8, NT, E), rb))
            vfence(v.reduce_max(gsc[:, m1c:m1c + NT], b3(g8b, SC, NT, E), axis=AX.X))
            m1b = bass.AP(gsc, m1c, [[6 * NT, 128], [1, NT], [0, E]])
            vfence(v.tensor_tensor(b3(g8b, MK, NT, E), b3(g8b, SC, NT, E), m1b, ALU.is_equal))
            vfence(v.scalar_tensor_tensor(b3(g8b, MK, NT, E), b3(g8b, MK, NT, E), -1e30,
                                          b3(g8b, SC, NT, E), ALU.mult, ALU.add))
            vfence(v.reduce_max(gsc[:, m2c:m2c + NT], b3(g8b, MK, NT, E), axis=AX.X))
            vfence(v.tensor_add(gsc[:, vsc:vsc + NT], gsc[:, m1c:m1c + NT],
                                gsc[:, m2c:m2c + NT]))
            m2b = bass.AP(gsc, m2c, [[6 * NT, 128], [1, NT], [0, E]])
            vfence(v.tensor_tensor(b3(g8b, CT, NT, E), b3(g8b, SC, NT, E), m2b, ALU.is_ge))
            vsb = bass.AP(gsc, vsc, [[6 * NT, 128], [1, NT], [0, E]])
            vfence(v.tensor_tensor(b3(G_sb, 0, NT, E), b3(g8b, CT, NT, E), vsb, ALU.mult))
            v.tensor_scalar(gsc[:, vsc:vsc + NT], gsc[:, vsc:vsc + NT], 2.0,
                            None, ALU.mult).then_inc(sm["sN"], 1)
            for tt in range(NT):
                v.wait_ge(sm["sO"], tt + 1)
                v.tensor_copy(gsumrow[0:1, tt * 128:(tt + 1) * 128],
                              pB_gs[0:1, :]).then_inc(sm["sP"], 1)

        @blk.scalar
        def _(s):
            s.wait_ge(sm["sE"], 1)
            s.activation(stats[:, 0:NT], gsc[:, 2 * NT:3 * NT],
                         AF.Sqrt).then_inc(sm["sF"], 1)
            for tt in range(NT):
                k = NT + tt
                s.wait_ge(dm["Bx0" if k % 2 == 0 else "Bx1"], 16 * (k // 2 + 1))
                if tt == 0:
                    s.wait_ge(sm["sG"], 1)
                if tt >= 2:
                    s.wait_ge(sm["sI"], tt - 1)
                nc.scalar.activation(xn_nat[k % 2][:, :], x_t[k % 2][:, :], AF.Identity,
                                     bias=stats[:, NT + tt:NT + tt + 1],
                                     scale=stats[:, tt:tt + 1]).then_inc(sm["sH"], 1)
            s.wait_ge(sm["sL"], NT)
            s.wait_ge(sm["sU"], 1)
            nc.scalar.activation(g8b[:, 0:NT * E], g8b[:, 0:NT * E],
                                 AF.Exp).then_inc(sm["sM"], 1)

        @blk.tensor
        def _(t):
            for tt in range(NT):
                k = NT + tt
                t.wait_ge(sm["sH"], tt + 1)
                if tt >= 1:
                    t.wait_ge(sm["sJ"], tt)
                for ct in range(ND):
                    ins = nc.tensor.transpose(pB_xn[:, ct * 128:(ct + 1) * 128],
                                              xn_nat[k % 2][:, ct * 128:(ct + 1) * 128],
                                              ident_bf[:, :])
                    if ct == ND - 1:
                        ins.then_inc(sm["sI"], 1)
                if tt >= 1:
                    t.wait_ge(sm["sL"], tt)
                t.wait_ge(sm["sJ"], tt + 1)
                for ct in range(ND):
                    t.matmul(pB_g[:, :], xnT[:, ct * L + tt * 128: ct * L + (tt + 1) * 128],
                             gwT[:, ct * E:(ct + 1) * E], start=(ct == 0), stop=False)
                t.matmul(pB_g[:, :], ones_row[:, :], gb_bf[:, :],
                         start=False, stop=True).then_inc(sm["sK"], 1)
            t.wait_ge(sm["sN"], 1)
            for tt in range(NT):
                if tt >= 1:
                    t.wait_ge(sm["sP"], tt)
                nc.tensor.transpose(pB_gs[:, :], gsc[:, 5 * NT + tt:5 * NT + tt + 1],
                                    ident_f[:, :]).then_inc(sm["sO"], 1)

    esB1.close()
    clear_block()

    esB2v = ExitStack()
    wv_sb = esB2v.enter_context(nc.sbuf_tensor("wv_sb", [128, ND * D], BF16))
    pB_v = esB2v.enter_context(nc.psum_tensor("pB_v", [128, D], F32))

    with nc.Block() as blk:
        @blk.gpsimd
        def _(gp):
            srcap = bass.AP(wvs_d, 0, [[D, 128], [128 * D, ND], [1, D]])
            dstap = bass.AP(wv_sb, 0, [[ND * D, 128], [D, ND], [1, D]])
            gp.dma_start(dstap, srcap).then_inc(dm["Bwv"], 16)

        @blk.tensor
        def _(t):
            t.wait_ge(dm["Bwv"], 16)
            for tt in range(NT):
                for ch in range(2):
                    for ct in range(ND):
                        ins = t.matmul(pB_v[:, ch * 512:(ch + 1) * 512],
                                       xnT[:, ct * L + tt * 128: ct * L + (tt + 1) * 128],
                                       wv_sb[:, ct * D + ch * 512: ct * D + ch * 512 + 512],
                                       start=(ct == 0), stop=(ct == ND - 1))
                        if ch == 1 and ct == ND - 1:
                            ins.then_inc(sm["sP"], 1)
                t.wait_ge(sm["sQ"], tt + 1)

        @blk.vector
        def _(v):
            for tt in range(NT):
                v.wait_ge(sm["sP"], tt + 1)
                v.tensor_copy(V_sb[:, tt * D:(tt + 1) * D], pB_v[:, :]).then_inc(sm["sQ"], 1)

    esB2v.close()
    clear_block()
    if max_phase <= 5:
        esB.close()
        es_bc.close()
        es_persist.close()
        es_sem.close()
        return nc


    esB2 = ExitStack()
    wq_sb = esB2.enter_context(nc.sbuf_tensor("wq_sb", [128, ND * D], BF16))
    wk_sb = esB2.enter_context(nc.sbuf_tensor("wk_sb", [128, ND * D], BF16))
    pB_qk = esB2.enter_context(nc.psum_tensor("pB_qk", [128, L], F32))

    with nc.Block() as blk:
        @blk.gpsimd
        def _(gp):
            for (dst, srcd) in ((wq_sb, wqs_d), (wk_sb, wks_d)):
                srcap = bass.AP(srcd, 0, [[D, 128], [128 * D, ND], [1, D]])
                dstap = bass.AP(dst, 0, [[ND * D, 128], [D, ND], [1, D]])
                gp.dma_start(dstap, srcap).then_inc(dm["Bwqk"], 16)

        @blk.tensor
        def _(t):
            t.wait_ge(dm["Bwqk"], 16 * 2)
            cnt = 0
            for wsb in (wq_sb, wk_sb):
                for dt in range(ND):
                    for ch in range(4):
                        for ct in range(ND):
                            ins = t.matmul(pB_qk[:, ch * 512:(ch + 1) * 512],
                                           wsb[:, ct * D + dt * 128: ct * D + (dt + 1) * 128],
                                           xnT[:, ct * L + ch * 512: ct * L + (ch + 1) * 512],
                                           start=(ct == 0), stop=(ct == ND - 1))
                            if ch == 3 and ct == ND - 1:
                                ins.then_inc(sm["sO"], 1)
                    cnt += 1
                    t.wait_ge(sm["sN"], cnt)

        @blk.scalar
        def _(s):
            for dt in range(ND):
                s.wait_ge(sm["sO"], dt + 1)
                nc.scalar.activation(QsT[:, dt * L:(dt + 1) * L], pB_qk[:, :], AF.Identity,
                                     bias=0.0, scale=cqs[:, :]).then_inc(sm["sN"], 1)

        @blk.vector
        def _(v):
            for dt in range(ND):
                v.wait_ge(sm["sO"], ND + dt + 1)
                v.tensor_copy(KT[:, dt * L:(dt + 1) * L], pB_qk[:, :]).then_inc(sm["sN"], 1)

    esB2.close()
    clear_block()
    if max_phase <= 6:
        esB.close()
        es_bc.close()
        es_persist.close()
        es_sem.close()
        return nc


    esB3 = ExitStack()
    pB_i = esB3.enter_context(nc.psum_tensor("pB_i", [128, L], F32))
    int_st = esB3.enter_context(nc.sbuf_tensor("int_st", [128, L], BF16))

    with nc.Block() as blk:
        @blk.tensor
        def _(t):
            for p in range(2):
                for ch in range(4):
                    for ct in range(ND):
                        ins = t.matmul(pB_i[:, ch * 512:(ch + 1) * 512],
                                       Apm[:, ct * 256 + p * 128: ct * 256 + (p + 1) * 128],
                                       xnT[:, ct * L + ch * 512: ct * L + (ch + 1) * 512],
                                       start=(ct == 0), stop=(ct == ND - 1))
                        if ch == 3 and ct == ND - 1:
                            ins.then_inc(sm["sR"], 1)
                t.wait_ge(sm["sS"], p + 1)

        @blk.vector
        def _(v):
            for p in range(2):
                v.wait_ge(sm["sR"], p + 1)
                if p >= 1:
                    v.wait_ge(dm["Bint"], 16 * p)
                v.tensor_copy(int_st[:, :], pB_i[:, :]).then_inc(sm["sS"], 1)

        @blk.gpsimd
        def _(gp):
            for p in range(2):
                gp.wait_ge(sm["sS"], p + 1)
                gp.dma_start(intpm_d[p * 128:(p + 1) * 128, :],
                             int_st[:, :]).then_inc(dm["Bint"], 16)
            gp.wait_ge(dm["Bint"], 16 * 2)

    esB3.close()
    esB.close()
    clear_block()
    if max_phase <= 7:
        if debug_dumps:
            dump_block([(0, intpm_d[:, :], 256 * L),
                        (512 * 1024, QsT[:, 0:2048], 128 * 2048),
                        (768 * 1024, KT[:, 0:2048], 128 * 2048),
                        (1024 * 1024, V_sb[:, 0:1024], 128 * 1024),
                        (1152 * 1024, bass.AP(G_sb, 0, [[NT * E, 128], [1, 128]]), 128 * 128),
                        (1168 * 1024, gsumrow[0:1, :], 2048)])
        es_bc.close()
        es_persist.close()
        es_sem.close()
        return nc


    esC = ExitStack()
    rkt_sb = [esC.enter_context(nc.sbuf_tensor(f"rkt_sb{i}", [128, 4096], BF16)) for i in range(2)]
    qeT = esC.enter_context(nc.sbuf_tensor("qeT", [128, L], BF16))
    qpT = esC.enter_context(nc.sbuf_tensor("qpT", [128, L], BF16))
    band_sb = [esC.enter_context(nc.sbuf_tensor(f"band_sb{i}", [128, W], BF16)) for i in range(2)]
    P_sb = [esC.enter_context(nc.sbuf_tensor(f"P_sb{i}", [128, L], BF16)) for i in range(2)]
    A_sb = [esC.enter_context(nc.sbuf_tensor(f"A_sb{i}", [128, L], BF16)) for i in range(2)]
    AT4 = esC.enter_context(nc.sbuf_tensor("AT4", [128, 16 * 512], BF16))
    sums_sb = esC.enter_context(nc.sbuf_tensor("sums_sb", [128, 4], F32))

    pC_S = esC.enter_context(nc.psum_tensor("pC_S", [128, L], F32))
    pC_b = [esC.enter_context(nc.psum_tensor(f"pC_b{i}", [128, 512], F32)) for i in range(2)]
    pC_at = esC.enter_context(nc.psum_tensor("pC_at", [128, 1024], BF16))
    pC_av = esC.enter_context(nc.psum_tensor("pC_av", [128, 512], F32))

    CH_W = [512, 512, 512, 512, 128]
    CH_OFF = [0, 512, 1024, 1536, 2048]
    NIT = H * NT
    if cnit is not None:
        NIT = cnit
    VCH = (0, 1)
    SCH = (2, 3, 4)

    def bevac_req(n):
        b, c = divmod(n, 5)
        if c in VCH:
            return ("sB", 2 * b + c + 1)
        return ("sC", 3 * b + c - 1)

    with nc.Block() as blk:
        @blk.tensor
        def _(t):
            nchunk = [0]

            def emit_band():
                n = nchunk[0]
                if n >= 5 * NIT:
                    return
                bi, c = divmod(n, 5)
                hb, jb = divmod(bi, NT)
                if c == 0 and jb == 0:
                    t.wait_ge(sm["sA"], hb + 1)
                    t.wait_ge(dm["Crk0" if hb % 2 == 0 else "Crk1"],
                              16 * (hb // 2 + 1))
                if n >= 2:
                    sname, cntv = bevac_req(n - 2)
                    t.wait_ge(sm[sname], cntv)
                j0 = 1920 - 128 * jb
                t.matmul(pC_b[n % 2][:, 0:CH_W[c]], qpT[:, jb * 128:(jb + 1) * 128],
                         rkt_sb[hb % 2][:, j0 + CH_OFF[c]: j0 + CH_OFF[c] + CH_W[c]],
                         start=True, stop=True).then_inc(sm["sD"], 1)
                nchunk[0] += 1

            def trans_batch(jj, half):
                if half == 0:
                    t.wait_ge(sm["sJ"], jj + 1)
                t.wait_ge(sm["sK"], 2 * jj + half)
                for kt in range(8):
                    ktg = half * 8 + kt
                    ins = nc.tensor.transpose(pC_at[:, kt * 128:(kt + 1) * 128],
                                              A_sb[jj % 2][:, ktg * 128:(ktg + 1) * 128],
                                              ident_bf[:, :])
                    if kt == 7:
                        ins.then_inc(sm["sL"], 1)

            def av_group(j):
                g = j // 4
                hh = j // NT
                t.wait_ge(sm["sK"], 2 * (j + 1))
                if g >= 1:
                    t.wait_ge(sm["sM"], g)
                for kt in range(16):
                    ins = t.matmul(pC_av[:, :],
                                   V_sb[:, kt * D + hh * 128: kt * D + (hh + 1) * 128],
                                   AT4[:, kt * 512:(kt + 1) * 512],
                                   start=(kt == 0), stop=(kt == 15))
                    if kt == 15:
                        ins.then_inc(sm["sP"], 1)

            for _n in range(15):
                emit_band()
            for i in range(NIT):
                h, j = divmod(i, NT)
                t.wait_ge(dm["Cpi0" if i % 2 == 0 else "Cpi1"],
                          16 * (i // 2 + 1))
                if j == 0:
                    t.wait_ge(sm["sF"], h + 1)
                for ch in range(4):
                    if i >= 1 and ch == 0:
                        t.wait_ge(sm["sH"], 2 * i - 1)
                    if i >= 1 and ch == 2:
                        t.wait_ge(sm["sH"], 2 * i)
                    t.matmul(pC_S[:, ch * 512:(ch + 1) * 512],
                             qeT[:, j * 128:(j + 1) * 128],
                             KT[:, h * L + ch * 512: h * L + (ch + 1) * 512],
                             start=True, stop=False)
                for ch in range(4):
                    ins = t.matmul(pC_S[:, ch * 512:(ch + 1) * 512], ident_bf[:, :],
                                   P_sb[i % 2][:, ch * 512:(ch + 1) * 512],
                                   start=False, stop=True)
                    if ch == 1:
                        ins.then_inc(sm["sT"], 1)
                    if ch == 3:
                        ins.then_inc(sm["sI"], 1)
                if i >= 1:
                    trans_batch(i - 1, 0)
                emit_band()
                emit_band()
                if i >= 1:
                    trans_batch(i - 1, 1)
                emit_band()
                if i >= 1 and (i - 1) % 4 == 3:
                    av_group(i - 1)
                emit_band()
                emit_band()
            trans_batch(NIT - 1, 0)
            trans_batch(NIT - 1, 1)
            av_group(NIT - 1)

        @blk.scalar
        def _(s):
            s.dma_start(rkt_sb[0][:, :], rkt_d[0:128, :]).then_inc(dm["Crk0"], 16)
            s.dma_start(rkt_sb[1][:, :], rkt_d[128:256, :]).then_inc(dm["Crk1"], 16)
            s.activation(qpT[:, :], QsT[:, 0:L], AF.Identity,
                         bias=rpbT[:, 0:1], scale=1.0).then_inc(sm["sA"], 1)
            s.activation(qeT[:, :], QsT[:, 0:L], AF.Identity,
                         bias=rebT[:, 0:1], scale=1.0).then_inc(sm["sF"], 1)
            for bi0 in (0, 1, 2):
                for c in SCH:
                    n = bi0 * 5 + c
                    s.wait_ge(sm["sD"], n + 1)
                    s.activation(band_sb[bi0 % 2][:, CH_OFF[c]:CH_OFF[c] + CH_W[c]],
                                 pC_b[n % 2][:, 0:CH_W[c]], AF.Copy).then_inc(sm["sC"], 1)
            for i in range(NIT):
                h, j = divmod(i, NT)
                s.wait_ge(sm["sT"], i + 1)
                if i >= 2:
                    s.wait_ge(sm["sJ"], i - 1)
                    s.wait_ge(sm["sL"], 2 * (i - 1))
                for hf in range(2):
                    if hf == 1:
                        s.wait_ge(sm["sI"], i + 1)
                    nc.scalar.activation(A_sb[i % 2][:, hf * 1024:(hf + 1) * 1024],
                                         pC_S[:, hf * 1024:(hf + 1) * 1024], AF.Exp,
                                         bias=cm20[:, :], scale=1.0,
                                         accum_out=sums_sb[:, 2 * (i % 2) + hf:
                                                           2 * (i % 2) + hf + 1]
                                         ).then_inc(sm["sH"], 1)
                if j == NT - 1 and h < H - 1:
                    s.activation(qeT[:, :], QsT[:, (h + 1) * L:(h + 2) * L], AF.Identity,
                                 bias=rebT[:, h + 1:h + 2], scale=1.0).then_inc(sm["sF"], 1)
                bi = i + 3
                if bi < NIT:
                    for c in SCH:
                        n = bi * 5 + c
                        s.wait_ge(sm["sD"], n + 1)
                        if bi >= 2 and c == SCH[0]:
                            s.wait_ge(dm["Cbo0" if bi % 2 == 0 else "Cbo1"],
                                      16 * (bi // 2))
                        s.activation(band_sb[bi % 2][:, CH_OFF[c]:CH_OFF[c] + CH_W[c]],
                                     pC_b[n % 2][:, 0:CH_W[c]],
                                     AF.Copy).then_inc(sm["sC"], 1)
                if j == NT - 4 and h < H - 1:
                    s.wait_ge(sm["sD"], 5 * NT * (h + 1))
                    s.activation(qpT[:, :], QsT[:, (h + 1) * L:(h + 2) * L], AF.Identity,
                                 bias=rpbT[:, h + 1:h + 2], scale=1.0).then_inc(sm["sA"], 1)
                    if h + 2 < H:
                        s.dma_start(rkt_sb[(h + 2) % 2][:, :],
                                    rkt_d[(h + 2) * 128:(h + 3) * 128, :]
                                    ).then_inc(dm["Crk0" if h % 2 == 0 else "Crk1"], 16)

        @blk.vector
        def _(v):
            for bi0 in (0, 1, 2):
                for c in VCH:
                    n = bi0 * 5 + c
                    v.wait_ge(sm["sD"], n + 1)
                    v.tensor_copy(band_sb[bi0 % 2][:, CH_OFF[c]:CH_OFF[c] + CH_W[c]],
                                  pC_b[n % 2][:, 0:CH_W[c]]).then_inc(sm["sB"], 1)

            def at_evac(jj, half):
                v.wait_ge(sm["sL"], 2 * jj + half + 1)
                if half == 0 and jj >= 4:
                    v.wait_ge(sm["sP"], jj // 4)
                dst = bass.AP(AT4, half * 8 * 512 + (jj % 4) * 128,
                              [[16 * 512, 128], [512, 8], [1, 128]])
                srcp = bass.AP(pC_at, 0, [[1024, 128], [128, 8], [1, 128]])
                v.tensor_copy(dst, srcp).then_inc(sm["sK"], 1)

            for i in range(NIT):
                jj = i - 1
                bi = i + 3

                def bevac(c):
                    n = bi * 5 + c
                    v.wait_ge(sm["sD"], n + 1)
                    if bi >= 2 and c == 0:
                        v.wait_ge(dm["Cbo0" if bi % 2 == 0 else "Cbo1"],
                                  16 * (bi // 2))
                    v.tensor_copy(band_sb[bi % 2][:, CH_OFF[c]:CH_OFF[c] + CH_W[c]],
                                  pC_b[n % 2][:, 0:CH_W[c]]).then_inc(sm["sB"], 1)

                if jj >= 0:
                    at_evac(jj, 0)
                if bi < NIT:
                    bevac(0)
                if jj >= 0:
                    at_evac(jj, 1)
                if bi < NIT:
                    bevac(1)
                jj2 = i - 2
                if jj2 >= 3 and jj2 % 4 == 3:
                    g = jj2 // 4
                    v.wait_ge(sm["sP"], g + 1)
                    v.tensor_copy(coT[:, (g // 4) * L + (g % 4) * 512:
                                      (g // 4) * L + (g % 4) * 512 + 512],
                                  pC_av[:, :]).then_inc(sm["sM"], 1)
                v.wait_ge(sm["sH"], 2 * (i + 1))
                ins = v.tensor_add(tiny[:, 14:15], sums_sb[:, 2 * (i % 2):2 * (i % 2) + 1],
                                   sums_sb[:, 2 * (i % 2) + 1:2 * (i % 2) + 2])
                ins.then_inc(sm["vch"], 1)
                v.wait_ge(sm["vch"], 2 * i + 1)
                ins = v.reciprocal(tiny[:, 12 + i % 2:13 + i % 2], tiny[:, 14:15])
                ins.then_inc(sm["vch"], 1)
                v.wait_ge(sm["vch"], 2 * i + 2)
                v.tensor_scalar(A_sb[i % 2][:, :], A_sb[i % 2][:, :],
                                tiny[:, 12 + i % 2:13 + i % 2], None,
                                ALU.mult).then_inc(sm["sJ"], 1)
            jj = NIT - 1
            at_evac(jj, 0)
            at_evac(jj, 1)
            g = NIT // 4 - 1
            v.wait_ge(sm["sP"], g + 1)
            v.tensor_copy(coT[:, (g // 4) * L + (g % 4) * 512:
                              (g // 4) * L + (g % 4) * 512 + 512],
                          pC_av[:, :]).then_inc(sm["sM"], 1)

        @blk.gpsimd
        def _(gp):
            for bi in (0, 1, 2):
                gp.wait_ge(sm["sB"], 2 * (bi + 1))
                gp.wait_ge(sm["sC"], 3 * (bi + 1))
                if bi >= 2:
                    gp.wait_ge(dm["Cpi0" if bi % 2 == 0 else "Cpi1"],
                               16 * (bi // 2))
                gp.dma_start(bounce_d[bi % 2][:, :],
                             band_sb[bi % 2][:, :]
                             ).then_inc(dm["Cbo0" if bi % 2 == 0 else "Cbo1"], 16)
            for i in range(NIT):
                bi = i + 3
                if bi < NIT:
                    gp.wait_ge(sm["sB"], 2 * (bi + 1))
                    gp.wait_ge(sm["sC"], 3 * (bi + 1))
                    gp.wait_ge(dm["Cpi0" if bi % 2 == 0 else "Cpi1"],
                               16 * (bi // 2))
                    gp.dma_start(bounce_d[bi % 2][:, :],
                                 band_sb[bi % 2][:, :]
                                 ).then_inc(dm["Cbo0" if bi % 2 == 0 else "Cbo1"], 16)

        @blk.sync
        def _(sy):
            for bi in range(NIT):
                sy.wait_ge(dm["Cbo0" if bi % 2 == 0 else "Cbo1"], 16 * (bi // 2 + 1))
                if bi >= 2:
                    sy.wait_ge(sm["sI"], bi - 1)
                sy.dma_start(P_sb[bi % 2][:, :],
                             bass.AP(bounce_d[bi % 2], 127, [[W - 1, 128], [1, L]])
                             ).then_inc(dm["Cpi0" if bi % 2 == 0 else "Cpi1"], 16)

    esC.close()
    if max_phase <= 8:
        es_bc.close()
        es_persist.close()
        es_sem.close()
        return nc


    esS = ExitStack()
    woutn = esS.enter_context(nc.sbuf_tensor("woutn", [128, ND * D], BF16))
    s_st = [esS.enter_context(nc.sbuf_tensor(f"s_st{i}", [128, D], BF16)) for i in range(2)]
    pS = esS.enter_context(nc.psum_tensor("pS", [128, D], F32))

    with nc.Block() as blk:
        @blk.gpsimd
        def _(gp):
            src = bass.AP(wout_d, 0, [[D, 128], [128 * D, ND], [1, D]])
            gp.dma_start(bass.AP(woutn, 0, [[ND * D, 128], [D, ND], [1, D]]),
                         src).then_inc(dm["Swo"], 16)
            for tt in range(NT):
                gp.wait_ge(sm["sW"], tt + 1)
                gp.dma_start(s_dram[tt * 128:(tt + 1) * 128, :],
                             s_st[tt % 2][:, :]).then_inc(dm["Sst0" if tt % 2 == 0 else "Sst1"], 16)
            gp.wait_ge(dm["Sst0"], 16 * (NT // 2))
            gp.wait_ge(dm["Sst1"], 16 * (NT // 2))

        @blk.tensor
        def _(t):
            t.wait_ge(dm["Swo"], 16)
            for tt in range(NT):
                if tt >= 1:
                    t.wait_ge(sm["sW"], tt)
                for ch in range(2):
                    for hv in range(ND):
                        t.matmul(pS[:, ch * 512:(ch + 1) * 512],
                                 coT[:, hv * L + tt * 128: hv * L + (tt + 1) * 128],
                                 woutn[:, hv * D + ch * 512: hv * D + ch * 512 + 512],
                                 start=(hv == 0), stop=False)
                    ins = t.matmul(pS[:, ch * 512:(ch + 1) * 512], ones_row[:, :],
                                   bout_bf[0:1, ch * 512:(ch + 1) * 512],
                                   start=False, stop=True)
                    if ch == 1:
                        ins.then_inc(sm["sR"], 1)

        @blk.scalar
        def _(s):
            for tt in range(NT):
                s.wait_ge(sm["sR"], tt + 1)
                if tt >= 2:
                    s.wait_ge(dm["Sst0" if tt % 2 == 0 else "Sst1"], 16 * (tt // 2))
                s.activation(s_st[tt % 2][:, :], pS[:, :], AF.Copy).then_inc(sm["sW"], 1)

    esS.close()
    es_bc.close()
    clear_block()
    if max_phase <= 9:
        if debug_dumps:
            dump_block([(0, s_dram[:, :], L * D)])
        es_persist.close()
        es_sem.close()
        return nc


    esD = ExitStack()
    wiT = esD.enter_context(nc.sbuf_tensor("wiT", [128, ND * F], BF16))
    woT = esD.enter_context(nc.sbuf_tensor("woT", [128, NF * D], BF16))
    xD_t = [esD.enter_context(nc.sbuf_tensor(f"xD_t{i}", [128, D], F32)) for i in range(2)]
    xnD = [esD.enter_context(nc.sbuf_tensor(f"xnD{i}", [128, D], BF16)) for i in range(2)]
    xnT_t = [esD.enter_context(nc.sbuf_tensor(f"xnT_t{i}", [128, D], BF16)) for i in range(2)]
    sD_t = [esD.enter_context(nc.sbuf_tensor(f"sD_t{i}", [128, D], BF16)) for i in range(2)]
    ipm_t = [esD.enter_context(nc.sbuf_tensor(f"ipm_t{i}", [32, E * 128], BF16)) for i in range(2)]
    btp_sb = [esD.enter_context(nc.sbuf_tensor(f"btp_sb{i}", [32, 1024], BF16)) for i in range(4)]
    rA = [esD.enter_context(nc.sbuf_tensor(f"rA{i}", [128, 512], BF16)) for i in range(2)]
    rB = [esD.enter_context(nc.sbuf_tensor(f"rB{i}", [128, 512], BF16)) for i in range(2)]
    mixn = esD.enter_context(nc.sbuf_tensor("mixn", [128, 1024], BF16))
    mixT_sb = esD.enter_context(nc.sbuf_tensor("mixT_sb", [128, 1024], BF16))
    gD = esD.enter_context(nc.sbuf_tensor("gD", [128, D], F32))
    tmpD = esD.enter_context(nc.sbuf_tensor("tmpD", [128, D], F32))
    outD = [esD.enter_context(nc.sbuf_tensor(f"outD{i}", [128, D], F32)) for i in range(2)]

    bi_bf = esD.enter_context(nc.sbuf_tensor("bi_bf", [1, F], BF16))
    bo_bf = esD.enter_context(nc.sbuf_tensor("bo_bf", [1, D], BF16))
    pD_z = esD.enter_context(nc.psum_tensor("pD_z", [128, 1024], F32))
    pD_mx = esD.enter_context(nc.psum_tensor("pD_mx", [128, 1024], F32))
    pD_mt = esD.enter_context(nc.psum_tensor("pD_mt", [128, 1024], BF16))
    pD_f = esD.enter_context(nc.psum_tensor("pD_f", [128, D], F32))
    pD_xn = esD.enter_context(nc.psum_tensor("pD_xn", [128, 1024], BF16))

    NQ = 4

    with nc.Block() as blk:
        @blk.gpsimd
        def _(gp):
            src = bass.AP(wiT_d, 0, [[F, 128], [128 * F, ND], [1, F]])
            gp.dma_start(bass.AP(wiT, 0, [[ND * F, 128], [F, ND], [1, F]]),
                         src).then_inc(dm["DwiT"], 16)
            gp.dma_start(bi_bf[:, :], bi_d[:, :]).then_inc(dm["DwiT"], 16)
            gp.dma_start(bo_bf[:, :], bo_d[:, :]).then_inc(dm["DwiT"], 16)
            nb = 0
            BTS = ["Dbt0", "Dbt1", "Dbt2", "Dbt3"]
            for tt in range(NT):
                for qtr in range(NQ):
                    for e in range(E):
                        if nb >= 4:
                            gp.wait_ge(sm["sM"], 2 * (nb - 3))
                        gp.dma_start(btp_sb[nb % 4][:, :],
                                     bass.AP(btp_d, e * 32 * F + qtr * 1024,
                                             [[F, 32], [1, 1024]])
                                     ).then_inc(dm[BTS[nb % 4]], 16)
                        nb += 1


        @blk.sync
        def _(sy):
            src = bass.AP(woT_d, 0, [[D, 128], [128 * D, NF], [1, D]])
            sy.dma_start(bass.AP(woT, 0, [[NF * D, 128], [D, NF], [1, D]]),
                         src).then_inc(dm["DwoT"], 16)
            for tt in range(NT):
                if tt >= 2:
                    sy.wait_ge(sm["sD"], tt - 1)
                sy.dma_start(xD_t[tt % 2][:, :],
                             x_d[tt * 128:(tt + 1) * 128, :]
                             ).then_inc(dm["Dxs0" if tt % 2 == 0 else "Dxs1"], 16)
                sy.dma_start(sD_t[tt % 2][:, :],
                             s_dram[tt * 128:(tt + 1) * 128, :]
                             ).then_inc(dm["Dxs0" if tt % 2 == 0 else "Dxs1"], 16)
                if tt >= 2:
                    sy.wait_ge(sm["sP"], tt - 1)
                sy.dma_start(bass.AP(ipm_t[tt % 2], 0, [[E * 128, 32], [128, E], [1, 128]]),
                             bass.AP(intpm_d, tt * 128,
                                     [[L, 32], [32 * L, E], [1, 128]])
                             ).then_inc(dm["Dip0" if tt % 2 == 0 else "Dip1"], 16)
                if tt >= 1:
                    sy.wait_ge(sm["sD"], tt)
                    sy.dma_start(out_d[(tt - 1) * 128:tt * 128, :],
                                 outD[(tt - 1) % 2][:, :]
                                 ).then_inc(dm["Dout0" if (tt - 1) % 2 == 0 else "Dout1"], 16)
            sy.wait_ge(sm["sD"], NT)
            sy.dma_start(out_d[(NT - 1) * 128:NT * 128, :],
                         outD[(NT - 1) % 2][:, :]
                         ).then_inc(dm["Dout0" if (NT - 1) % 2 == 0 else "Dout1"], 16)
            sy.wait_ge(dm["Dout0"], 16 * (NT // 2))
            sy.wait_ge(dm["Dout1"], 16 * (NT // 2))

        @blk.scalar
        def _(s):
            for tt in range(NT):
                s.wait_ge(dm["Dxs0" if tt % 2 == 0 else "Dxs1"], 32 * (tt // 2 + 1))
                if tt >= 2:
                    s.wait_ge(sm["sK"], tt - 1)
                nc.scalar.activation(xnD[tt % 2][:, :], xD_t[tt % 2][:, :], AF.Identity,
                                     bias=stats[:, NT + tt:NT + tt + 1],
                                     scale=stats[:, tt:tt + 1]).then_inc(sm["sL"], 1)
                if tt >= 1:
                    if tt >= 2:
                        s.wait_ge(sm["sD"], tt - 1)
                    s.wait_ge(sm["sP"], tt)
                    nc.scalar.activation(gD[:, :], pD_f[:, :], AF.Silu).then_inc(sm["sQ"], 1)
                for qtr in range(NQ):
                    for e in range(E):
                        idx = (tt * NQ + qtr) * E + e
                        s.wait_ge(sm["sM"], 2 * idx + 1)
                        if idx >= 2:
                            s.wait_ge(sm["sN"], idx - 1)
                        nc.scalar.activation(rA[idx % 2][:, :], pD_z[:, 0:512], AF.Relu,
                                             bias=0.0,
                                             scale=G_sb[:, tt * E + e: tt * E + e + 1]
                                             ).then_inc(sm["sO"], 1)
                if tt == NT - 1:
                    s.wait_ge(sm["sD"], tt)
                    s.wait_ge(sm["sP"], tt + 1)
                    nc.scalar.activation(gD[:, :], pD_f[:, :], AF.Silu).then_inc(sm["sQ"], 1)

        @blk.vector
        def _(v):
            for tt in range(NT):
                v.wait_ge(sm["sK"], tt + 1)
                if tt >= 2:
                    v.wait_ge(sm["sP"], tt - 1)
                v.tensor_copy(xnT_t[tt % 2][:, :], pD_xn[:, :]).then_inc(sm["sR"], 1)
                if tt >= 1:
                    ttp = tt - 1
                    v.wait_ge(sm["sQ"], ttp + 1)
                    v.wait_ge(dm["Dxs0" if ttp % 2 == 0 else "Dxs1"], 32 * (ttp // 2 + 1))
                    if ttp >= 2:
                        v.wait_ge(dm["Dout0" if ttp % 2 == 0 else "Dout1"], 16 * (ttp // 2))
                    ins = v.tensor_mul(tmpD[:, :], sD_t[ttp % 2][:, :], gD[:, :])
                    ins.then_inc(sm["vch"], 1)
                    v.wait_ge(sm["vch"], ttp + 1)
                    v.tensor_add(outD[ttp % 2][:, :], tmpD[:, :],
                                 xD_t[ttp % 2][:, :]).then_inc(sm["sD"], 1)
                for qtr in range(NQ):
                    qi = tt * NQ + qtr
                    for e in range(E):
                        idx = qi * E + e
                        v.wait_ge(sm["sM"], 2 * idx + 2)
                        if idx >= 2:
                            v.wait_ge(sm["sX"], idx - 1)
                        v.tensor_scalar(rB[idx % 2][:, :], pD_z[:, 512:1024],
                                        G_sb[:, tt * E + e: tt * E + e + 1], 0.0,
                                        ALU.mult, ALU.max).then_inc(sm["sS"], 1)
                    v.wait_ge(sm["sX"], 8 * (qi + 1))
                    v.tensor_copy(mixn[:, :], pD_mx[:, :]).then_inc(sm["sU"], 1)
                    v.wait_ge(sm["sV"], qi + 1)
                    v.tensor_copy(mixT_sb[:, :], pD_mt[:, :]).then_inc(sm["sW"], 1)
                if tt == NT - 1:
                    v.wait_ge(sm["sQ"], tt + 1)
                    v.wait_ge(dm["Dxs0" if tt % 2 == 0 else "Dxs1"], 32 * (tt // 2 + 1))
                    v.wait_ge(dm["Dout0" if tt % 2 == 0 else "Dout1"], 16 * (tt // 2))
                    ins = v.tensor_mul(tmpD[:, :], sD_t[tt % 2][:, :], gD[:, :])
                    ins.then_inc(sm["vch"], 1)
                    v.wait_ge(sm["vch"], tt + 1)
                    v.tensor_add(outD[tt % 2][:, :], tmpD[:, :],
                                 xD_t[tt % 2][:, :]).then_inc(sm["sD"], 1)

        @blk.tensor
        def _(t):
            t.wait_ge(dm["DwiT"], 48)
            t.wait_ge(dm["DwoT"], 16)
            pend = []

            def emit_down(k):
                for _ in range(k):
                    if not pend:
                        return
                    pend.pop(0)()

            def push_down(tt, qtr):
                qi = tt * NQ + qtr

                def first_waits(tt=tt, qtr=qtr, qi=qi):
                    t.wait_ge(sm["sW"], qi + 1)
                    if qtr == 0:
                        t.wait_ge(sm["sQ"], tt)
                first = [True]
                for ft in range(8):
                    for ch in range(2):
                        def job(tt=tt, qtr=qtr, ft=ft, ch=ch, first=first):
                            if first[0]:
                                first_waits()
                                first[0] = False
                            ftg = qtr * 8 + ft
                            t.matmul(pD_f[:, ch * 512:(ch + 1) * 512],
                                     mixT_sb[:, ft * 128:(ft + 1) * 128],
                                     woT[:, ftg * D + ch * 512: ftg * D + ch * 512 + 512],
                                     start=(qtr == 0 and ft == 0), stop=False)
                        pend.append(job)
                if qtr == NQ - 1:
                    def bo_job(tt=tt):
                        for ch in range(2):
                            ins = t.matmul(pD_f[:, ch * 512:(ch + 1) * 512],
                                           gsumrow[0:1, tt * 128:(tt + 1) * 128],
                                           bo_bf[0:1, ch * 512:(ch + 1) * 512],
                                           start=False, stop=True)
                            if ch == 1:
                                ins.then_inc(sm["sP"], 1)
                    pend.append(bo_job)

            def mix_transposes(tt, qtr):
                qi = tt * NQ + qtr
                t.wait_ge(sm["sU"], qi + 1)
                if qi >= 1:
                    t.wait_ge(sm["sW"], qi)
                for ft in range(8):
                    ins = nc.tensor.transpose(pD_mt[:, ft * 128:(ft + 1) * 128],
                                              mixn[:, ft * 128:(ft + 1) * 128],
                                              ident_bf[:, :])
                    if ft == 7:
                        ins.then_inc(sm["sV"], 1)
                push_down(tt, qtr)

            prev_q = [None]
            for tt in range(NT):
                t.wait_ge(sm["sL"], tt + 1)
                if tt >= 2:
                    t.wait_ge(sm["sR"], tt - 1)
                for ct in range(ND):
                    ins = nc.tensor.transpose(pD_xn[:, ct * 128:(ct + 1) * 128],
                                              xnD[tt % 2][:, ct * 128:(ct + 1) * 128],
                                              ident_bf[:, :])
                    if ct == ND - 1:
                        ins.then_inc(sm["sK"], 1)
                if prev_q[0] is not None:
                    mix_transposes(*prev_q[0])
                    prev_q[0] = None
                    emit_down(len(pend))
                t.wait_ge(sm["sR"], tt + 1)
                t.wait_ge(dm["Dip0" if tt % 2 == 0 else "Dip1"], 16 * (tt // 2 + 1))
                for qtr in range(NQ):
                    qi = tt * NQ + qtr
                    for hf in range(2):
                        for dt in range(ND):
                            t.matmul(pD_z[:, hf * 512:(hf + 1) * 512],
                                     xnT_t[tt % 2][:, dt * 128:(dt + 1) * 128],
                                     wiT[:, dt * F + qtr * 1024 + hf * 512:
                                         dt * F + qtr * 1024 + hf * 512 + 512],
                                     start=(dt == 0), stop=False)
                        t.matmul(pD_z[:, hf * 512:(hf + 1) * 512], ones_row[:, :],
                                 bi_bf[0:1, qtr * 1024 + hf * 512: qtr * 1024 + hf * 512 + 512],
                                 start=False, stop=True)
                    if prev_q[0] is not None:
                        mix_transposes(*prev_q[0])
                        prev_q[0] = None
                    for e in range(E):
                        idx = qi * E + e
                        t.wait_ge(dm[["Dbt0", "Dbt1", "Dbt2", "Dbt3"][idx % 4]],
                                  16 * (idx // 4 + 1))
                        for hf in range(2):
                            if e >= 1:
                                t.wait_ge(sm["sO" if hf == 0 else "sS"], idx)
                            t.matmul(pD_z[:, hf * 512:(hf + 1) * 512],
                                     ipm_t[tt % 2][:, e * 128:(e + 1) * 128],
                                     btp_sb[idx % 4][:, hf * 512:(hf + 1) * 512],
                                     start=False, stop=False, skip_group_check=True
                                     ).then_inc(sm["sM"], 1)
                        if e >= 1:
                            ip = idx - 1
                            if e == 1 and qi >= 1:
                                t.wait_ge(sm["sU"], qi)
                            t.matmul(pD_mx[:, 0:512], ident_bf[:, :], rA[ip % 2][:, :],
                                     start=(e == 1), stop=False).then_inc(sm["sN"], 1)
                            t.matmul(pD_mx[:, 512:1024], ident_bf[:, :], rB[ip % 2][:, :],
                                     start=(e == 1), stop=False).then_inc(sm["sX"], 1)
                        emit_down(2)
                    ip = qi * E + E - 1
                    t.wait_ge(sm["sO"], ip + 1)
                    t.wait_ge(sm["sS"], ip + 1)
                    t.matmul(pD_mx[:, 0:512], ident_bf[:, :], rA[ip % 2][:, :],
                             start=False, stop=True).then_inc(sm["sN"], 1)
                    t.matmul(pD_mx[:, 512:1024], ident_bf[:, :], rB[ip % 2][:, :],
                             start=False, stop=True).then_inc(sm["sX"], 1)
                    emit_down(3)
                    prev_q[0] = (tt, qtr)
            mix_transposes(*prev_q[0])
            emit_down(len(pend))

    esD.close()
    es_persist.close()
    es_sem.close()
    return nc


def kernel(**inputs):
    i = inputs
    x = np.ascontiguousarray(i["L0_attn_q"], dtype=np.float32)
    pos = positions_np()
    posT = np.zeros((192, 4096), dtype=ml_dtypes.bfloat16)
    posT[:, :4095] = pos.T.astype(ml_dtypes.bfloat16)

    common = {
        "w_qs": np.ascontiguousarray(i["L0_attn_w_qs"], np.float32),
        "w_ks": np.ascontiguousarray(i["L0_attn_w_ks"], np.float32),
        "w_vs": np.ascontiguousarray(i["L0_attn_w_vs"], np.float32),
        "w_rel": np.ascontiguousarray(i["L0_attn_w_rel_pos_k"], np.float32),
        "reb": np.ascontiguousarray(np.asarray(i["L0_attn_rel_embed_bias"]).reshape(H, DK), np.float32),
        "rpb": np.ascontiguousarray(np.asarray(i["L0_attn_rel_pos_bias"]).reshape(H, DK), np.float32),
        "w_out": np.ascontiguousarray(i["L0_attn_w_out"], np.float32),
        "b_out": np.ascontiguousarray(np.asarray(i["L0_attn_b_out"]).reshape(1, D), np.float32),
        "wi": np.ascontiguousarray(i["L0_ffn_wi"], np.float32),
        "bi": np.ascontiguousarray(np.asarray(i["L0_ffn_bi"]).reshape(1, F), np.float32),
        "wo": np.ascontiguousarray(i["L0_ffn_wo"], np.float32),
        "bo": np.ascontiguousarray(np.asarray(i["L0_ffn_bo"]).reshape(1, D), np.float32),
        "lora_A": np.ascontiguousarray(np.asarray(i["L0_ffn_lora_A"]).reshape(E * 16, D), np.float32),
        "lora_B": np.ascontiguousarray(np.asarray(i["L0_ffn_lora_B"]).reshape(E * F, 16), np.float32),
        "gate_w": np.ascontiguousarray(i["L0_ffn_gate_w"], np.float32),
        "gate_b": np.ascontiguousarray(np.asarray(i["L0_ffn_gate_b"]).reshape(1, E), np.float32),
        "posT": posT,
    }
    nc = build()
    in_maps = [dict(common, x=np.ascontiguousarray(x[b])) for b in range(8)]
    res = run_bass_kernel_spmd(nc, in_maps, core_ids=list(range(8)))
    return np.stack([np.asarray(res.results[b]["out"], np.float32) for b in range(8)], axis=0)




# version 26 (final)
# speedup vs baseline: 1.0674x; 1.0248x over previous
import math
import numpy as np
import ml_dtypes
from contextlib import ExitStack

import concourse.bass as bass
import concourse.mybir as mybir
from concourse.bass_utils import run_bass_kernel_spmd

F32 = mybir.dt.float32
BF16 = mybir.dt.bfloat16
AF = mybir.ActivationFunctionType
ALU = mybir.AluOpType
AX = mybir.AxisListType

L = 2048
D = 1024
H = 8
DK = 128
F = 4096
E = 8
NT = L // 128
ND = D // 128
NF = F // 128
W = 2176
SOFT_SHIFT = -20.0
QSCALE = DK ** -0.5


def positions_np():
    from scipy.special import gammaln
    d = np.arange(-L + 1, L, dtype=np.float64)
    n = 192 // 6
    absd = np.abs(d)[:, None]
    half_life = 2.0 ** np.linspace(3.0, math.log(L) / math.log(2.0), n)
    feat_exp = np.exp(-math.log(2.0) / half_life[None, :] * absd)
    center_widths = 2.0 ** np.arange(1, n + 1, dtype=np.float64) - 1.0
    feat_cm = (center_widths[None, :] > absd).astype(np.float64)
    stddev = L / (2.0 * n)
    mean = np.linspace(L / n, float(L), n)[None, :]
    conc = (mean / stddev) ** 2
    rate = mean / stddev ** 2
    with np.errstate(divide="ignore"):
        log_absd = np.where(absd > 0, np.log(np.where(absd > 0, absd, 1.0)), -np.inf)
    log_unnorm = (conc - 1.0) * log_absd - rate * absd
    log_norm = gammaln(conc) - conc * np.log(rate)
    probs = np.exp(log_unnorm - log_norm) + 1e-08
    feat_gamma = probs / np.max(probs, axis=-1, keepdims=True)
    emb = np.concatenate([feat_exp, feat_cm, feat_gamma], axis=-1)
    emb = np.concatenate([emb, np.sign(d)[:, None] * emb], axis=-1)
    return emb.astype(np.float32)


CDBG = set()


def build(max_phase=99, debug_dumps=False, cnit=None):
    nc = bass.Bass("TRN2", target_bir_lowering=False, detect_race_conditions=False)

    x_d = nc.dram_tensor("x", [L, D], F32, kind="ExternalInput")
    wqs_d = nc.dram_tensor("w_qs", [D, D], F32, kind="ExternalInput")
    wks_d = nc.dram_tensor("w_ks", [D, D], F32, kind="ExternalInput")
    wvs_d = nc.dram_tensor("w_vs", [D, D], F32, kind="ExternalInput")
    wrel_d = nc.dram_tensor("w_rel", [192, D], F32, kind="ExternalInput")
    reb_d = nc.dram_tensor("reb", [H, DK], F32, kind="ExternalInput")
    rpb_d = nc.dram_tensor("rpb", [H, DK], F32, kind="ExternalInput")
    wout_d = nc.dram_tensor("w_out", [D, D], F32, kind="ExternalInput")
    bout_d = nc.dram_tensor("b_out", [1, D], F32, kind="ExternalInput")
    wi_d = nc.dram_tensor("wi", [F, D], F32, kind="ExternalInput")
    bi_d = nc.dram_tensor("bi", [1, F], F32, kind="ExternalInput")
    wo_d = nc.dram_tensor("wo", [D, F], F32, kind="ExternalInput")
    bo_d = nc.dram_tensor("bo", [1, D], F32, kind="ExternalInput")
    lA_d = nc.dram_tensor("lora_A", [E * 16, D], F32, kind="ExternalInput")
    lB_d = nc.dram_tensor("lora_B", [E * F, 16], F32, kind="ExternalInput")
    gw_d = nc.dram_tensor("gate_w", [E, D], F32, kind="ExternalInput")
    gb_d = nc.dram_tensor("gate_b", [1, E], F32, kind="ExternalInput")
    posT_d = nc.dram_tensor("posT", [192, 4096], BF16, kind="ExternalInput")
    out_d = nc.dram_tensor("out", [L, D], F32, kind="ExternalOutput")

    bounce_d = [nc.dram_tensor(f"bounce{i}", [128, W], BF16, kind="Internal") for i in range(2)]
    rkt_d = nc.dram_tensor("rkt", [H * 128, 4096], BF16, kind="Internal")
    intpm_d = nc.dram_tensor("intpm", [E * 32, L], BF16, kind="Internal")
    btp_d = nc.dram_tensor("btp", [E * 32, F], BF16, kind="Internal")
    s_dram = nc.dram_tensor("s_dram", [L, D], BF16, kind="Internal")
    wiT_d = nc.dram_tensor("wiT_d", [ND * 128, F], BF16, kind="Internal")
    woT_d = nc.dram_tensor("woT_d", [NF * 128, D], BF16, kind="Internal")

    ident_bf_d = nc.inline_tensor(np.eye(128, dtype=ml_dtypes.bfloat16), "ident_bf_d")
    ident_f_d = nc.inline_tensor(np.eye(128, dtype=np.float32), "ident_f_d")
    ones_row_d = nc.inline_tensor(np.ones((1, 128), dtype=ml_dtypes.bfloat16), "ones_row_d")

    es_persist = ExitStack()
    P = es_persist.enter_context

    ident_bf = P(nc.sbuf_tensor("ident_bf", [128, 128], BF16))
    ident_f = P(nc.sbuf_tensor("ident_f", [128, 128], F32))
    ones_row = P(nc.sbuf_tensor("ones_row", [1, 128], BF16))
    rebT = P(nc.sbuf_tensor("rebT", [128, H], F32))
    rpbT = P(nc.sbuf_tensor("rpbT", [128, H], F32))
    gwT = P(nc.sbuf_tensor("gwT", [128, ND * E], BF16))
    Apm = P(nc.sbuf_tensor("Apm", [128, ND * 256], BF16))
    bout_bf = P(nc.sbuf_tensor("bout_bf", [1, D], BF16))
    gb_bf = P(nc.sbuf_tensor("gb_bf", [1, E], BF16))
    G_sb = P(nc.sbuf_tensor("G_sb", [128, NT * E], F32))
    gsumrow = P(nc.sbuf_tensor("gsumrow", [1, L], BF16))
    stats = P(nc.sbuf_tensor("stats", [128, 2 * NT], F32))
    tiny = P(nc.sbuf_tensor("tiny", [128, 16], F32))
    cm20 = P(nc.sbuf_tensor("cm20", [128, 1], F32))
    cqs = P(nc.sbuf_tensor("cqs", [128, 1], F32))

    SEMN = ["sA", "sB", "sC", "sD", "sE", "sF", "sG", "sH", "sI", "sJ",
            "sK", "sL", "sM", "sN", "sO", "sP", "sQ", "sR", "sS", "sT",
            "sU", "sV", "sW", "sX"]
    CHN = ["vch", "sch"]
    DSEMN = ["A1s", "A1g", "A2r", "A2b", "A3l0", "A3l1", "A3s0", "A3s1",
             "Bx0", "Bx1", "Bwv", "Bwqk", "Bint",
             "Cbo0", "Cbo1", "Cpi0", "Cpi1", "Crk0", "Crk1",
             "Swo", "Sst0", "Sst1",
             "DwiT", "DwoT", "Dbt0", "Dbt1", "Dbt2", "Dbt3", "Dxs0", "Dxs1",
             "Dip0", "Dip1", "Dout0", "Dout1"]
    es_sem = ExitStack()
    sm = {n: es_sem.enter_context(nc.semaphore(n)) for n in SEMN + CHN}
    dm = {n: es_sem.enter_context(nc.semaphore(n)) for n in DSEMN}

    def dump_block(pairs):
        out_flat = out_d
        with nc.Block() as blk:
            @blk.gpsimd
            def _(gp):
                k = [0]
                for off, srcap, n in pairs:
                    row0, col0 = divmod(off, D)
                    assert col0 == 0 and n % D == 0
                    dst = out_flat[row0:row0 + n // D, :]
                    gp.dma_start(dst, srcap).then_inc(dm["Dout0"], 16)
                    k[0] += 16
                gp.wait_ge(dm["Dout0"], k[0])
    def clear_block():
        with nc.Block() as blk:
            @blk.sync
            def _(sy):
                for n in SEMN + CHN:
                    sy.sem_clear(sm[n])

    esA = ExitStack()
    posTA = esA.enter_context(nc.sbuf_tensor("posTA", [128, 4096], BF16))
    posTB = esA.enter_context(nc.sbuf_tensor("posTB", [64, 4096], BF16))
    wrelA = esA.enter_context(nc.sbuf_tensor("wrelA", [128, D], BF16))
    wrelB = esA.enter_context(nc.sbuf_tensor("wrelB", [64, D], BF16))
    rkt_stage = esA.enter_context(nc.sbuf_tensor("rkt_stage", [128, 4096], BF16))
    An_sb = esA.enter_context(nc.sbuf_tensor("An_sb", [128, D], BF16))
    AT_sb = esA.enter_context(nc.sbuf_tensor("AT_sb", [128, ND * 128], BF16))
    Bn_sb = esA.enter_context(nc.sbuf_tensor("Bn_sb", [128, E * 512], BF16))
    BT_sb = esA.enter_context(nc.sbuf_tensor("BT_sb", [16, F], BF16))
    BT_z = esA.enter_context(nc.sbuf_tensor("BT_z", [16, F], BF16))
    gwn = esA.enter_context(nc.sbuf_tensor("gwn", [E, D], BF16))
    rebn = esA.enter_context(nc.sbuf_tensor("rebn", [H, DK], F32))
    rpbn = esA.enter_context(nc.sbuf_tensor("rpbn", [H, DK], F32))
    brow_f = esA.enter_context(nc.sbuf_tensor("brow_f", [1, D], F32))

    pA1 = esA.enter_context(nc.psum_tensor("pA1", [128, 512], F32))
    pA2 = esA.enter_context(nc.psum_tensor("pA2", [128, 512], F32))
    pAt = esA.enter_context(nc.psum_tensor("pAt", [128, 128], F32))
    pAtb = esA.enter_context(nc.psum_tensor("pAtb", [16, 2048], BF16))
    pAab = esA.enter_context(nc.psum_tensor("pAab", [128, 128], BF16))
    pAg = esA.enter_context(nc.psum_tensor("pAg", [128, E], BF16))

    with nc.Block() as blk:
        @blk.sync
        def _(sy):
            sy.dma_start(posTA[:, :], posT_d[0:128, :]).then_inc(dm["A1s"], 16)
            sy.dma_start(posTB[:, :], posT_d[128:192, :]).then_inc(dm["A1s"], 16)
            sy.dma_start(rebn[:, :], reb_d[:, :]).then_inc(dm["A1s"], 16)
            sy.dma_start(rpbn[:, :], rpb_d[:, :]).then_inc(dm["A1s"], 16)
            sy.dma_start(brow_f[:, :], bout_d[:, :]).then_inc(dm["A1s"], 16)

        @blk.gpsimd
        def _(gp):
            gp.dma_start(wrelA[:, :], wrel_d[0:128, :]).then_inc(dm["A1g"], 16)
            gp.dma_start(wrelB[:, :], wrel_d[128:192, :]).then_inc(dm["A1g"], 16)
            gp.dma_start(An_sb[:, :], lA_d[:, :]).then_inc(dm["A1g"], 16)
            gp.dma_start(gwn[:, :], gw_d[:, :]).then_inc(dm["A1g"], 16)
            gp.dma_start(gb_bf[:, :], gb_d[:, :]).then_inc(dm["A1g"], 16)
            for e in range(E):
                src = bass.AP(lB_d, e * F * 16, [[16, 128], [128 * 16, 32], [1, 16]])
                dstap = bass.AP(Bn_sb, e * 512, [[E * 512, 128], [16, 32], [1, 16]])
                gp.dma_start(dstap, src).then_inc(dm["A1g"], 16)
            gp.dma_start(ident_bf[:, :], ident_bf_d[:, :]).then_inc(dm["A1g"], 16)
            gp.dma_start(ident_f[:, :], ident_f_d[:, :]).then_inc(dm["A1g"], 16)
            gp.dma_start(ones_row[:, :], ones_row_d[:, :]).then_inc(dm["A1g"], 16)

    with nc.Block() as blk:
        @blk.tensor
        def _(t):
            t.wait_ge(dm["A1s"], 16 * 5)
            t.wait_ge(dm["A1g"], 16 * 16)
            mmv = 0
            for h in range(H):
                for ch in range(8):
                    pb = [pA1, pA2][mmv % 2]
                    if mmv >= 2:
                        t.wait_ge(sm["sD"], mmv - 1)
                    t.matmul(pb[:, :], wrelA[:, h * 128:(h + 1) * 128],
                             posTA[:, ch * 512:(ch + 1) * 512], start=True, stop=False)
                    t.matmul(pb[:, :], wrelB[:, h * 128:(h + 1) * 128],
                             posTB[:, ch * 512:(ch + 1) * 512],
                             start=False, stop=True).then_inc(sm["sC"], 1)
                    mmv += 1
            nc.tensor.transpose(pAt[:, 0:8], rebn[:, :], ident_f[0:8, 0:8])
            nc.tensor.transpose(pAt[:, 8:16], rpbn[:, :],
                                ident_f[0:8, 0:8]).then_inc(sm["sE"], 1)
            for dt in range(ND):
                nc.tensor.transpose(pAg[:, :], gwn[:, dt * 128:(dt + 1) * 128],
                                    ident_bf[0:8, 0:8]).then_inc(sm["sF"], 1)
                t.wait_ge(sm["sG"], dt + 1)
            for dt in range(ND):
                nc.tensor.transpose(pAab[:, :], An_sb[:, dt * 128:(dt + 1) * 128],
                                    ident_bf[:, :]).then_inc(sm["sH"], 1)
                t.wait_ge(sm["sI"], dt + 1)
            for e in range(E):
                for half in range(2):
                    for tt in range(16):
                        tg = half * 16 + tt
                        ins = nc.tensor.transpose(
                            pAtb[:, tt * 128:(tt + 1) * 128],
                            Bn_sb[:, e * 512 + tg * 16: e * 512 + tg * 16 + 16],
                            ident_bf[:, :])
                        if tt == 15:
                            ins.then_inc(sm["sJ"], 1)
                    t.wait_ge(sm["sK"], e * 2 + half + 1)

        @blk.vector
        def _(v):
            for h in range(H):
                if h >= 1:
                    v.wait_ge(dm["A2r"], 16 * h)
                for ch in range(8):
                    mmv = h * 8 + ch
                    v.wait_ge(sm["sC"], mmv + 1)
                    v.tensor_copy(rkt_stage[:, ch * 512:(ch + 1) * 512],
                                  [pA1, pA2][mmv % 2][:, :]).then_inc(sm["sD"], 1)
            v.wait_ge(sm["sE"], 1)
            v.tensor_copy(rebT[:, :], pAt[:, 0:8])
            v.tensor_copy(rpbT[:, :], pAt[:, 8:16])
            for dt in range(ND):
                v.wait_ge(sm["sF"], dt + 1)
                v.tensor_copy(gwT[:, dt * E:(dt + 1) * E], pAg[:, :]).then_inc(sm["sG"], 1)
            for dt in range(ND):
                v.wait_ge(sm["sH"], dt + 1)
                v.tensor_copy(AT_sb[:, dt * 128:(dt + 1) * 128],
                              pAab[:, :]).then_inc(sm["sI"], 1)
            v.memset(BT_z[:, :], 0.0).then_inc(sm["sL"], 1)
            for e in range(E):
                if e >= 1:
                    v.wait_ge(dm["A2b"], 16 * (2 * e + 1))
                for half in range(2):
                    v.wait_ge(sm["sJ"], e * 2 + half + 1)
                    v.tensor_copy(BT_sb[:, half * 2048:(half + 1) * 2048],
                                  pAtb[:, :]).then_inc(sm["sK"], 1)
            for dt in range(ND):
                for e in range(E):
                    base = dt * 256 + e * 32
                    v.tensor_copy(Apm[:, base:base + 16],
                                  AT_sb[:, dt * 128 + e * 16: dt * 128 + e * 16 + 16])
                    if e == 0:
                        v.memset(Apm[:, base + 16:base + 32], 0.0)
                    else:
                        v.tensor_scalar(Apm[:, base + 16:base + 32],
                                        AT_sb[:, dt * 128 + (e - 1) * 16: dt * 128 + e * 16],
                                        -1.0, None, ALU.mult)
            v.wait_ge(dm["A1s"], 16 * 5)
            v.tensor_copy(bout_bf[:, :], brow_f[:, :])
            v.memset(cm20[:, :], SOFT_SHIFT)
            v.memset(cqs[:, :], QSCALE)

        @blk.gpsimd
        def _(gp):
            for h in range(H):
                gp.wait_ge(sm["sD"], h * 8 + 8)
                gp.dma_start(rkt_d[h * 128:(h + 1) * 128, :],
                             rkt_stage[:, :]).then_inc(dm["A2r"], 16)
            gp.wait_ge(sm["sL"], 1)
            gp.dma_start(btp_d[16:32, :], BT_z[:, :]).then_inc(dm["A2b"], 16)
            for e in range(E):
                gp.wait_ge(sm["sK"], 2 * (e + 1))
                gp.dma_start(btp_d[e * 32:e * 32 + 16, :],
                             BT_sb[:, :]).then_inc(dm["A2b"], 16)
                if e < E - 1:
                    gp.dma_start(btp_d[(e + 1) * 32 + 16:(e + 2) * 32, :],
                                 BT_sb[:, :]).then_inc(dm["A2b"], 16)
            gp.wait_ge(dm["A2b"], 16 * 16)
            gp.wait_ge(dm["A2r"], 16 * H)

    esA.close()
    clear_block()
    if max_phase <= 1:
        if debug_dumps:
            dump_block([(0, rkt_d[0:128, :], 128 * 4096),
                        (512 * 1024, btp_d[0:64, :], 64 * F),
                        (768 * 1024, intpm_d[0:32, :], 32 * L)])
        es_persist.close()
        es_sem.close()
        return nc


    esA3 = ExitStack()
    natst = [esA3.enter_context(nc.sbuf_tensor(f"natst{i}", [128, 4096], BF16)) for i in range(2)]
    tost = [esA3.enter_context(nc.sbuf_tensor(f"tost{i}", [128, 1024], BF16)) for i in range(2)]
    pW = [esA3.enter_context(nc.psum_tensor(f"pW{i}", [128, 1024], BF16)) for i in range(2)]

    with nc.Block() as blk:
        @blk.gpsimd
        def _(gp):
            nld = 0
            for r in range(8):
                if nld >= 2:
                    gp.wait_ge(sm["sC"], 4 * (nld - 1))
                src = bass.AP(wi_d, r * 512 * D, [[D, 128], [128 * D, 4], [1, D]])
                dstap = bass.AP(natst[nld % 2], 0, [[4096, 128], [1024, 4], [1, D]])
                gp.dma_start(dstap, src).then_inc(dm["A3l0" if nld % 2 == 0 else "A3l1"], 16)
                nld += 1
            for dt in range(8):
                if nld >= 2:
                    gp.wait_ge(sm["sC"], 4 * (nld - 1))
                src = bass.AP(wo_d, dt * 128 * F, [[F, 128], [1, F]])
                gp.dma_start(natst[nld % 2][:, :],
                             src).then_inc(dm["A3l0" if nld % 2 == 0 else "A3l1"], 16)
                nld += 1

        @blk.tensor
        def _(t):
            nb = 0
            for r in range(8):
                t.wait_ge(dm["A3l0" if r % 2 == 0 else "A3l1"], 16 * (r // 2 + 1))
                for ftl in range(4):
                    if nb >= 2:
                        t.wait_ge(sm["sC"], nb - 1)
                    for dt in range(8):
                        ins = nc.tensor.transpose(
                            pW[nb % 2][:, dt * 128:(dt + 1) * 128],
                            natst[r % 2][:, ftl * 1024 + dt * 128: ftl * 1024 + (dt + 1) * 128],
                            ident_bf[:, :])
                        if dt == 7:
                            ins.then_inc(sm["sD"], 1)
                    nb += 1
            for dtw in range(8):
                nw = 8 + dtw
                t.wait_ge(dm["A3l0" if nw % 2 == 0 else "A3l1"], 16 * (nw // 2 + 1))
                for b4 in range(4):
                    if nb >= 2:
                        t.wait_ge(sm["sC"], nb - 1)
                    for ftl in range(8):
                        ins = nc.tensor.transpose(
                            pW[nb % 2][:, ftl * 128:(ftl + 1) * 128],
                            natst[nw % 2][:, b4 * 1024 + ftl * 128: b4 * 1024 + (ftl + 1) * 128],
                            ident_bf[:, :])
                        if ftl == 7:
                            ins.then_inc(sm["sD"], 1)
                    nb += 1

        @blk.vector
        def _(v):
            for nb in range(64):
                v.wait_ge(sm["sD"], nb + 1)
                if nb >= 2:
                    v.wait_ge(dm["A3s0" if nb % 2 == 0 else "A3s1"],
                              16 * (nb // 2))
                v.tensor_copy(tost[nb % 2][:, :], pW[nb % 2][:, :]).then_inc(sm["sC"], 1)

        @blk.sync
        def _(sy):
            nb = 0
            for r in range(8):
                for ftl in range(4):
                    sy.wait_ge(sm["sC"], nb + 1)
                    dst = bass.AP(wiT_d, (r * 4 + ftl) * 128,
                                  [[F, 128], [128 * F, 8], [1, 128]])
                    sy.dma_start(dst, bass.AP(tost[nb % 2], 0,
                                              [[1024, 128], [128, 8], [1, 128]])
                                 ).then_inc(dm["A3s0" if nb % 2 == 0 else "A3s1"], 16)
                    nb += 1
            for dtw in range(8):
                for b4 in range(4):
                    sy.wait_ge(sm["sC"], nb + 1)
                    dst = bass.AP(woT_d, (b4 * 8) * 128 * D + dtw * 128,
                                  [[D, 128], [128 * D, 8], [1, 128]])
                    sy.dma_start(dst, bass.AP(tost[nb % 2], 0,
                                              [[1024, 128], [128, 8], [1, 128]])
                                 ).then_inc(dm["A3s0" if nb % 2 == 0 else "A3s1"], 16)
                    nb += 1
            sy.wait_ge(dm["A3s0"], 16 * 32)
            sy.wait_ge(dm["A3s1"], 16 * 32)

    esA3.close()
    clear_block()
    if max_phase <= 2:
        if debug_dumps:
            dump_block([(0, bass.AP(wiT_d, 0, [[F, 256], [1, F]]), 256 * F),
                        (1024 * 1024, bass.AP(woT_d, 0, [[D, 1024], [1, D]]), 1024 * D)])
        es_persist.close()
        es_sem.close()
        return nc


    es_bc = ExitStack()
    QsT = es_bc.enter_context(nc.sbuf_tensor("QsT", [128, ND * L], BF16))
    KT = es_bc.enter_context(nc.sbuf_tensor("KT", [128, ND * L], BF16))
    V_sb = es_bc.enter_context(nc.sbuf_tensor("V_sb", [128, NT * D], BF16))
    coT = es_bc.enter_context(nc.sbuf_tensor("coT", [128, ND * L], BF16))

    esB = ExitStack()
    xnT = esB.enter_context(nc.sbuf_tensor("xnT", [128, ND * L], BF16))

    esB1 = ExitStack()
    x_t = [esB1.enter_context(nc.sbuf_tensor(f"x_t{i}", [128, D], F32)) for i in range(2)]
    xn_nat = [esB1.enter_context(nc.sbuf_tensor(f"xn_nat{i}", [128, D], BF16)) for i in range(2)]
    sq_sc = [esB1.enter_context(nc.sbuf_tensor(f"sq_sc{i}", [128, D], F32)) for i in range(2)]
    stg = esB1.enter_context(nc.sbuf_tensor("stg", [128, 32], F32))
    Lg = esB1.enter_context(nc.sbuf_tensor("Lg", [128, NT * E], F32))
    g8b = esB1.enter_context(nc.sbuf_tensor("g8b", [128, 4 * NT * E], F32))
    gsc = esB1.enter_context(nc.sbuf_tensor("gsc", [128, 6 * NT], F32))
    pB_xn = esB1.enter_context(nc.psum_tensor("pB_xn", [128, ND * 128], BF16))
    pB_g = esB1.enter_context(nc.psum_tensor("pB_g", [128, E], F32))
    pB_gs = esB1.enter_context(nc.psum_tensor("pB_gs", [1, 128], F32))

    def b3(tens, off, n2, n3):
        return bass.AP(tens, off, [[tens.shape[1], 128], [n3, n2], [1, n3]])

    with nc.Block() as blk:
        @blk.sync
        def _(sy):
            for p in range(2):
                for tt in range(NT):
                    k = p * NT + tt
                    if 2 <= k and k - 2 < NT:
                        sy.wait_ge(sm["sD"], k - 1)
                    elif k - 2 >= NT:
                        sy.wait_ge(sm["sH"], k - NT - 1)
                    sy.dma_start(x_t[k % 2][:, :],
                                 x_d[tt * 128:(tt + 1) * 128, :]
                                 ).then_inc(dm["Bx0" if k % 2 == 0 else "Bx1"], 16)

        @blk.vector
        def _(v):
            vc = [0]
            def vfence(ins):
                ins.then_inc(sm["vch"], 1)
                vc[0] += 1
                v.wait_ge(sm["vch"], vc[0])
            for tt in range(NT):
                v.wait_ge(dm["Bx0" if tt % 2 == 0 else "Bx1"], 16 * (tt // 2 + 1))
                xt = x_t[tt % 2]
                v.reduce_sum(stg[:, tt:tt + 1], xt[:, :], axis=AX.X)
                vfence(v.tensor_mul(sq_sc[tt % 2][:, :], xt[:, :], xt[:, :]))
                ins = v.reduce_sum(stg[:, NT + tt:NT + tt + 1], sq_sc[tt % 2][:, :], axis=AX.X)
                ins.then_inc(sm["sD"], 1)
            v.tensor_scalar(gsc[:, 0:NT], stg[:, 0:NT], 1.0 / D, None, ALU.mult)
            vfence(v.tensor_scalar(gsc[:, NT:2 * NT], stg[:, NT:2 * NT], 1.0 / D, None, ALU.mult))
            vfence(v.tensor_mul(gsc[:, 2 * NT:3 * NT], gsc[:, 0:NT], gsc[:, 0:NT]))
            vfence(v.tensor_sub(gsc[:, NT:2 * NT], gsc[:, NT:2 * NT], gsc[:, 2 * NT:3 * NT]))
            vfence(v.tensor_scalar(gsc[:, NT:2 * NT], gsc[:, NT:2 * NT], 1e-5, None, ALU.add))
            ins = v.reciprocal(gsc[:, 2 * NT:3 * NT], gsc[:, NT:2 * NT])
            ins.then_inc(sm["sE"], 1)
            v.wait_ge(sm["sF"], 1)
            v.tensor_mul(stats[:, NT:2 * NT], gsc[:, 0:NT],
                         stats[:, 0:NT]).then_inc(sm["vch"], 1)
            vc[0] += 1
            v.wait_ge(sm["vch"], vc[0])
            v.tensor_scalar(stats[:, NT:2 * NT], stats[:, NT:2 * NT], -1.0,
                            None, ALU.mult).then_inc(sm["sG"], 1)
            for tt in range(NT):
                v.wait_ge(sm["sI"], tt + 1)
                dst = bass.AP(xnT, tt * 128, [[ND * L, 128], [L, ND], [1, 128]])
                srcp = bass.AP(pB_xn, 0, [[ND * 128, 128], [128, ND], [1, 128]])
                v.tensor_copy(dst, srcp).then_inc(sm["sJ"], 1)
                v.wait_ge(sm["sK"], tt + 1)
                v.tensor_copy(Lg[:, tt * E:(tt + 1) * E],
                              pB_g[:, :]).then_inc(sm["sL"], 1)
            E8, SC, MK, CT = 0, NT * E, 2 * NT * E, 3 * NT * E
            m1c, m2c, vsc, s8c = 3 * NT, 4 * NT, 5 * NT, NT
            vfence(v.reduce_max(gsc[:, m1c:m1c + NT], b3(Lg, 0, NT, E), axis=AX.X))
            mb = bass.AP(gsc, m1c, [[6 * NT, 128], [1, NT], [0, E]])
            ins = v.tensor_sub(b3(g8b, E8, NT, E), b3(Lg, 0, NT, E), mb)
            ins.then_inc(sm["sU"], 1)
            v.wait_ge(sm["sM"], 1)
            vfence(v.reduce_sum(gsc[:, s8c:s8c + NT], b3(g8b, E8, NT, E), axis=AX.X))
            vfence(v.reciprocal(gsc[:, 2 * NT:3 * NT], gsc[:, s8c:s8c + NT]))
            rb = bass.AP(gsc, 2 * NT, [[6 * NT, 128], [1, NT], [0, E]])
            vfence(v.tensor_mul(b3(g8b, SC, NT, E), b3(g8b, E8, NT, E), rb))
            vfence(v.reduce_max(gsc[:, m1c:m1c + NT], b3(g8b, SC, NT, E), axis=AX.X))
            m1b = bass.AP(gsc, m1c, [[6 * NT, 128], [1, NT], [0, E]])
            vfence(v.tensor_tensor(b3(g8b, MK, NT, E), b3(g8b, SC, NT, E), m1b, ALU.is_equal))
            vfence(v.scalar_tensor_tensor(b3(g8b, MK, NT, E), b3(g8b, MK, NT, E), -1e30,
                                          b3(g8b, SC, NT, E), ALU.mult, ALU.add))
            vfence(v.reduce_max(gsc[:, m2c:m2c + NT], b3(g8b, MK, NT, E), axis=AX.X))
            vfence(v.tensor_add(gsc[:, vsc:vsc + NT], gsc[:, m1c:m1c + NT],
                                gsc[:, m2c:m2c + NT]))
            m2b = bass.AP(gsc, m2c, [[6 * NT, 128], [1, NT], [0, E]])
            vfence(v.tensor_tensor(b3(g8b, CT, NT, E), b3(g8b, SC, NT, E), m2b, ALU.is_ge))
            vsb = bass.AP(gsc, vsc, [[6 * NT, 128], [1, NT], [0, E]])
            vfence(v.tensor_tensor(b3(G_sb, 0, NT, E), b3(g8b, CT, NT, E), vsb, ALU.mult))
            v.tensor_scalar(gsc[:, vsc:vsc + NT], gsc[:, vsc:vsc + NT], 2.0,
                            None, ALU.mult).then_inc(sm["sN"], 1)
            for tt in range(NT):
                v.wait_ge(sm["sO"], tt + 1)
                v.tensor_copy(gsumrow[0:1, tt * 128:(tt + 1) * 128],
                              pB_gs[0:1, :]).then_inc(sm["sP"], 1)

        @blk.scalar
        def _(s):
            s.wait_ge(sm["sE"], 1)
            s.activation(stats[:, 0:NT], gsc[:, 2 * NT:3 * NT],
                         AF.Sqrt).then_inc(sm["sF"], 1)
            for tt in range(NT):
                k = NT + tt
                s.wait_ge(dm["Bx0" if k % 2 == 0 else "Bx1"], 16 * (k // 2 + 1))
                if tt == 0:
                    s.wait_ge(sm["sG"], 1)
                if tt >= 2:
                    s.wait_ge(sm["sI"], tt - 1)
                nc.scalar.activation(xn_nat[k % 2][:, :], x_t[k % 2][:, :], AF.Identity,
                                     bias=stats[:, NT + tt:NT + tt + 1],
                                     scale=stats[:, tt:tt + 1]).then_inc(sm["sH"], 1)
            s.wait_ge(sm["sL"], NT)
            s.wait_ge(sm["sU"], 1)
            nc.scalar.activation(g8b[:, 0:NT * E], g8b[:, 0:NT * E],
                                 AF.Exp).then_inc(sm["sM"], 1)

        @blk.tensor
        def _(t):
            for tt in range(NT):
                k = NT + tt
                t.wait_ge(sm["sH"], tt + 1)
                if tt >= 1:
                    t.wait_ge(sm["sJ"], tt)
                for ct in range(ND):
                    ins = nc.tensor.transpose(pB_xn[:, ct * 128:(ct + 1) * 128],
                                              xn_nat[k % 2][:, ct * 128:(ct + 1) * 128],
                                              ident_bf[:, :])
                    if ct == ND - 1:
                        ins.then_inc(sm["sI"], 1)
                if tt >= 1:
                    t.wait_ge(sm["sL"], tt)
                t.wait_ge(sm["sJ"], tt + 1)
                for ct in range(ND):
                    t.matmul(pB_g[:, :], xnT[:, ct * L + tt * 128: ct * L + (tt + 1) * 128],
                             gwT[:, ct * E:(ct + 1) * E], start=(ct == 0), stop=False)
                t.matmul(pB_g[:, :], ones_row[:, :], gb_bf[:, :],
                         start=False, stop=True).then_inc(sm["sK"], 1)
            t.wait_ge(sm["sN"], 1)
            for tt in range(NT):
                if tt >= 1:
                    t.wait_ge(sm["sP"], tt)
                nc.tensor.transpose(pB_gs[:, :], gsc[:, 5 * NT + tt:5 * NT + tt + 1],
                                    ident_f[:, :]).then_inc(sm["sO"], 1)

    esB1.close()
    clear_block()

    esB2v = ExitStack()
    wv_sb = esB2v.enter_context(nc.sbuf_tensor("wv_sb", [128, ND * D], BF16))
    pB_v = esB2v.enter_context(nc.psum_tensor("pB_v", [128, D], F32))

    with nc.Block() as blk:
        @blk.gpsimd
        def _(gp):
            srcap = bass.AP(wvs_d, 0, [[D, 128], [128 * D, ND], [1, D]])
            dstap = bass.AP(wv_sb, 0, [[ND * D, 128], [D, ND], [1, D]])
            gp.dma_start(dstap, srcap).then_inc(dm["Bwv"], 16)

        @blk.tensor
        def _(t):
            t.wait_ge(dm["Bwv"], 16)
            for tt in range(NT):
                for ch in range(2):
                    for ct in range(ND):
                        ins = t.matmul(pB_v[:, ch * 512:(ch + 1) * 512],
                                       xnT[:, ct * L + tt * 128: ct * L + (tt + 1) * 128],
                                       wv_sb[:, ct * D + ch * 512: ct * D + ch * 512 + 512],
                                       start=(ct == 0), stop=(ct == ND - 1))
                        if ch == 1 and ct == ND - 1:
                            ins.then_inc(sm["sP"], 1)
                t.wait_ge(sm["sQ"], tt + 1)

        @blk.vector
        def _(v):
            for tt in range(NT):
                v.wait_ge(sm["sP"], tt + 1)
                v.tensor_copy(V_sb[:, tt * D:(tt + 1) * D], pB_v[:, :]).then_inc(sm["sQ"], 1)

    esB2v.close()
    clear_block()
    if max_phase <= 5:
        esB.close()
        es_bc.close()
        es_persist.close()
        es_sem.close()
        return nc


    esB2 = ExitStack()
    wq_sb = esB2.enter_context(nc.sbuf_tensor("wq_sb", [128, ND * D], BF16))
    wk_sb = esB2.enter_context(nc.sbuf_tensor("wk_sb", [128, ND * D], BF16))
    pB_qk = [esB2.enter_context(nc.psum_tensor(f"pB_qk{i}", [128, L], F32))
             for i in range(2)]

    with nc.Block() as blk:
        @blk.gpsimd
        def _(gp):
            for (dst, srcd) in ((wq_sb, wqs_d), (wk_sb, wks_d)):
                srcap = bass.AP(srcd, 0, [[D, 128], [128 * D, ND], [1, D]])
                dstap = bass.AP(dst, 0, [[ND * D, 128], [D, ND], [1, D]])
                gp.dma_start(dstap, srcap).then_inc(dm["Bwqk"], 16)

        @blk.tensor
        def _(t):
            t.wait_ge(dm["Bwqk"], 16 * 2)
            cnt = 0
            for wsb in (wq_sb, wk_sb):
                for dt in range(ND):
                    if cnt >= 2:
                        if cnt - 2 < ND:
                            t.wait_ge(sm["sN"], cnt - 1)
                        else:
                            t.wait_ge(sm["sU"], cnt - ND - 1)
                    for ch in range(4):
                        for ct in range(ND):
                            ins = t.matmul(pB_qk[cnt % 2][:, ch * 512:(ch + 1) * 512],
                                           wsb[:, ct * D + dt * 128: ct * D + (dt + 1) * 128],
                                           xnT[:, ct * L + ch * 512: ct * L + (ch + 1) * 512],
                                           start=(ct == 0), stop=(ct == ND - 1))
                            if ch == 3 and ct == ND - 1:
                                ins.then_inc(sm["sO"], 1)
                    cnt += 1

        @blk.scalar
        def _(s):
            for dt in range(ND):
                s.wait_ge(sm["sO"], dt + 1)
                nc.scalar.activation(QsT[:, dt * L:(dt + 1) * L], pB_qk[dt % 2][:, :],
                                     AF.Identity,
                                     bias=0.0, scale=cqs[:, :]).then_inc(sm["sN"], 1)

        @blk.vector
        def _(v):
            for dt in range(ND):
                v.wait_ge(sm["sO"], ND + dt + 1)
                v.tensor_copy(KT[:, dt * L:(dt + 1) * L],
                              pB_qk[dt % 2][:, :]).then_inc(sm["sU"], 1)

    esB2.close()
    clear_block()
    if max_phase <= 6:
        esB.close()
        es_bc.close()
        es_persist.close()
        es_sem.close()
        return nc


    esB3 = ExitStack()
    pB_i = esB3.enter_context(nc.psum_tensor("pB_i", [128, L], F32))
    int_st = esB3.enter_context(nc.sbuf_tensor("int_st", [128, L], BF16))

    with nc.Block() as blk:
        @blk.tensor
        def _(t):
            for p in range(2):
                for ch in range(4):
                    for ct in range(ND):
                        ins = t.matmul(pB_i[:, ch * 512:(ch + 1) * 512],
                                       Apm[:, ct * 256 + p * 128: ct * 256 + (p + 1) * 128],
                                       xnT[:, ct * L + ch * 512: ct * L + (ch + 1) * 512],
                                       start=(ct == 0), stop=(ct == ND - 1))
                        if ch == 3 and ct == ND - 1:
                            ins.then_inc(sm["sR"], 1)
                t.wait_ge(sm["sS"], p + 1)

        @blk.vector
        def _(v):
            for p in range(2):
                v.wait_ge(sm["sR"], p + 1)
                if p >= 1:
                    v.wait_ge(dm["Bint"], 16 * p)
                v.tensor_copy(int_st[:, :], pB_i[:, :]).then_inc(sm["sS"], 1)

        @blk.gpsimd
        def _(gp):
            for p in range(2):
                gp.wait_ge(sm["sS"], p + 1)
                gp.dma_start(intpm_d[p * 128:(p + 1) * 128, :],
                             int_st[:, :]).then_inc(dm["Bint"], 16)
            gp.wait_ge(dm["Bint"], 16 * 2)

    esB3.close()
    esB.close()
    clear_block()
    if max_phase <= 7:
        if debug_dumps:
            dump_block([(0, intpm_d[:, :], 256 * L),
                        (512 * 1024, QsT[:, 0:2048], 128 * 2048),
                        (768 * 1024, KT[:, 0:2048], 128 * 2048),
                        (1024 * 1024, V_sb[:, 0:1024], 128 * 1024),
                        (1152 * 1024, bass.AP(G_sb, 0, [[NT * E, 128], [1, 128]]), 128 * 128),
                        (1168 * 1024, gsumrow[0:1, :], 2048)])
        es_bc.close()
        es_persist.close()
        es_sem.close()
        return nc


    esC = ExitStack()
    rkt_sb = [esC.enter_context(nc.sbuf_tensor(f"rkt_sb{i}", [128, 4096], BF16)) for i in range(2)]
    qeT = esC.enter_context(nc.sbuf_tensor("qeT", [128, L], BF16))
    qpT = esC.enter_context(nc.sbuf_tensor("qpT", [128, L], BF16))
    band_sb = [esC.enter_context(nc.sbuf_tensor(f"band_sb{i}", [128, W], BF16)) for i in range(2)]
    P_sb = [esC.enter_context(nc.sbuf_tensor(f"P_sb{i}", [128, L], BF16)) for i in range(2)]
    A_sb = [esC.enter_context(nc.sbuf_tensor(f"A_sb{i}", [128, L], BF16)) for i in range(2)]
    AT4 = esC.enter_context(nc.sbuf_tensor("AT4", [128, 16 * 512], BF16))
    sums_sb = esC.enter_context(nc.sbuf_tensor("sums_sb", [128, 4], F32))

    pC_S = esC.enter_context(nc.psum_tensor("pC_S", [128, L], F32))
    pC_b = [esC.enter_context(nc.psum_tensor(f"pC_b{i}", [128, 512], F32)) for i in range(2)]
    pC_at = esC.enter_context(nc.psum_tensor("pC_at", [128, 1024], BF16))
    pC_av = esC.enter_context(nc.psum_tensor("pC_av", [128, 512], F32))

    CH_W = [512, 512, 512, 512, 128]
    CH_OFF = [0, 512, 1024, 1536, 2048]
    NIT = H * NT
    if cnit is not None:
        NIT = cnit
    VCH = (0, 1)
    SCH = (2, 3, 4)

    def bevac_req(n):
        b, c = divmod(n, 5)
        if c in VCH:
            return ("sB", 2 * b + c + 1)
        return ("sC", 3 * b + c - 1)

    with nc.Block() as blk:
        @blk.tensor
        def _(t):
            nchunk = [0]

            def emit_band():
                n = nchunk[0]
                if n >= 5 * NIT:
                    return
                bi, c = divmod(n, 5)
                hb, jb = divmod(bi, NT)
                if c == 0 and jb == 0:
                    t.wait_ge(sm["sA"], hb + 1)
                    t.wait_ge(dm["Crk0" if hb % 2 == 0 else "Crk1"],
                              16 * (hb // 2 + 1))
                if n >= 2:
                    sname, cntv = bevac_req(n - 2)
                    t.wait_ge(sm[sname], cntv)
                j0 = 1920 - 128 * jb
                t.matmul(pC_b[n % 2][:, 0:CH_W[c]], qpT[:, jb * 128:(jb + 1) * 128],
                         rkt_sb[hb % 2][:, j0 + CH_OFF[c]: j0 + CH_OFF[c] + CH_W[c]],
                         start=True, stop=True).then_inc(sm["sD"], 1)
                nchunk[0] += 1

            def trans_batch(jj, half):
                if half == 0:
                    t.wait_ge(sm["sJ"], jj + 1)
                t.wait_ge(sm["sK"], 2 * jj + half)
                for kt in range(8):
                    ktg = half * 8 + kt
                    ins = nc.tensor.transpose(pC_at[:, kt * 128:(kt + 1) * 128],
                                              A_sb[jj % 2][:, ktg * 128:(ktg + 1) * 128],
                                              ident_bf[:, :])
                    if kt == 7:
                        ins.then_inc(sm["sL"], 1)

            def av_group(j):
                g = j // 4
                hh = j // NT
                t.wait_ge(sm["sK"], 2 * (j + 1))
                if g >= 1:
                    t.wait_ge(sm["sM"], g)
                for kt in range(16):
                    ins = t.matmul(pC_av[:, :],
                                   V_sb[:, kt * D + hh * 128: kt * D + (hh + 1) * 128],
                                   AT4[:, kt * 512:(kt + 1) * 512],
                                   start=(kt == 0), stop=(kt == 15))
                    if kt == 15:
                        ins.then_inc(sm["sP"], 1)

            for _n in range(15):
                emit_band()
            for i in range(NIT):
                h, j = divmod(i, NT)
                t.wait_ge(dm["Cpi0" if i % 2 == 0 else "Cpi1"],
                          16 * (i // 2 + 1))
                if j == 0:
                    t.wait_ge(sm["sF"], h + 1)
                for ch in range(4):
                    if i >= 1 and ch == 0:
                        t.wait_ge(sm["sH"], 2 * i - 1)
                    if i >= 1 and ch == 2:
                        t.wait_ge(sm["sH"], 2 * i)
                    t.matmul(pC_S[:, ch * 512:(ch + 1) * 512],
                             qeT[:, j * 128:(j + 1) * 128],
                             KT[:, h * L + ch * 512: h * L + (ch + 1) * 512],
                             start=True, stop=False)
                for ch in range(4):
                    ins = t.matmul(pC_S[:, ch * 512:(ch + 1) * 512], ident_bf[:, :],
                                   P_sb[i % 2][:, ch * 512:(ch + 1) * 512],
                                   start=False, stop=True)
                    if ch == 1:
                        ins.then_inc(sm["sT"], 1)
                    if ch == 3:
                        ins.then_inc(sm["sI"], 1)
                if i >= 1:
                    trans_batch(i - 1, 0)
                emit_band()
                emit_band()
                if i >= 1:
                    trans_batch(i - 1, 1)
                emit_band()
                if i >= 1 and (i - 1) % 4 == 3:
                    av_group(i - 1)
                emit_band()
                emit_band()
            trans_batch(NIT - 1, 0)
            trans_batch(NIT - 1, 1)
            av_group(NIT - 1)

        @blk.scalar
        def _(s):
            s.dma_start(rkt_sb[0][:, :], rkt_d[0:128, :]).then_inc(dm["Crk0"], 16)
            s.dma_start(rkt_sb[1][:, :], rkt_d[128:256, :]).then_inc(dm["Crk1"], 16)
            s.activation(qpT[:, :], QsT[:, 0:L], AF.Identity,
                         bias=rpbT[:, 0:1], scale=1.0).then_inc(sm["sA"], 1)
            s.activation(qeT[:, :], QsT[:, 0:L], AF.Identity,
                         bias=rebT[:, 0:1], scale=1.0).then_inc(sm["sF"], 1)
            for bi0 in (0, 1, 2):
                for c in SCH:
                    n = bi0 * 5 + c
                    s.wait_ge(sm["sD"], n + 1)
                    s.activation(band_sb[bi0 % 2][:, CH_OFF[c]:CH_OFF[c] + CH_W[c]],
                                 pC_b[n % 2][:, 0:CH_W[c]], AF.Copy).then_inc(sm["sC"], 1)
            for i in range(NIT):
                h, j = divmod(i, NT)
                s.wait_ge(sm["sT"], i + 1)
                if i >= 2:
                    s.wait_ge(sm["sJ"], i - 1)
                    s.wait_ge(sm["sL"], 2 * (i - 1))
                for hf in range(2):
                    if hf == 1:
                        s.wait_ge(sm["sI"], i + 1)
                    nc.scalar.activation(A_sb[i % 2][:, hf * 1024:(hf + 1) * 1024],
                                         pC_S[:, hf * 1024:(hf + 1) * 1024], AF.Exp,
                                         bias=cm20[:, :], scale=1.0,
                                         accum_out=sums_sb[:, 2 * (i % 2) + hf:
                                                           2 * (i % 2) + hf + 1]
                                         ).then_inc(sm["sH"], 1)
                if j == NT - 1 and h < H - 1:
                    s.activation(qeT[:, :], QsT[:, (h + 1) * L:(h + 2) * L], AF.Identity,
                                 bias=rebT[:, h + 1:h + 2], scale=1.0).then_inc(sm["sF"], 1)
                bi = i + 3
                if bi < NIT:
                    for c in SCH:
                        n = bi * 5 + c
                        s.wait_ge(sm["sD"], n + 1)
                        if bi >= 2 and c == SCH[0]:
                            s.wait_ge(dm["Cbo0" if bi % 2 == 0 else "Cbo1"],
                                      16 * (bi // 2))
                        s.activation(band_sb[bi % 2][:, CH_OFF[c]:CH_OFF[c] + CH_W[c]],
                                     pC_b[n % 2][:, 0:CH_W[c]],
                                     AF.Copy).then_inc(sm["sC"], 1)
                if j == NT - 4 and h < H - 1:
                    s.wait_ge(sm["sD"], 5 * NT * (h + 1))
                    s.activation(qpT[:, :], QsT[:, (h + 1) * L:(h + 2) * L], AF.Identity,
                                 bias=rpbT[:, h + 1:h + 2], scale=1.0).then_inc(sm["sA"], 1)
                    if h + 2 < H:
                        s.dma_start(rkt_sb[(h + 2) % 2][:, :],
                                    rkt_d[(h + 2) * 128:(h + 3) * 128, :]
                                    ).then_inc(dm["Crk0" if h % 2 == 0 else "Crk1"], 16)

        @blk.vector
        def _(v):
            for bi0 in (0, 1, 2):
                for c in VCH:
                    n = bi0 * 5 + c
                    v.wait_ge(sm["sD"], n + 1)
                    v.tensor_copy(band_sb[bi0 % 2][:, CH_OFF[c]:CH_OFF[c] + CH_W[c]],
                                  pC_b[n % 2][:, 0:CH_W[c]]).then_inc(sm["sB"], 1)

            def at_evac(jj, half):
                v.wait_ge(sm["sL"], 2 * jj + half + 1)
                if half == 0 and jj >= 4:
                    v.wait_ge(sm["sP"], jj // 4)
                dst = bass.AP(AT4, half * 8 * 512 + (jj % 4) * 128,
                              [[16 * 512, 128], [512, 8], [1, 128]])
                srcp = bass.AP(pC_at, 0, [[1024, 128], [128, 8], [1, 128]])
                v.tensor_copy(dst, srcp).then_inc(sm["sK"], 1)

            for i in range(NIT):
                jj = i - 1
                bi = i + 3

                def bevac(c):
                    n = bi * 5 + c
                    v.wait_ge(sm["sD"], n + 1)
                    if bi >= 2 and c == 0:
                        v.wait_ge(dm["Cbo0" if bi % 2 == 0 else "Cbo1"],
                                  16 * (bi // 2))
                    v.tensor_copy(band_sb[bi % 2][:, CH_OFF[c]:CH_OFF[c] + CH_W[c]],
                                  pC_b[n % 2][:, 0:CH_W[c]]).then_inc(sm["sB"], 1)

                if jj >= 0:
                    at_evac(jj, 0)
                if bi < NIT:
                    bevac(0)
                if jj >= 0:
                    at_evac(jj, 1)
                if bi < NIT:
                    bevac(1)
                jj2 = i - 2
                if jj2 >= 3 and jj2 % 4 == 3:
                    g = jj2 // 4
                    v.wait_ge(sm["sP"], g + 1)
                    v.tensor_copy(coT[:, (g // 4) * L + (g % 4) * 512:
                                      (g // 4) * L + (g % 4) * 512 + 512],
                                  pC_av[:, :]).then_inc(sm["sM"], 1)
                v.wait_ge(sm["sH"], 2 * (i + 1))
                ins = v.tensor_add(tiny[:, 14:15], sums_sb[:, 2 * (i % 2):2 * (i % 2) + 1],
                                   sums_sb[:, 2 * (i % 2) + 1:2 * (i % 2) + 2])
                ins.then_inc(sm["vch"], 1)
                v.wait_ge(sm["vch"], 2 * i + 1)
                ins = v.reciprocal(tiny[:, 12 + i % 2:13 + i % 2], tiny[:, 14:15])
                ins.then_inc(sm["vch"], 1)
                v.wait_ge(sm["vch"], 2 * i + 2)
                v.tensor_scalar(A_sb[i % 2][:, :], A_sb[i % 2][:, :],
                                tiny[:, 12 + i % 2:13 + i % 2], None,
                                ALU.mult).then_inc(sm["sJ"], 1)
            jj = NIT - 1
            at_evac(jj, 0)
            at_evac(jj, 1)
            g = NIT // 4 - 1
            v.wait_ge(sm["sP"], g + 1)
            v.tensor_copy(coT[:, (g // 4) * L + (g % 4) * 512:
                              (g // 4) * L + (g % 4) * 512 + 512],
                          pC_av[:, :]).then_inc(sm["sM"], 1)

        @blk.gpsimd
        def _(gp):
            for bi in (0, 1, 2):
                gp.wait_ge(sm["sB"], 2 * (bi + 1))
                gp.wait_ge(sm["sC"], 3 * (bi + 1))
                if bi >= 2:
                    gp.wait_ge(dm["Cpi0" if bi % 2 == 0 else "Cpi1"],
                               16 * (bi // 2))
                gp.dma_start(bounce_d[bi % 2][:, :],
                             band_sb[bi % 2][:, :]
                             ).then_inc(dm["Cbo0" if bi % 2 == 0 else "Cbo1"], 16)
            for i in range(NIT):
                bi = i + 3
                if bi < NIT:
                    gp.wait_ge(sm["sB"], 2 * (bi + 1))
                    gp.wait_ge(sm["sC"], 3 * (bi + 1))
                    gp.wait_ge(dm["Cpi0" if bi % 2 == 0 else "Cpi1"],
                               16 * (bi // 2))
                    gp.dma_start(bounce_d[bi % 2][:, :],
                                 band_sb[bi % 2][:, :]
                                 ).then_inc(dm["Cbo0" if bi % 2 == 0 else "Cbo1"], 16)

        @blk.sync
        def _(sy):
            for bi in range(NIT):
                sy.wait_ge(dm["Cbo0" if bi % 2 == 0 else "Cbo1"], 16 * (bi // 2 + 1))
                if bi >= 2:
                    sy.wait_ge(sm["sI"], bi - 1)
                sy.dma_start(P_sb[bi % 2][:, :],
                             bass.AP(bounce_d[bi % 2], 127, [[W - 1, 128], [1, L]])
                             ).then_inc(dm["Cpi0" if bi % 2 == 0 else "Cpi1"], 16)

    esC.close()
    if max_phase <= 8:
        es_bc.close()
        es_persist.close()
        es_sem.close()
        return nc


    esS = ExitStack()
    woutn = esS.enter_context(nc.sbuf_tensor("woutn", [128, ND * D], BF16))
    s_st = [esS.enter_context(nc.sbuf_tensor(f"s_st{i}", [128, D], BF16)) for i in range(2)]
    pS = esS.enter_context(nc.psum_tensor("pS", [128, D], F32))

    with nc.Block() as blk:
        @blk.gpsimd
        def _(gp):
            src = bass.AP(wout_d, 0, [[D, 128], [128 * D, ND], [1, D]])
            gp.dma_start(bass.AP(woutn, 0, [[ND * D, 128], [D, ND], [1, D]]),
                         src).then_inc(dm["Swo"], 16)
            for tt in range(NT):
                gp.wait_ge(sm["sW"], tt + 1)
                gp.dma_start(s_dram[tt * 128:(tt + 1) * 128, :],
                             s_st[tt % 2][:, :]).then_inc(dm["Sst0" if tt % 2 == 0 else "Sst1"], 16)
            gp.wait_ge(dm["Sst0"], 16 * (NT // 2))
            gp.wait_ge(dm["Sst1"], 16 * (NT // 2))

        @blk.tensor
        def _(t):
            t.wait_ge(dm["Swo"], 16)
            for tt in range(NT):
                if tt >= 1:
                    t.wait_ge(sm["sW"], tt)
                for ch in range(2):
                    for hv in range(ND):
                        t.matmul(pS[:, ch * 512:(ch + 1) * 512],
                                 coT[:, hv * L + tt * 128: hv * L + (tt + 1) * 128],
                                 woutn[:, hv * D + ch * 512: hv * D + ch * 512 + 512],
                                 start=(hv == 0), stop=False)
                    ins = t.matmul(pS[:, ch * 512:(ch + 1) * 512], ones_row[:, :],
                                   bout_bf[0:1, ch * 512:(ch + 1) * 512],
                                   start=False, stop=True)
                    if ch == 1:
                        ins.then_inc(sm["sR"], 1)

        @blk.scalar
        def _(s):
            for tt in range(NT):
                s.wait_ge(sm["sR"], tt + 1)
                if tt >= 2:
                    s.wait_ge(dm["Sst0" if tt % 2 == 0 else "Sst1"], 16 * (tt // 2))
                s.activation(s_st[tt % 2][:, :], pS[:, :], AF.Copy).then_inc(sm["sW"], 1)

    esS.close()
    es_bc.close()
    clear_block()
    if max_phase <= 9:
        if debug_dumps:
            dump_block([(0, s_dram[:, :], L * D)])
        es_persist.close()
        es_sem.close()
        return nc


    esD = ExitStack()
    wiT = esD.enter_context(nc.sbuf_tensor("wiT", [128, ND * F], BF16))
    woT = esD.enter_context(nc.sbuf_tensor("woT", [128, NF * D], BF16))
    xD_t = [esD.enter_context(nc.sbuf_tensor(f"xD_t{i}", [128, D], F32)) for i in range(2)]
    xnD = [esD.enter_context(nc.sbuf_tensor(f"xnD{i}", [128, D], BF16)) for i in range(2)]
    xnT_t = [esD.enter_context(nc.sbuf_tensor(f"xnT_t{i}", [128, D], BF16)) for i in range(2)]
    sD_t = [esD.enter_context(nc.sbuf_tensor(f"sD_t{i}", [128, D], BF16)) for i in range(2)]
    ipm_t = [esD.enter_context(nc.sbuf_tensor(f"ipm_t{i}", [32, E * 128], BF16)) for i in range(2)]
    btp_sb = [esD.enter_context(nc.sbuf_tensor(f"btp_sb{i}", [32, 1024], BF16)) for i in range(4)]
    rA = [esD.enter_context(nc.sbuf_tensor(f"rA{i}", [128, 512], BF16)) for i in range(2)]
    rB = [esD.enter_context(nc.sbuf_tensor(f"rB{i}", [128, 512], BF16)) for i in range(2)]
    mixn = esD.enter_context(nc.sbuf_tensor("mixn", [128, 1024], BF16))
    mixT_sb = esD.enter_context(nc.sbuf_tensor("mixT_sb", [128, 1024], BF16))
    gD = esD.enter_context(nc.sbuf_tensor("gD", [128, D], F32))
    tmpD = esD.enter_context(nc.sbuf_tensor("tmpD", [128, D], F32))
    outD = [esD.enter_context(nc.sbuf_tensor(f"outD{i}", [128, D], F32)) for i in range(2)]

    bi_bf = esD.enter_context(nc.sbuf_tensor("bi_bf", [1, F], BF16))
    bo_bf = esD.enter_context(nc.sbuf_tensor("bo_bf", [1, D], BF16))
    pD_z = esD.enter_context(nc.psum_tensor("pD_z", [128, 1024], F32))
    pD_mx = esD.enter_context(nc.psum_tensor("pD_mx", [128, 1024], F32))
    pD_mt = esD.enter_context(nc.psum_tensor("pD_mt", [128, 1024], BF16))
    pD_f = esD.enter_context(nc.psum_tensor("pD_f", [128, D], F32))
    pD_xn = esD.enter_context(nc.psum_tensor("pD_xn", [128, 1024], BF16))

    NQ = 4

    with nc.Block() as blk:
        @blk.gpsimd
        def _(gp):
            src = bass.AP(wiT_d, 0, [[F, 128], [128 * F, ND], [1, F]])
            gp.dma_start(bass.AP(wiT, 0, [[ND * F, 128], [F, ND], [1, F]]),
                         src).then_inc(dm["DwiT"], 16)
            gp.dma_start(bi_bf[:, :], bi_d[:, :]).then_inc(dm["DwiT"], 16)
            gp.dma_start(bo_bf[:, :], bo_d[:, :]).then_inc(dm["DwiT"], 16)
            nb = 0
            BTS = ["Dbt0", "Dbt1", "Dbt2", "Dbt3"]
            for tt in range(NT):
                for qtr in range(NQ):
                    for e in range(E):
                        if nb >= 4:
                            gp.wait_ge(sm["sM"], 2 * (nb - 3))
                        gp.dma_start(btp_sb[nb % 4][:, :],
                                     bass.AP(btp_d, e * 32 * F + qtr * 1024,
                                             [[F, 32], [1, 1024]])
                                     ).then_inc(dm[BTS[nb % 4]], 16)
                        nb += 1


        @blk.sync
        def _(sy):
            src = bass.AP(woT_d, 0, [[D, 128], [128 * D, NF], [1, D]])
            sy.dma_start(bass.AP(woT, 0, [[NF * D, 128], [D, NF], [1, D]]),
                         src).then_inc(dm["DwoT"], 16)
            for tt in range(NT):
                if tt >= 2:
                    sy.wait_ge(sm["sD"], tt - 1)
                sy.dma_start(xD_t[tt % 2][:, :],
                             x_d[tt * 128:(tt + 1) * 128, :]
                             ).then_inc(dm["Dxs0" if tt % 2 == 0 else "Dxs1"], 16)
                sy.dma_start(sD_t[tt % 2][:, :],
                             s_dram[tt * 128:(tt + 1) * 128, :]
                             ).then_inc(dm["Dxs0" if tt % 2 == 0 else "Dxs1"], 16)
                if tt >= 2:
                    sy.wait_ge(sm["sP"], tt - 1)
                sy.dma_start(bass.AP(ipm_t[tt % 2], 0, [[E * 128, 32], [128, E], [1, 128]]),
                             bass.AP(intpm_d, tt * 128,
                                     [[L, 32], [32 * L, E], [1, 128]])
                             ).then_inc(dm["Dip0" if tt % 2 == 0 else "Dip1"], 16)
                if tt >= 1:
                    sy.wait_ge(sm["sD"], tt)
                    sy.dma_start(out_d[(tt - 1) * 128:tt * 128, :],
                                 outD[(tt - 1) % 2][:, :]
                                 ).then_inc(dm["Dout0" if (tt - 1) % 2 == 0 else "Dout1"], 16)
            sy.wait_ge(sm["sD"], NT)
            sy.dma_start(out_d[(NT - 1) * 128:NT * 128, :],
                         outD[(NT - 1) % 2][:, :]
                         ).then_inc(dm["Dout0" if (NT - 1) % 2 == 0 else "Dout1"], 16)
            sy.wait_ge(dm["Dout0"], 16 * (NT // 2))
            sy.wait_ge(dm["Dout1"], 16 * (NT // 2))

        @blk.scalar
        def _(s):
            for tt in range(NT):
                s.wait_ge(dm["Dxs0" if tt % 2 == 0 else "Dxs1"], 32 * (tt // 2 + 1))
                if tt >= 2:
                    s.wait_ge(sm["sK"], tt - 1)
                nc.scalar.activation(xnD[tt % 2][:, :], xD_t[tt % 2][:, :], AF.Identity,
                                     bias=stats[:, NT + tt:NT + tt + 1],
                                     scale=stats[:, tt:tt + 1]).then_inc(sm["sL"], 1)
                if tt >= 1:
                    if tt >= 2:
                        s.wait_ge(sm["sD"], tt - 1)
                    s.wait_ge(sm["sP"], tt)
                    nc.scalar.activation(gD[:, :], pD_f[:, :], AF.Silu).then_inc(sm["sQ"], 1)
                for qtr in range(NQ):
                    for e in range(E):
                        idx = (tt * NQ + qtr) * E + e
                        s.wait_ge(sm["sM"], 2 * idx + 1)
                        if idx >= 2:
                            s.wait_ge(sm["sN"], idx - 1)
                        nc.scalar.activation(rA[idx % 2][:, :], pD_z[:, 0:512], AF.Relu,
                                             bias=0.0,
                                             scale=G_sb[:, tt * E + e: tt * E + e + 1]
                                             ).then_inc(sm["sO"], 1)
                if tt == NT - 1:
                    s.wait_ge(sm["sD"], tt)
                    s.wait_ge(sm["sP"], tt + 1)
                    nc.scalar.activation(gD[:, :], pD_f[:, :], AF.Silu).then_inc(sm["sQ"], 1)

        @blk.vector
        def _(v):
            for tt in range(NT):
                v.wait_ge(sm["sK"], tt + 1)
                if tt >= 2:
                    v.wait_ge(sm["sP"], tt - 1)
                v.tensor_copy(xnT_t[tt % 2][:, :], pD_xn[:, :]).then_inc(sm["sR"], 1)
                if tt >= 1:
                    ttp = tt - 1
                    v.wait_ge(sm["sQ"], ttp + 1)
                    v.wait_ge(dm["Dxs0" if ttp % 2 == 0 else "Dxs1"], 32 * (ttp // 2 + 1))
                    if ttp >= 2:
                        v.wait_ge(dm["Dout0" if ttp % 2 == 0 else "Dout1"], 16 * (ttp // 2))
                    ins = v.tensor_mul(tmpD[:, :], sD_t[ttp % 2][:, :], gD[:, :])
                    ins.then_inc(sm["vch"], 1)
                    v.wait_ge(sm["vch"], ttp + 1)
                    v.tensor_add(outD[ttp % 2][:, :], tmpD[:, :],
                                 xD_t[ttp % 2][:, :]).then_inc(sm["sD"], 1)
                for qtr in range(NQ):
                    qi = tt * NQ + qtr
                    for e in range(E):
                        idx = qi * E + e
                        v.wait_ge(sm["sM"], 2 * idx + 2)
                        if idx >= 2:
                            v.wait_ge(sm["sX"], idx - 1)
                        v.tensor_scalar(rB[idx % 2][:, :], pD_z[:, 512:1024],
                                        G_sb[:, tt * E + e: tt * E + e + 1], 0.0,
                                        ALU.mult, ALU.max).then_inc(sm["sS"], 1)
                    v.wait_ge(sm["sX"], 8 * (qi + 1))
                    v.tensor_copy(mixn[:, :], pD_mx[:, :]).then_inc(sm["sU"], 1)
                    v.wait_ge(sm["sV"], qi + 1)
                    v.tensor_copy(mixT_sb[:, :], pD_mt[:, :]).then_inc(sm["sW"], 1)
                if tt == NT - 1:
                    v.wait_ge(sm["sQ"], tt + 1)
                    v.wait_ge(dm["Dxs0" if tt % 2 == 0 else "Dxs1"], 32 * (tt // 2 + 1))
                    v.wait_ge(dm["Dout0" if tt % 2 == 0 else "Dout1"], 16 * (tt // 2))
                    ins = v.tensor_mul(tmpD[:, :], sD_t[tt % 2][:, :], gD[:, :])
                    ins.then_inc(sm["vch"], 1)
                    v.wait_ge(sm["vch"], tt + 1)
                    v.tensor_add(outD[tt % 2][:, :], tmpD[:, :],
                                 xD_t[tt % 2][:, :]).then_inc(sm["sD"], 1)

        @blk.tensor
        def _(t):
            t.wait_ge(dm["DwiT"], 48)
            t.wait_ge(dm["DwoT"], 16)
            pend = []

            def emit_down(k):
                for _ in range(k):
                    if not pend:
                        return
                    pend.pop(0)()

            def push_down(tt, qtr):
                qi = tt * NQ + qtr

                def first_waits(tt=tt, qtr=qtr, qi=qi):
                    t.wait_ge(sm["sW"], qi + 1)
                    if qtr == 0:
                        t.wait_ge(sm["sQ"], tt)
                first = [True]
                for ft in range(8):
                    for ch in range(2):
                        def job(tt=tt, qtr=qtr, ft=ft, ch=ch, first=first):
                            if first[0]:
                                first_waits()
                                first[0] = False
                            ftg = qtr * 8 + ft
                            t.matmul(pD_f[:, ch * 512:(ch + 1) * 512],
                                     mixT_sb[:, ft * 128:(ft + 1) * 128],
                                     woT[:, ftg * D + ch * 512: ftg * D + ch * 512 + 512],
                                     start=(qtr == 0 and ft == 0), stop=False)
                        pend.append(job)
                if qtr == NQ - 1:
                    def bo_job(tt=tt):
                        for ch in range(2):
                            ins = t.matmul(pD_f[:, ch * 512:(ch + 1) * 512],
                                           gsumrow[0:1, tt * 128:(tt + 1) * 128],
                                           bo_bf[0:1, ch * 512:(ch + 1) * 512],
                                           start=False, stop=True)
                            if ch == 1:
                                ins.then_inc(sm["sP"], 1)
                    pend.append(bo_job)

            def mix_transposes(tt, qtr):
                qi = tt * NQ + qtr
                t.wait_ge(sm["sU"], qi + 1)
                if qi >= 1:
                    t.wait_ge(sm["sW"], qi)
                for ft in range(8):
                    ins = nc.tensor.transpose(pD_mt[:, ft * 128:(ft + 1) * 128],
                                              mixn[:, ft * 128:(ft + 1) * 128],
                                              ident_bf[:, :])
                    if ft == 7:
                        ins.then_inc(sm["sV"], 1)
                push_down(tt, qtr)

            prev_q = [None]
            for tt in range(NT):
                t.wait_ge(sm["sL"], tt + 1)
                if tt >= 2:
                    t.wait_ge(sm["sR"], tt - 1)
                for ct in range(ND):
                    ins = nc.tensor.transpose(pD_xn[:, ct * 128:(ct + 1) * 128],
                                              xnD[tt % 2][:, ct * 128:(ct + 1) * 128],
                                              ident_bf[:, :])
                    if ct == ND - 1:
                        ins.then_inc(sm["sK"], 1)
                if prev_q[0] is not None:
                    mix_transposes(*prev_q[0])
                    prev_q[0] = None
                    emit_down(len(pend))
                t.wait_ge(sm["sR"], tt + 1)
                t.wait_ge(dm["Dip0" if tt % 2 == 0 else "Dip1"], 16 * (tt // 2 + 1))
                for qtr in range(NQ):
                    qi = tt * NQ + qtr
                    for hf in range(2):
                        for dt in range(ND):
                            t.matmul(pD_z[:, hf * 512:(hf + 1) * 512],
                                     xnT_t[tt % 2][:, dt * 128:(dt + 1) * 128],
                                     wiT[:, dt * F + qtr * 1024 + hf * 512:
                                         dt * F + qtr * 1024 + hf * 512 + 512],
                                     start=(dt == 0), stop=False)
                        t.matmul(pD_z[:, hf * 512:(hf + 1) * 512], ones_row[:, :],
                                 bi_bf[0:1, qtr * 1024 + hf * 512: qtr * 1024 + hf * 512 + 512],
                                 start=False, stop=True)
                    if prev_q[0] is not None:
                        mix_transposes(*prev_q[0])
                        prev_q[0] = None
                    for e in range(E):
                        idx = qi * E + e
                        t.wait_ge(dm[["Dbt0", "Dbt1", "Dbt2", "Dbt3"][idx % 4]],
                                  16 * (idx // 4 + 1))
                        for hf in range(2):
                            if e >= 1:
                                t.wait_ge(sm["sO" if hf == 0 else "sS"], idx)
                            t.matmul(pD_z[:, hf * 512:(hf + 1) * 512],
                                     ipm_t[tt % 2][:, e * 128:(e + 1) * 128],
                                     btp_sb[idx % 4][:, hf * 512:(hf + 1) * 512],
                                     start=False, stop=False, skip_group_check=True
                                     ).then_inc(sm["sM"], 1)
                        if e >= 1:
                            ip = idx - 1
                            if e == 1 and qi >= 1:
                                t.wait_ge(sm["sU"], qi)
                            t.matmul(pD_mx[:, 0:512], ident_bf[:, :], rA[ip % 2][:, :],
                                     start=(e == 1), stop=False).then_inc(sm["sN"], 1)
                            t.matmul(pD_mx[:, 512:1024], ident_bf[:, :], rB[ip % 2][:, :],
                                     start=(e == 1), stop=False).then_inc(sm["sX"], 1)
                        emit_down(2)
                    ip = qi * E + E - 1
                    t.wait_ge(sm["sO"], ip + 1)
                    t.wait_ge(sm["sS"], ip + 1)
                    t.matmul(pD_mx[:, 0:512], ident_bf[:, :], rA[ip % 2][:, :],
                             start=False, stop=True).then_inc(sm["sN"], 1)
                    t.matmul(pD_mx[:, 512:1024], ident_bf[:, :], rB[ip % 2][:, :],
                             start=False, stop=True).then_inc(sm["sX"], 1)
                    emit_down(3)
                    prev_q[0] = (tt, qtr)
            mix_transposes(*prev_q[0])
            emit_down(len(pend))

    esD.close()
    es_persist.close()
    es_sem.close()
    return nc


def kernel(**inputs):
    i = inputs
    x = np.ascontiguousarray(i["L0_attn_q"], dtype=np.float32)
    pos = positions_np()
    posT = np.zeros((192, 4096), dtype=ml_dtypes.bfloat16)
    posT[:, :4095] = pos.T.astype(ml_dtypes.bfloat16)

    common = {
        "w_qs": np.ascontiguousarray(i["L0_attn_w_qs"], np.float32),
        "w_ks": np.ascontiguousarray(i["L0_attn_w_ks"], np.float32),
        "w_vs": np.ascontiguousarray(i["L0_attn_w_vs"], np.float32),
        "w_rel": np.ascontiguousarray(i["L0_attn_w_rel_pos_k"], np.float32),
        "reb": np.ascontiguousarray(np.asarray(i["L0_attn_rel_embed_bias"]).reshape(H, DK), np.float32),
        "rpb": np.ascontiguousarray(np.asarray(i["L0_attn_rel_pos_bias"]).reshape(H, DK), np.float32),
        "w_out": np.ascontiguousarray(i["L0_attn_w_out"], np.float32),
        "b_out": np.ascontiguousarray(np.asarray(i["L0_attn_b_out"]).reshape(1, D), np.float32),
        "wi": np.ascontiguousarray(i["L0_ffn_wi"], np.float32),
        "bi": np.ascontiguousarray(np.asarray(i["L0_ffn_bi"]).reshape(1, F), np.float32),
        "wo": np.ascontiguousarray(i["L0_ffn_wo"], np.float32),
        "bo": np.ascontiguousarray(np.asarray(i["L0_ffn_bo"]).reshape(1, D), np.float32),
        "lora_A": np.ascontiguousarray(np.asarray(i["L0_ffn_lora_A"]).reshape(E * 16, D), np.float32),
        "lora_B": np.ascontiguousarray(np.asarray(i["L0_ffn_lora_B"]).reshape(E * F, 16), np.float32),
        "gate_w": np.ascontiguousarray(i["L0_ffn_gate_w"], np.float32),
        "gate_b": np.ascontiguousarray(np.asarray(i["L0_ffn_gate_b"]).reshape(1, E), np.float32),
        "posT": posT,
    }
    nc = build()
    in_maps = [dict(common, x=np.ascontiguousarray(x[b])) for b in range(8)]
    res = run_bass_kernel_spmd(nc, in_maps, core_ids=list(range(8)))
    return np.stack([np.asarray(res.results[b]["out"], np.float32) for b in range(8)], axis=0)




# version 27
# speedup vs baseline: 1.0730x; 1.0053x over previous
import math
import numpy as np
import ml_dtypes
from contextlib import ExitStack

import concourse.bass as bass
import concourse.mybir as mybir
from concourse.bass_utils import run_bass_kernel_spmd

F32 = mybir.dt.float32
BF16 = mybir.dt.bfloat16
AF = mybir.ActivationFunctionType
ALU = mybir.AluOpType
AX = mybir.AxisListType

L = 2048
D = 1024
H = 8
DK = 128
F = 4096
E = 8
NT = L // 128
ND = D // 128
NF = F // 128
W = 2176
SOFT_SHIFT = -20.0
QSCALE = DK ** -0.5


def positions_np():
    from scipy.special import gammaln
    d = np.arange(-L + 1, L, dtype=np.float64)
    n = 192 // 6
    absd = np.abs(d)[:, None]
    half_life = 2.0 ** np.linspace(3.0, math.log(L) / math.log(2.0), n)
    feat_exp = np.exp(-math.log(2.0) / half_life[None, :] * absd)
    center_widths = 2.0 ** np.arange(1, n + 1, dtype=np.float64) - 1.0
    feat_cm = (center_widths[None, :] > absd).astype(np.float64)
    stddev = L / (2.0 * n)
    mean = np.linspace(L / n, float(L), n)[None, :]
    conc = (mean / stddev) ** 2
    rate = mean / stddev ** 2
    with np.errstate(divide="ignore"):
        log_absd = np.where(absd > 0, np.log(np.where(absd > 0, absd, 1.0)), -np.inf)
    log_unnorm = (conc - 1.0) * log_absd - rate * absd
    log_norm = gammaln(conc) - conc * np.log(rate)
    probs = np.exp(log_unnorm - log_norm) + 1e-08
    feat_gamma = probs / np.max(probs, axis=-1, keepdims=True)
    emb = np.concatenate([feat_exp, feat_cm, feat_gamma], axis=-1)
    emb = np.concatenate([emb, np.sign(d)[:, None] * emb], axis=-1)
    return emb.astype(np.float32)


CDBG = set()


def build(max_phase=99, debug_dumps=False, cnit=None):
    nc = bass.Bass("TRN2", target_bir_lowering=False, detect_race_conditions=False)

    x_d = nc.dram_tensor("x", [L, D], F32, kind="ExternalInput")
    wqs_d = nc.dram_tensor("w_qs", [D, D], F32, kind="ExternalInput")
    wks_d = nc.dram_tensor("w_ks", [D, D], F32, kind="ExternalInput")
    wvs_d = nc.dram_tensor("w_vs", [D, D], F32, kind="ExternalInput")
    wrel_d = nc.dram_tensor("w_rel", [192, D], F32, kind="ExternalInput")
    reb_d = nc.dram_tensor("reb", [H, DK], F32, kind="ExternalInput")
    rpb_d = nc.dram_tensor("rpb", [H, DK], F32, kind="ExternalInput")
    wout_d = nc.dram_tensor("w_out", [D, D], F32, kind="ExternalInput")
    bout_d = nc.dram_tensor("b_out", [1, D], F32, kind="ExternalInput")
    wi_d = nc.dram_tensor("wi", [F, D], F32, kind="ExternalInput")
    bi_d = nc.dram_tensor("bi", [1, F], F32, kind="ExternalInput")
    wo_d = nc.dram_tensor("wo", [D, F], F32, kind="ExternalInput")
    bo_d = nc.dram_tensor("bo", [1, D], F32, kind="ExternalInput")
    lA_d = nc.dram_tensor("lora_A", [E * 16, D], F32, kind="ExternalInput")
    lB_d = nc.dram_tensor("lora_B", [E * F, 16], F32, kind="ExternalInput")
    gw_d = nc.dram_tensor("gate_w", [E, D], F32, kind="ExternalInput")
    gb_d = nc.dram_tensor("gate_b", [1, E], F32, kind="ExternalInput")
    posT_d = nc.dram_tensor("posT", [192, 4096], BF16, kind="ExternalInput")
    out_d = nc.dram_tensor("out", [L, D], F32, kind="ExternalOutput")

    bounce_d = [nc.dram_tensor(f"bounce{i}", [128, W], BF16, kind="Internal") for i in range(2)]
    rkt_d = nc.dram_tensor("rkt", [H * 128, 4096], BF16, kind="Internal")
    intpm_d = nc.dram_tensor("intpm", [E * 32, L], BF16, kind="Internal")
    btp_d = nc.dram_tensor("btp", [E * 32, F], BF16, kind="Internal")
    s_dram = nc.dram_tensor("s_dram", [L, D], BF16, kind="Internal")
    wiT_d = nc.dram_tensor("wiT_d", [ND * 128, F], BF16, kind="Internal")
    woT_d = nc.dram_tensor("woT_d", [NF * 128, D], BF16, kind="Internal")

    ident_bf_d = nc.inline_tensor(np.eye(128, dtype=ml_dtypes.bfloat16), "ident_bf_d")
    ident_f_d = nc.inline_tensor(np.eye(128, dtype=np.float32), "ident_f_d")
    ones_row_d = nc.inline_tensor(np.ones((1, 128), dtype=ml_dtypes.bfloat16), "ones_row_d")

    es_persist = ExitStack()
    P = es_persist.enter_context

    ident_bf = P(nc.sbuf_tensor("ident_bf", [128, 128], BF16))
    ident_f = P(nc.sbuf_tensor("ident_f", [128, 128], F32))
    ones_row = P(nc.sbuf_tensor("ones_row", [1, 128], BF16))
    rebT = P(nc.sbuf_tensor("rebT", [128, H], F32))
    rpbT = P(nc.sbuf_tensor("rpbT", [128, H], F32))
    gwT = P(nc.sbuf_tensor("gwT", [128, ND * E], BF16))
    Apm = P(nc.sbuf_tensor("Apm", [128, ND * 256], BF16))
    bout_bf = P(nc.sbuf_tensor("bout_bf", [1, D], BF16))
    gb_bf = P(nc.sbuf_tensor("gb_bf", [1, E], BF16))
    G_sb = P(nc.sbuf_tensor("G_sb", [128, NT * E], F32))
    gsumrow = P(nc.sbuf_tensor("gsumrow", [1, L], BF16))
    stats = P(nc.sbuf_tensor("stats", [128, 2 * NT], F32))
    tiny = P(nc.sbuf_tensor("tiny", [128, 16], F32))
    cm20 = P(nc.sbuf_tensor("cm20", [128, 1], F32))
    cqs = P(nc.sbuf_tensor("cqs", [128, 1], F32))

    SEMN = ["sA", "sB", "sC", "sD", "sE", "sF", "sG", "sH", "sI", "sJ",
            "sK", "sL", "sM", "sN", "sO", "sP", "sQ", "sR", "sS", "sT",
            "sU", "sV", "sW", "sX"]
    CHN = ["vch", "sch"]
    DSEMN = ["A1s", "A1g", "A2r", "A2b", "A3l0", "A3l1", "A3s0", "A3s1",
             "Bx0", "Bx1", "Bwv", "Bwqk", "Bint",
             "Cbo0", "Cbo1", "Cpi0", "Cpi1", "Crk0", "Crk1",
             "Swo", "Sst0", "Sst1",
             "DwiT", "DwoT", "Dbt0", "Dbt1", "Dbt2", "Dbt3", "Dxs0", "Dxs1",
             "Dip0", "Dip1", "Dout0", "Dout1"]
    es_sem = ExitStack()
    sm = {n: es_sem.enter_context(nc.semaphore(n)) for n in SEMN + CHN}
    dm = {n: es_sem.enter_context(nc.semaphore(n)) for n in DSEMN}

    def dump_block(pairs):
        out_flat = out_d
        with nc.Block() as blk:
            @blk.gpsimd
            def _(gp):
                k = [0]
                for off, srcap, n in pairs:
                    row0, col0 = divmod(off, D)
                    assert col0 == 0 and n % D == 0
                    dst = out_flat[row0:row0 + n // D, :]
                    gp.dma_start(dst, srcap).then_inc(dm["Dout0"], 16)
                    k[0] += 16
                gp.wait_ge(dm["Dout0"], k[0])
    def clear_block():
        with nc.Block() as blk:
            @blk.sync
            def _(sy):
                for n in SEMN + CHN:
                    sy.sem_clear(sm[n])

    esA = ExitStack()
    posTA = esA.enter_context(nc.sbuf_tensor("posTA", [128, 4096], BF16))
    posTB = esA.enter_context(nc.sbuf_tensor("posTB", [64, 4096], BF16))
    wrelA = esA.enter_context(nc.sbuf_tensor("wrelA", [128, D], BF16))
    wrelB = esA.enter_context(nc.sbuf_tensor("wrelB", [64, D], BF16))
    rkt_stage = esA.enter_context(nc.sbuf_tensor("rkt_stage", [128, 4096], BF16))
    An_sb = esA.enter_context(nc.sbuf_tensor("An_sb", [128, D], BF16))
    AT_sb = esA.enter_context(nc.sbuf_tensor("AT_sb", [128, ND * 128], BF16))
    Bn_sb = esA.enter_context(nc.sbuf_tensor("Bn_sb", [128, E * 512], BF16))
    BT_sb = esA.enter_context(nc.sbuf_tensor("BT_sb", [16, F], BF16))
    BT_z = esA.enter_context(nc.sbuf_tensor("BT_z", [16, F], BF16))
    gwn = esA.enter_context(nc.sbuf_tensor("gwn", [E, D], BF16))
    rebn = esA.enter_context(nc.sbuf_tensor("rebn", [H, DK], F32))
    rpbn = esA.enter_context(nc.sbuf_tensor("rpbn", [H, DK], F32))
    brow_f = esA.enter_context(nc.sbuf_tensor("brow_f", [1, D], F32))

    pA1 = esA.enter_context(nc.psum_tensor("pA1", [128, 512], F32))
    pA2 = esA.enter_context(nc.psum_tensor("pA2", [128, 512], F32))
    pAt = esA.enter_context(nc.psum_tensor("pAt", [128, 128], F32))
    pAtb = esA.enter_context(nc.psum_tensor("pAtb", [16, 2048], BF16))
    pAab = esA.enter_context(nc.psum_tensor("pAab", [128, 128], BF16))
    pAg = esA.enter_context(nc.psum_tensor("pAg", [128, E], BF16))

    with nc.Block() as blk:
        @blk.sync
        def _(sy):
            sy.dma_start(posTA[:, :], posT_d[0:128, :]).then_inc(dm["A1s"], 16)
            sy.dma_start(posTB[:, :], posT_d[128:192, :]).then_inc(dm["A1s"], 16)
            sy.dma_start(rebn[:, :], reb_d[:, :]).then_inc(dm["A1s"], 16)
            sy.dma_start(rpbn[:, :], rpb_d[:, :]).then_inc(dm["A1s"], 16)
            sy.dma_start(brow_f[:, :], bout_d[:, :]).then_inc(dm["A1s"], 16)

        @blk.gpsimd
        def _(gp):
            gp.dma_start(wrelA[:, :], wrel_d[0:128, :]).then_inc(dm["A1g"], 16)
            gp.dma_start(wrelB[:, :], wrel_d[128:192, :]).then_inc(dm["A1g"], 16)
            gp.dma_start(An_sb[:, :], lA_d[:, :]).then_inc(dm["A1g"], 16)
            gp.dma_start(gwn[:, :], gw_d[:, :]).then_inc(dm["A1g"], 16)
            gp.dma_start(gb_bf[:, :], gb_d[:, :]).then_inc(dm["A1g"], 16)
            for e in range(E):
                src = bass.AP(lB_d, e * F * 16, [[16, 128], [128 * 16, 32], [1, 16]])
                dstap = bass.AP(Bn_sb, e * 512, [[E * 512, 128], [16, 32], [1, 16]])
                gp.dma_start(dstap, src).then_inc(dm["A1g"], 16)
            gp.dma_start(ident_bf[:, :], ident_bf_d[:, :]).then_inc(dm["A1g"], 16)
            gp.dma_start(ident_f[:, :], ident_f_d[:, :]).then_inc(dm["A1g"], 16)
            gp.dma_start(ones_row[:, :], ones_row_d[:, :]).then_inc(dm["A1g"], 16)

    with nc.Block() as blk:
        @blk.tensor
        def _(t):
            t.wait_ge(dm["A1s"], 16 * 5)
            t.wait_ge(dm["A1g"], 16 * 16)
            mmv = 0
            for h in range(H):
                for ch in range(8):
                    pb = [pA1, pA2][mmv % 2]
                    if mmv >= 2:
                        t.wait_ge(sm["sD"], mmv - 1)
                    t.matmul(pb[:, :], wrelA[:, h * 128:(h + 1) * 128],
                             posTA[:, ch * 512:(ch + 1) * 512], start=True, stop=False)
                    t.matmul(pb[:, :], wrelB[:, h * 128:(h + 1) * 128],
                             posTB[:, ch * 512:(ch + 1) * 512],
                             start=False, stop=True).then_inc(sm["sC"], 1)
                    mmv += 1
            nc.tensor.transpose(pAt[:, 0:8], rebn[:, :], ident_f[0:8, 0:8])
            nc.tensor.transpose(pAt[:, 8:16], rpbn[:, :],
                                ident_f[0:8, 0:8]).then_inc(sm["sE"], 1)
            for dt in range(ND):
                nc.tensor.transpose(pAg[:, :], gwn[:, dt * 128:(dt + 1) * 128],
                                    ident_bf[0:8, 0:8]).then_inc(sm["sF"], 1)
                t.wait_ge(sm["sG"], dt + 1)
            for dt in range(ND):
                nc.tensor.transpose(pAab[:, :], An_sb[:, dt * 128:(dt + 1) * 128],
                                    ident_bf[:, :]).then_inc(sm["sH"], 1)
                t.wait_ge(sm["sI"], dt + 1)
            for e in range(E):
                for half in range(2):
                    for tt in range(16):
                        tg = half * 16 + tt
                        ins = nc.tensor.transpose(
                            pAtb[:, tt * 128:(tt + 1) * 128],
                            Bn_sb[:, e * 512 + tg * 16: e * 512 + tg * 16 + 16],
                            ident_bf[:, :])
                        if tt == 15:
                            ins.then_inc(sm["sJ"], 1)
                    t.wait_ge(sm["sK"], e * 2 + half + 1)

        @blk.vector
        def _(v):
            for h in range(H):
                if h >= 1:
                    v.wait_ge(dm["A2r"], 16 * h)
                for ch in range(8):
                    mmv = h * 8 + ch
                    v.wait_ge(sm["sC"], mmv + 1)
                    v.tensor_copy(rkt_stage[:, ch * 512:(ch + 1) * 512],
                                  [pA1, pA2][mmv % 2][:, :]).then_inc(sm["sD"], 1)
            v.wait_ge(sm["sE"], 1)
            v.tensor_copy(rebT[:, :], pAt[:, 0:8])
            v.tensor_copy(rpbT[:, :], pAt[:, 8:16])
            for dt in range(ND):
                v.wait_ge(sm["sF"], dt + 1)
                v.tensor_copy(gwT[:, dt * E:(dt + 1) * E], pAg[:, :]).then_inc(sm["sG"], 1)
            for dt in range(ND):
                v.wait_ge(sm["sH"], dt + 1)
                v.tensor_copy(AT_sb[:, dt * 128:(dt + 1) * 128],
                              pAab[:, :]).then_inc(sm["sI"], 1)
            v.memset(BT_z[:, :], 0.0).then_inc(sm["sL"], 1)
            for e in range(E):
                if e >= 1:
                    v.wait_ge(dm["A2b"], 16 * (2 * e + 1))
                for half in range(2):
                    v.wait_ge(sm["sJ"], e * 2 + half + 1)
                    v.tensor_copy(BT_sb[:, half * 2048:(half + 1) * 2048],
                                  pAtb[:, :]).then_inc(sm["sK"], 1)
            for dt in range(ND):
                for e in range(E):
                    base = dt * 256 + e * 32
                    v.tensor_copy(Apm[:, base:base + 16],
                                  AT_sb[:, dt * 128 + e * 16: dt * 128 + e * 16 + 16])
                    if e == 0:
                        v.memset(Apm[:, base + 16:base + 32], 0.0)
                    else:
                        v.tensor_scalar(Apm[:, base + 16:base + 32],
                                        AT_sb[:, dt * 128 + (e - 1) * 16: dt * 128 + e * 16],
                                        -1.0, None, ALU.mult)
            v.wait_ge(dm["A1s"], 16 * 5)
            v.tensor_copy(bout_bf[:, :], brow_f[:, :])
            v.memset(cm20[:, :], SOFT_SHIFT)
            v.memset(cqs[:, :], QSCALE)

        @blk.gpsimd
        def _(gp):
            for h in range(H):
                gp.wait_ge(sm["sD"], h * 8 + 8)
                gp.dma_start(rkt_d[h * 128:(h + 1) * 128, :],
                             rkt_stage[:, :]).then_inc(dm["A2r"], 16)
            gp.wait_ge(sm["sL"], 1)
            gp.dma_start(btp_d[16:32, :], BT_z[:, :]).then_inc(dm["A2b"], 16)
            for e in range(E):
                gp.wait_ge(sm["sK"], 2 * (e + 1))
                gp.dma_start(btp_d[e * 32:e * 32 + 16, :],
                             BT_sb[:, :]).then_inc(dm["A2b"], 16)
                if e < E - 1:
                    gp.dma_start(btp_d[(e + 1) * 32 + 16:(e + 2) * 32, :],
                                 BT_sb[:, :]).then_inc(dm["A2b"], 16)
            gp.wait_ge(dm["A2b"], 16 * 16)
            gp.wait_ge(dm["A2r"], 16 * H)

    esA.close()
    clear_block()
    if max_phase <= 1:
        if debug_dumps:
            dump_block([(0, rkt_d[0:128, :], 128 * 4096),
                        (512 * 1024, btp_d[0:64, :], 64 * F),
                        (768 * 1024, intpm_d[0:32, :], 32 * L)])
        es_persist.close()
        es_sem.close()
        return nc


    esA3 = ExitStack()
    natst = [esA3.enter_context(nc.sbuf_tensor(f"natst{i}", [128, 4096], BF16)) for i in range(2)]
    tost = [esA3.enter_context(nc.sbuf_tensor(f"tost{i}", [128, 1024], BF16)) for i in range(2)]
    pW = [esA3.enter_context(nc.psum_tensor(f"pW{i}", [128, 1024], BF16)) for i in range(2)]

    with nc.Block() as blk:
        @blk.gpsimd
        def _(gp):
            nld = 0
            for r in range(8):
                if nld >= 2:
                    gp.wait_ge(sm["sC"], 4 * (nld - 1))
                src = bass.AP(wi_d, r * 512 * D, [[D, 128], [128 * D, 4], [1, D]])
                dstap = bass.AP(natst[nld % 2], 0, [[4096, 128], [1024, 4], [1, D]])
                gp.dma_start(dstap, src).then_inc(dm["A3l0" if nld % 2 == 0 else "A3l1"], 16)
                nld += 1
            for dt in range(8):
                if nld >= 2:
                    gp.wait_ge(sm["sC"], 4 * (nld - 1))
                src = bass.AP(wo_d, dt * 128 * F, [[F, 128], [1, F]])
                gp.dma_start(natst[nld % 2][:, :],
                             src).then_inc(dm["A3l0" if nld % 2 == 0 else "A3l1"], 16)
                nld += 1

        @blk.tensor
        def _(t):
            nb = 0
            for r in range(8):
                t.wait_ge(dm["A3l0" if r % 2 == 0 else "A3l1"], 16 * (r // 2 + 1))
                for ftl in range(4):
                    if nb >= 2:
                        t.wait_ge(sm["sC"], nb - 1)
                    for dt in range(8):
                        ins = nc.tensor.transpose(
                            pW[nb % 2][:, dt * 128:(dt + 1) * 128],
                            natst[r % 2][:, ftl * 1024 + dt * 128: ftl * 1024 + (dt + 1) * 128],
                            ident_bf[:, :])
                        if dt == 7:
                            ins.then_inc(sm["sD"], 1)
                    nb += 1
            for dtw in range(8):
                nw = 8 + dtw
                t.wait_ge(dm["A3l0" if nw % 2 == 0 else "A3l1"], 16 * (nw // 2 + 1))
                for b4 in range(4):
                    if nb >= 2:
                        t.wait_ge(sm["sC"], nb - 1)
                    for ftl in range(8):
                        ins = nc.tensor.transpose(
                            pW[nb % 2][:, ftl * 128:(ftl + 1) * 128],
                            natst[nw % 2][:, b4 * 1024 + ftl * 128: b4 * 1024 + (ftl + 1) * 128],
                            ident_bf[:, :])
                        if ftl == 7:
                            ins.then_inc(sm["sD"], 1)
                    nb += 1

        @blk.vector
        def _(v):
            for nb in range(64):
                v.wait_ge(sm["sD"], nb + 1)
                if nb >= 2:
                    v.wait_ge(dm["A3s0" if nb % 2 == 0 else "A3s1"],
                              16 * (nb // 2))
                v.tensor_copy(tost[nb % 2][:, :], pW[nb % 2][:, :]).then_inc(sm["sC"], 1)

        @blk.sync
        def _(sy):
            nb = 0
            for r in range(8):
                for ftl in range(4):
                    sy.wait_ge(sm["sC"], nb + 1)
                    dst = bass.AP(wiT_d, (r * 4 + ftl) * 128,
                                  [[F, 128], [128 * F, 8], [1, 128]])
                    sy.dma_start(dst, bass.AP(tost[nb % 2], 0,
                                              [[1024, 128], [128, 8], [1, 128]])
                                 ).then_inc(dm["A3s0" if nb % 2 == 0 else "A3s1"], 16)
                    nb += 1
            for dtw in range(8):
                for b4 in range(4):
                    sy.wait_ge(sm["sC"], nb + 1)
                    dst = bass.AP(woT_d, (b4 * 8) * 128 * D + dtw * 128,
                                  [[D, 128], [128 * D, 8], [1, 128]])
                    sy.dma_start(dst, bass.AP(tost[nb % 2], 0,
                                              [[1024, 128], [128, 8], [1, 128]])
                                 ).then_inc(dm["A3s0" if nb % 2 == 0 else "A3s1"], 16)
                    nb += 1
            sy.wait_ge(dm["A3s0"], 16 * 32)
            sy.wait_ge(dm["A3s1"], 16 * 32)

    esA3.close()
    clear_block()
    if max_phase <= 2:
        if debug_dumps:
            dump_block([(0, bass.AP(wiT_d, 0, [[F, 256], [1, F]]), 256 * F),
                        (1024 * 1024, bass.AP(woT_d, 0, [[D, 1024], [1, D]]), 1024 * D)])
        es_persist.close()
        es_sem.close()
        return nc


    es_bc = ExitStack()
    QsT = es_bc.enter_context(nc.sbuf_tensor("QsT", [128, ND * L], BF16))
    KT = es_bc.enter_context(nc.sbuf_tensor("KT", [128, ND * L], BF16))
    V_sb = es_bc.enter_context(nc.sbuf_tensor("V_sb", [128, NT * D], BF16))
    coT = es_bc.enter_context(nc.sbuf_tensor("coT", [128, ND * L], BF16))

    esB = ExitStack()
    xnT = esB.enter_context(nc.sbuf_tensor("xnT", [128, ND * L], BF16))

    esB1 = ExitStack()
    x_t = [esB1.enter_context(nc.sbuf_tensor(f"x_t{i}", [128, D], F32)) for i in range(2)]
    xn_nat = [esB1.enter_context(nc.sbuf_tensor(f"xn_nat{i}", [128, D], BF16)) for i in range(2)]
    sq_sc = [esB1.enter_context(nc.sbuf_tensor(f"sq_sc{i}", [128, D], F32)) for i in range(2)]
    stg = esB1.enter_context(nc.sbuf_tensor("stg", [128, 32], F32))
    Lg = esB1.enter_context(nc.sbuf_tensor("Lg", [128, NT * E], F32))
    g8b = esB1.enter_context(nc.sbuf_tensor("g8b", [128, 4 * NT * E], F32))
    gsc = esB1.enter_context(nc.sbuf_tensor("gsc", [128, 6 * NT], F32))
    pB_xn = esB1.enter_context(nc.psum_tensor("pB_xn", [128, ND * 128], BF16))
    pB_g = esB1.enter_context(nc.psum_tensor("pB_g", [128, E], F32))
    pB_gs = esB1.enter_context(nc.psum_tensor("pB_gs", [1, 128], F32))

    def b3(tens, off, n2, n3):
        return bass.AP(tens, off, [[tens.shape[1], 128], [n3, n2], [1, n3]])

    with nc.Block() as blk:
        @blk.sync
        def _(sy):
            for p in range(2):
                for tt in range(NT):
                    k = p * NT + tt
                    if 2 <= k and k - 2 < NT:
                        sy.wait_ge(sm["sD"], k - 1)
                    elif k - 2 >= NT:
                        sy.wait_ge(sm["sH"], k - NT - 1)
                    sy.dma_start(x_t[k % 2][:, :],
                                 x_d[tt * 128:(tt + 1) * 128, :]
                                 ).then_inc(dm["Bx0" if k % 2 == 0 else "Bx1"], 16)

        @blk.vector
        def _(v):
            vc = [0]
            def vfence(ins):
                ins.then_inc(sm["vch"], 1)
                vc[0] += 1
                v.wait_ge(sm["vch"], vc[0])
            for tt in range(NT):
                v.wait_ge(dm["Bx0" if tt % 2 == 0 else "Bx1"], 16 * (tt // 2 + 1))
                xt = x_t[tt % 2]
                v.reduce_sum(stg[:, tt:tt + 1], xt[:, :], axis=AX.X)
                vfence(v.tensor_mul(sq_sc[tt % 2][:, :], xt[:, :], xt[:, :]))
                ins = v.reduce_sum(stg[:, NT + tt:NT + tt + 1], sq_sc[tt % 2][:, :], axis=AX.X)
                ins.then_inc(sm["sD"], 1)
            v.tensor_scalar(gsc[:, 0:NT], stg[:, 0:NT], 1.0 / D, None, ALU.mult)
            vfence(v.tensor_scalar(gsc[:, NT:2 * NT], stg[:, NT:2 * NT], 1.0 / D, None, ALU.mult))
            vfence(v.tensor_mul(gsc[:, 2 * NT:3 * NT], gsc[:, 0:NT], gsc[:, 0:NT]))
            vfence(v.tensor_sub(gsc[:, NT:2 * NT], gsc[:, NT:2 * NT], gsc[:, 2 * NT:3 * NT]))
            vfence(v.tensor_scalar(gsc[:, NT:2 * NT], gsc[:, NT:2 * NT], 1e-5, None, ALU.add))
            ins = v.reciprocal(gsc[:, 2 * NT:3 * NT], gsc[:, NT:2 * NT])
            ins.then_inc(sm["sE"], 1)
            v.wait_ge(sm["sF"], 1)
            v.tensor_mul(stats[:, NT:2 * NT], gsc[:, 0:NT],
                         stats[:, 0:NT]).then_inc(sm["vch"], 1)
            vc[0] += 1
            v.wait_ge(sm["vch"], vc[0])
            v.tensor_scalar(stats[:, NT:2 * NT], stats[:, NT:2 * NT], -1.0,
                            None, ALU.mult).then_inc(sm["sG"], 1)
            for tt in range(NT):
                v.wait_ge(sm["sI"], tt + 1)
                dst = bass.AP(xnT, tt * 128, [[ND * L, 128], [L, ND], [1, 128]])
                srcp = bass.AP(pB_xn, 0, [[ND * 128, 128], [128, ND], [1, 128]])
                v.tensor_copy(dst, srcp).then_inc(sm["sJ"], 1)
                v.wait_ge(sm["sK"], tt + 1)
                v.tensor_copy(Lg[:, tt * E:(tt + 1) * E],
                              pB_g[:, :]).then_inc(sm["sL"], 1)
            E8, SC, MK, CT = 0, NT * E, 2 * NT * E, 3 * NT * E
            m1c, m2c, vsc, s8c = 3 * NT, 4 * NT, 5 * NT, NT
            vfence(v.reduce_max(gsc[:, m1c:m1c + NT], b3(Lg, 0, NT, E), axis=AX.X))
            mb = bass.AP(gsc, m1c, [[6 * NT, 128], [1, NT], [0, E]])
            ins = v.tensor_sub(b3(g8b, E8, NT, E), b3(Lg, 0, NT, E), mb)
            ins.then_inc(sm["sU"], 1)
            v.wait_ge(sm["sM"], 1)
            vfence(v.reduce_sum(gsc[:, s8c:s8c + NT], b3(g8b, E8, NT, E), axis=AX.X))
            vfence(v.reciprocal(gsc[:, 2 * NT:3 * NT], gsc[:, s8c:s8c + NT]))
            rb = bass.AP(gsc, 2 * NT, [[6 * NT, 128], [1, NT], [0, E]])
            vfence(v.tensor_mul(b3(g8b, SC, NT, E), b3(g8b, E8, NT, E), rb))
            vfence(v.reduce_max(gsc[:, m1c:m1c + NT], b3(g8b, SC, NT, E), axis=AX.X))
            m1b = bass.AP(gsc, m1c, [[6 * NT, 128], [1, NT], [0, E]])
            vfence(v.tensor_tensor(b3(g8b, MK, NT, E), b3(g8b, SC, NT, E), m1b, ALU.is_equal))
            vfence(v.scalar_tensor_tensor(b3(g8b, MK, NT, E), b3(g8b, MK, NT, E), -1e30,
                                          b3(g8b, SC, NT, E), ALU.mult, ALU.add))
            vfence(v.reduce_max(gsc[:, m2c:m2c + NT], b3(g8b, MK, NT, E), axis=AX.X))
            vfence(v.tensor_add(gsc[:, vsc:vsc + NT], gsc[:, m1c:m1c + NT],
                                gsc[:, m2c:m2c + NT]))
            m2b = bass.AP(gsc, m2c, [[6 * NT, 128], [1, NT], [0, E]])
            vfence(v.tensor_tensor(b3(g8b, CT, NT, E), b3(g8b, SC, NT, E), m2b, ALU.is_ge))
            vsb = bass.AP(gsc, vsc, [[6 * NT, 128], [1, NT], [0, E]])
            vfence(v.tensor_tensor(b3(G_sb, 0, NT, E), b3(g8b, CT, NT, E), vsb, ALU.mult))
            v.tensor_scalar(gsc[:, vsc:vsc + NT], gsc[:, vsc:vsc + NT], 2.0,
                            None, ALU.mult).then_inc(sm["sN"], 1)
            for tt in range(NT):
                v.wait_ge(sm["sO"], tt + 1)
                v.tensor_copy(gsumrow[0:1, tt * 128:(tt + 1) * 128],
                              pB_gs[0:1, :]).then_inc(sm["sP"], 1)

        @blk.scalar
        def _(s):
            s.wait_ge(sm["sE"], 1)
            s.activation(stats[:, 0:NT], gsc[:, 2 * NT:3 * NT],
                         AF.Sqrt).then_inc(sm["sF"], 1)
            for tt in range(NT):
                k = NT + tt
                s.wait_ge(dm["Bx0" if k % 2 == 0 else "Bx1"], 16 * (k // 2 + 1))
                if tt == 0:
                    s.wait_ge(sm["sG"], 1)
                if tt >= 2:
                    s.wait_ge(sm["sI"], tt - 1)
                nc.scalar.activation(xn_nat[k % 2][:, :], x_t[k % 2][:, :], AF.Identity,
                                     bias=stats[:, NT + tt:NT + tt + 1],
                                     scale=stats[:, tt:tt + 1]).then_inc(sm["sH"], 1)
            s.wait_ge(sm["sL"], NT)
            s.wait_ge(sm["sU"], 1)
            nc.scalar.activation(g8b[:, 0:NT * E], g8b[:, 0:NT * E],
                                 AF.Exp).then_inc(sm["sM"], 1)

        @blk.tensor
        def _(t):
            for tt in range(NT):
                k = NT + tt
                t.wait_ge(sm["sH"], tt + 1)
                if tt >= 1:
                    t.wait_ge(sm["sJ"], tt)
                for ct in range(ND):
                    ins = nc.tensor.transpose(pB_xn[:, ct * 128:(ct + 1) * 128],
                                              xn_nat[k % 2][:, ct * 128:(ct + 1) * 128],
                                              ident_bf[:, :])
                    if ct == ND - 1:
                        ins.then_inc(sm["sI"], 1)
                if tt >= 1:
                    t.wait_ge(sm["sL"], tt)
                t.wait_ge(sm["sJ"], tt + 1)
                for ct in range(ND):
                    t.matmul(pB_g[:, :], xnT[:, ct * L + tt * 128: ct * L + (tt + 1) * 128],
                             gwT[:, ct * E:(ct + 1) * E], start=(ct == 0), stop=False)
                t.matmul(pB_g[:, :], ones_row[:, :], gb_bf[:, :],
                         start=False, stop=True).then_inc(sm["sK"], 1)
            t.wait_ge(sm["sN"], 1)
            for tt in range(NT):
                if tt >= 1:
                    t.wait_ge(sm["sP"], tt)
                nc.tensor.transpose(pB_gs[:, :], gsc[:, 5 * NT + tt:5 * NT + tt + 1],
                                    ident_f[:, :]).then_inc(sm["sO"], 1)

    esB1.close()
    clear_block()

    esB2v = ExitStack()
    wv_sb = esB2v.enter_context(nc.sbuf_tensor("wv_sb", [128, ND * D], BF16))
    pB_v = [esB2v.enter_context(nc.psum_tensor(f"pB_v{i}", [128, D], F32))
            for i in range(2)]

    with nc.Block() as blk:
        @blk.gpsimd
        def _(gp):
            srcap = bass.AP(wvs_d, 0, [[D, 128], [128 * D, ND], [1, D]])
            dstap = bass.AP(wv_sb, 0, [[ND * D, 128], [D, ND], [1, D]])
            gp.dma_start(dstap, srcap).then_inc(dm["Bwv"], 16)

        @blk.tensor
        def _(t):
            t.wait_ge(dm["Bwv"], 16)
            for tt in range(NT):
                if tt >= 2:
                    t.wait_ge(sm["sQ"], tt - 1)
                for ch in range(2):
                    for ct in range(ND):
                        ins = t.matmul(pB_v[tt % 2][:, ch * 512:(ch + 1) * 512],
                                       xnT[:, ct * L + tt * 128: ct * L + (tt + 1) * 128],
                                       wv_sb[:, ct * D + ch * 512: ct * D + ch * 512 + 512],
                                       start=(ct == 0), stop=(ct == ND - 1))
                        if ch == 1 and ct == ND - 1:
                            ins.then_inc(sm["sP"], 1)

        @blk.vector
        def _(v):
            for tt in range(NT):
                v.wait_ge(sm["sP"], tt + 1)
                v.tensor_copy(V_sb[:, tt * D:(tt + 1) * D],
                              pB_v[tt % 2][:, :]).then_inc(sm["sQ"], 1)

    esB2v.close()
    clear_block()
    if max_phase <= 5:
        esB.close()
        es_bc.close()
        es_persist.close()
        es_sem.close()
        return nc


    esB2 = ExitStack()
    wq_sb = esB2.enter_context(nc.sbuf_tensor("wq_sb", [128, ND * D], BF16))
    wk_sb = esB2.enter_context(nc.sbuf_tensor("wk_sb", [128, ND * D], BF16))
    pB_qk = [esB2.enter_context(nc.psum_tensor(f"pB_qk{i}", [128, L], F32))
             for i in range(2)]

    with nc.Block() as blk:
        @blk.gpsimd
        def _(gp):
            for (dst, srcd) in ((wq_sb, wqs_d), (wk_sb, wks_d)):
                srcap = bass.AP(srcd, 0, [[D, 128], [128 * D, ND], [1, D]])
                dstap = bass.AP(dst, 0, [[ND * D, 128], [D, ND], [1, D]])
                gp.dma_start(dstap, srcap).then_inc(dm["Bwqk"], 16)

        @blk.tensor
        def _(t):
            t.wait_ge(dm["Bwqk"], 16 * 2)
            cnt = 0
            for wsb in (wq_sb, wk_sb):
                for dt in range(ND):
                    if cnt >= 2:
                        if cnt - 2 < ND:
                            t.wait_ge(sm["sN"], cnt - 1)
                        else:
                            t.wait_ge(sm["sU"], cnt - ND - 1)
                    for ch in range(4):
                        for ct in range(ND):
                            ins = t.matmul(pB_qk[cnt % 2][:, ch * 512:(ch + 1) * 512],
                                           wsb[:, ct * D + dt * 128: ct * D + (dt + 1) * 128],
                                           xnT[:, ct * L + ch * 512: ct * L + (ch + 1) * 512],
                                           start=(ct == 0), stop=(ct == ND - 1))
                            if ch == 3 and ct == ND - 1:
                                ins.then_inc(sm["sO"], 1)
                    cnt += 1

        @blk.scalar
        def _(s):
            for dt in range(ND):
                s.wait_ge(sm["sO"], dt + 1)
                nc.scalar.activation(QsT[:, dt * L:(dt + 1) * L], pB_qk[dt % 2][:, :],
                                     AF.Identity,
                                     bias=0.0, scale=cqs[:, :]).then_inc(sm["sN"], 1)

        @blk.vector
        def _(v):
            for dt in range(ND):
                v.wait_ge(sm["sO"], ND + dt + 1)
                v.tensor_copy(KT[:, dt * L:(dt + 1) * L],
                              pB_qk[dt % 2][:, :]).then_inc(sm["sU"], 1)

    esB2.close()
    clear_block()
    if max_phase <= 6:
        esB.close()
        es_bc.close()
        es_persist.close()
        es_sem.close()
        return nc


    esB3 = ExitStack()
    pB_i = esB3.enter_context(nc.psum_tensor("pB_i", [128, L], F32))
    int_st = esB3.enter_context(nc.sbuf_tensor("int_st", [128, L], BF16))

    with nc.Block() as blk:
        @blk.tensor
        def _(t):
            for p in range(2):
                for ch in range(4):
                    for ct in range(ND):
                        ins = t.matmul(pB_i[:, ch * 512:(ch + 1) * 512],
                                       Apm[:, ct * 256 + p * 128: ct * 256 + (p + 1) * 128],
                                       xnT[:, ct * L + ch * 512: ct * L + (ch + 1) * 512],
                                       start=(ct == 0), stop=(ct == ND - 1))
                        if ch == 3 and ct == ND - 1:
                            ins.then_inc(sm["sR"], 1)
                t.wait_ge(sm["sS"], p + 1)

        @blk.vector
        def _(v):
            for p in range(2):
                v.wait_ge(sm["sR"], p + 1)
                if p >= 1:
                    v.wait_ge(dm["Bint"], 16 * p)
                v.tensor_copy(int_st[:, :], pB_i[:, :]).then_inc(sm["sS"], 1)

        @blk.gpsimd
        def _(gp):
            for p in range(2):
                gp.wait_ge(sm["sS"], p + 1)
                gp.dma_start(intpm_d[p * 128:(p + 1) * 128, :],
                             int_st[:, :]).then_inc(dm["Bint"], 16)
            gp.wait_ge(dm["Bint"], 16 * 2)

    esB3.close()
    esB.close()
    clear_block()
    if max_phase <= 7:
        if debug_dumps:
            dump_block([(0, intpm_d[:, :], 256 * L),
                        (512 * 1024, QsT[:, 0:2048], 128 * 2048),
                        (768 * 1024, KT[:, 0:2048], 128 * 2048),
                        (1024 * 1024, V_sb[:, 0:1024], 128 * 1024),
                        (1152 * 1024, bass.AP(G_sb, 0, [[NT * E, 128], [1, 128]]), 128 * 128),
                        (1168 * 1024, gsumrow[0:1, :], 2048)])
        es_bc.close()
        es_persist.close()
        es_sem.close()
        return nc


    esC = ExitStack()
    rkt_sb = [esC.enter_context(nc.sbuf_tensor(f"rkt_sb{i}", [128, 4096], BF16)) for i in range(2)]
    qeT = esC.enter_context(nc.sbuf_tensor("qeT", [128, L], BF16))
    qpT = esC.enter_context(nc.sbuf_tensor("qpT", [128, L], BF16))
    band_sb = [esC.enter_context(nc.sbuf_tensor(f"band_sb{i}", [128, W], BF16)) for i in range(2)]
    P_sb = [esC.enter_context(nc.sbuf_tensor(f"P_sb{i}", [128, L], BF16)) for i in range(2)]
    A_sb = [esC.enter_context(nc.sbuf_tensor(f"A_sb{i}", [128, L], BF16)) for i in range(2)]
    AT4 = esC.enter_context(nc.sbuf_tensor("AT4", [128, 16 * 512], BF16))
    sums_sb = esC.enter_context(nc.sbuf_tensor("sums_sb", [128, 4], F32))

    pC_S = esC.enter_context(nc.psum_tensor("pC_S", [128, L], F32))
    pC_b = [esC.enter_context(nc.psum_tensor(f"pC_b{i}", [128, 512], F32)) for i in range(2)]
    pC_at = esC.enter_context(nc.psum_tensor("pC_at", [128, 1024], BF16))
    pC_av = esC.enter_context(nc.psum_tensor("pC_av", [128, 512], F32))

    CH_W = [512, 512, 512, 512, 128]
    CH_OFF = [0, 512, 1024, 1536, 2048]
    NIT = H * NT
    if cnit is not None:
        NIT = cnit
    VCH = (0, 1)
    SCH = (2, 3, 4)

    def bevac_req(n):
        b, c = divmod(n, 5)
        if c in VCH:
            return ("sB", 2 * b + c + 1)
        return ("sC", 3 * b + c - 1)

    with nc.Block() as blk:
        @blk.tensor
        def _(t):
            nchunk = [0]

            def emit_band():
                n = nchunk[0]
                if n >= 5 * NIT:
                    return
                bi, c = divmod(n, 5)
                hb, jb = divmod(bi, NT)
                if c == 0 and jb == 0:
                    t.wait_ge(sm["sA"], hb + 1)
                    t.wait_ge(dm["Crk0" if hb % 2 == 0 else "Crk1"],
                              16 * (hb // 2 + 1))
                if n >= 2:
                    sname, cntv = bevac_req(n - 2)
                    t.wait_ge(sm[sname], cntv)
                j0 = 1920 - 128 * jb
                t.matmul(pC_b[n % 2][:, 0:CH_W[c]], qpT[:, jb * 128:(jb + 1) * 128],
                         rkt_sb[hb % 2][:, j0 + CH_OFF[c]: j0 + CH_OFF[c] + CH_W[c]],
                         start=True, stop=True).then_inc(sm["sD"], 1)
                nchunk[0] += 1

            def trans_batch(jj, half):
                if half == 0:
                    t.wait_ge(sm["sJ"], jj + 1)
                t.wait_ge(sm["sK"], 2 * jj + half)
                for kt in range(8):
                    ktg = half * 8 + kt
                    ins = nc.tensor.transpose(pC_at[:, kt * 128:(kt + 1) * 128],
                                              A_sb[jj % 2][:, ktg * 128:(ktg + 1) * 128],
                                              ident_bf[:, :])
                    if kt == 7:
                        ins.then_inc(sm["sL"], 1)

            def av_group(j):
                g = j // 4
                hh = j // NT
                t.wait_ge(sm["sK"], 2 * (j + 1))
                if g >= 1:
                    t.wait_ge(sm["sM"], g)
                for kt in range(16):
                    ins = t.matmul(pC_av[:, :],
                                   V_sb[:, kt * D + hh * 128: kt * D + (hh + 1) * 128],
                                   AT4[:, kt * 512:(kt + 1) * 512],
                                   start=(kt == 0), stop=(kt == 15))
                    if kt == 15:
                        ins.then_inc(sm["sP"], 1)

            for _n in range(15):
                emit_band()
            for i in range(NIT):
                h, j = divmod(i, NT)
                t.wait_ge(dm["Cpi0" if i % 2 == 0 else "Cpi1"],
                          16 * (i // 2 + 1))
                if j == 0:
                    t.wait_ge(sm["sF"], h + 1)
                for ch in range(4):
                    if i >= 1 and ch == 0:
                        t.wait_ge(sm["sH"], 2 * i - 1)
                    if i >= 1 and ch == 2:
                        t.wait_ge(sm["sH"], 2 * i)
                    t.matmul(pC_S[:, ch * 512:(ch + 1) * 512],
                             qeT[:, j * 128:(j + 1) * 128],
                             KT[:, h * L + ch * 512: h * L + (ch + 1) * 512],
                             start=True, stop=False)
                for ch in range(4):
                    ins = t.matmul(pC_S[:, ch * 512:(ch + 1) * 512], ident_bf[:, :],
                                   P_sb[i % 2][:, ch * 512:(ch + 1) * 512],
                                   start=False, stop=True)
                    if ch == 1:
                        ins.then_inc(sm["sT"], 1)
                    if ch == 3:
                        ins.then_inc(sm["sI"], 1)
                if i >= 1:
                    trans_batch(i - 1, 0)
                emit_band()
                emit_band()
                if i >= 1:
                    trans_batch(i - 1, 1)
                emit_band()
                if i >= 1 and (i - 1) % 4 == 3:
                    av_group(i - 1)
                emit_band()
                emit_band()
            trans_batch(NIT - 1, 0)
            trans_batch(NIT - 1, 1)
            av_group(NIT - 1)

        @blk.scalar
        def _(s):
            s.dma_start(rkt_sb[0][:, :], rkt_d[0:128, :]).then_inc(dm["Crk0"], 16)
            s.dma_start(rkt_sb[1][:, :], rkt_d[128:256, :]).then_inc(dm["Crk1"], 16)
            s.activation(qpT[:, :], QsT[:, 0:L], AF.Identity,
                         bias=rpbT[:, 0:1], scale=1.0).then_inc(sm["sA"], 1)
            s.activation(qeT[:, :], QsT[:, 0:L], AF.Identity,
                         bias=rebT[:, 0:1], scale=1.0).then_inc(sm["sF"], 1)
            for bi0 in (0, 1, 2):
                for c in SCH:
                    n = bi0 * 5 + c
                    s.wait_ge(sm["sD"], n + 1)
                    s.activation(band_sb[bi0 % 2][:, CH_OFF[c]:CH_OFF[c] + CH_W[c]],
                                 pC_b[n % 2][:, 0:CH_W[c]], AF.Copy).then_inc(sm["sC"], 1)
            for i in range(NIT):
                h, j = divmod(i, NT)
                s.wait_ge(sm["sT"], i + 1)
                if i >= 2:
                    s.wait_ge(sm["sJ"], i - 1)
                    s.wait_ge(sm["sL"], 2 * (i - 1))
                for hf in range(2):
                    if hf == 1:
                        s.wait_ge(sm["sI"], i + 1)
                    nc.scalar.activation(A_sb[i % 2][:, hf * 1024:(hf + 1) * 1024],
                                         pC_S[:, hf * 1024:(hf + 1) * 1024], AF.Exp,
                                         bias=cm20[:, :], scale=1.0,
                                         accum_out=sums_sb[:, 2 * (i % 2) + hf:
                                                           2 * (i % 2) + hf + 1]
                                         ).then_inc(sm["sH"], 1)
                if j == NT - 1 and h < H - 1:
                    s.activation(qeT[:, :], QsT[:, (h + 1) * L:(h + 2) * L], AF.Identity,
                                 bias=rebT[:, h + 1:h + 2], scale=1.0).then_inc(sm["sF"], 1)
                bi = i + 3
                if bi < NIT:
                    for c in SCH:
                        n = bi * 5 + c
                        s.wait_ge(sm["sD"], n + 1)
                        if bi >= 2 and c == SCH[0]:
                            s.wait_ge(dm["Cbo0" if bi % 2 == 0 else "Cbo1"],
                                      16 * (bi // 2))
                        s.activation(band_sb[bi % 2][:, CH_OFF[c]:CH_OFF[c] + CH_W[c]],
                                     pC_b[n % 2][:, 0:CH_W[c]],
                                     AF.Copy).then_inc(sm["sC"], 1)
                if j == NT - 4 and h < H - 1:
                    s.wait_ge(sm["sD"], 5 * NT * (h + 1))
                    s.activation(qpT[:, :], QsT[:, (h + 1) * L:(h + 2) * L], AF.Identity,
                                 bias=rpbT[:, h + 1:h + 2], scale=1.0).then_inc(sm["sA"], 1)
                    if h + 2 < H:
                        s.dma_start(rkt_sb[(h + 2) % 2][:, :],
                                    rkt_d[(h + 2) * 128:(h + 3) * 128, :]
                                    ).then_inc(dm["Crk0" if h % 2 == 0 else "Crk1"], 16)

        @blk.vector
        def _(v):
            for bi0 in (0, 1, 2):
                for c in VCH:
                    n = bi0 * 5 + c
                    v.wait_ge(sm["sD"], n + 1)
                    v.tensor_copy(band_sb[bi0 % 2][:, CH_OFF[c]:CH_OFF[c] + CH_W[c]],
                                  pC_b[n % 2][:, 0:CH_W[c]]).then_inc(sm["sB"], 1)

            def at_evac(jj, half):
                v.wait_ge(sm["sL"], 2 * jj + half + 1)
                if half == 0 and jj >= 4:
                    v.wait_ge(sm["sP"], jj // 4)
                dst = bass.AP(AT4, half * 8 * 512 + (jj % 4) * 128,
                              [[16 * 512, 128], [512, 8], [1, 128]])
                srcp = bass.AP(pC_at, 0, [[1024, 128], [128, 8], [1, 128]])
                v.tensor_copy(dst, srcp).then_inc(sm["sK"], 1)

            for i in range(NIT):
                jj = i - 1
                bi = i + 3

                def bevac(c):
                    n = bi * 5 + c
                    v.wait_ge(sm["sD"], n + 1)
                    if bi >= 2 and c == 0:
                        v.wait_ge(dm["Cbo0" if bi % 2 == 0 else "Cbo1"],
                                  16 * (bi // 2))
                    v.tensor_copy(band_sb[bi % 2][:, CH_OFF[c]:CH_OFF[c] + CH_W[c]],
                                  pC_b[n % 2][:, 0:CH_W[c]]).then_inc(sm["sB"], 1)

                if jj >= 0:
                    at_evac(jj, 0)
                if bi < NIT:
                    bevac(0)
                if jj >= 0:
                    at_evac(jj, 1)
                if bi < NIT:
                    bevac(1)
                jj2 = i - 2
                if jj2 >= 3 and jj2 % 4 == 3:
                    g = jj2 // 4
                    v.wait_ge(sm["sP"], g + 1)
                    v.tensor_copy(coT[:, (g // 4) * L + (g % 4) * 512:
                                      (g // 4) * L + (g % 4) * 512 + 512],
                                  pC_av[:, :]).then_inc(sm["sM"], 1)
                v.wait_ge(sm["sH"], 2 * (i + 1))
                ins = v.tensor_add(tiny[:, 14:15], sums_sb[:, 2 * (i % 2):2 * (i % 2) + 1],
                                   sums_sb[:, 2 * (i % 2) + 1:2 * (i % 2) + 2])
                ins.then_inc(sm["vch"], 1)
                v.wait_ge(sm["vch"], 2 * i + 1)
                ins = v.reciprocal(tiny[:, 12 + i % 2:13 + i % 2], tiny[:, 14:15])
                ins.then_inc(sm["vch"], 1)
                v.wait_ge(sm["vch"], 2 * i + 2)
                v.tensor_scalar(A_sb[i % 2][:, :], A_sb[i % 2][:, :],
                                tiny[:, 12 + i % 2:13 + i % 2], None,
                                ALU.mult).then_inc(sm["sJ"], 1)
            jj = NIT - 1
            at_evac(jj, 0)
            at_evac(jj, 1)
            g = NIT // 4 - 1
            v.wait_ge(sm["sP"], g + 1)
            v.tensor_copy(coT[:, (g // 4) * L + (g % 4) * 512:
                              (g // 4) * L + (g % 4) * 512 + 512],
                          pC_av[:, :]).then_inc(sm["sM"], 1)

        @blk.gpsimd
        def _(gp):
            for bi in (0, 1, 2):
                gp.wait_ge(sm["sB"], 2 * (bi + 1))
                gp.wait_ge(sm["sC"], 3 * (bi + 1))
                if bi >= 2:
                    gp.wait_ge(dm["Cpi0" if bi % 2 == 0 else "Cpi1"],
                               16 * (bi // 2))
                gp.dma_start(bounce_d[bi % 2][:, :],
                             band_sb[bi % 2][:, :]
                             ).then_inc(dm["Cbo0" if bi % 2 == 0 else "Cbo1"], 16)
            for i in range(NIT):
                bi = i + 3
                if bi < NIT:
                    gp.wait_ge(sm["sB"], 2 * (bi + 1))
                    gp.wait_ge(sm["sC"], 3 * (bi + 1))
                    gp.wait_ge(dm["Cpi0" if bi % 2 == 0 else "Cpi1"],
                               16 * (bi // 2))
                    gp.dma_start(bounce_d[bi % 2][:, :],
                                 band_sb[bi % 2][:, :]
                                 ).then_inc(dm["Cbo0" if bi % 2 == 0 else "Cbo1"], 16)

        @blk.sync
        def _(sy):
            for bi in range(NIT):
                sy.wait_ge(dm["Cbo0" if bi % 2 == 0 else "Cbo1"], 16 * (bi // 2 + 1))
                if bi >= 2:
                    sy.wait_ge(sm["sI"], bi - 1)
                sy.dma_start(P_sb[bi % 2][:, :],
                             bass.AP(bounce_d[bi % 2], 127, [[W - 1, 128], [1, L]])
                             ).then_inc(dm["Cpi0" if bi % 2 == 0 else "Cpi1"], 16)

    esC.close()
    if max_phase <= 8:
        es_bc.close()
        es_persist.close()
        es_sem.close()
        return nc


    esS = ExitStack()
    woutn = esS.enter_context(nc.sbuf_tensor("woutn", [128, ND * D], BF16))
    s_st = [esS.enter_context(nc.sbuf_tensor(f"s_st{i}", [128, D], BF16)) for i in range(2)]
    pS = esS.enter_context(nc.psum_tensor("pS", [128, D], F32))

    with nc.Block() as blk:
        @blk.gpsimd
        def _(gp):
            src = bass.AP(wout_d, 0, [[D, 128], [128 * D, ND], [1, D]])
            gp.dma_start(bass.AP(woutn, 0, [[ND * D, 128], [D, ND], [1, D]]),
                         src).then_inc(dm["Swo"], 16)
            for tt in range(NT):
                gp.wait_ge(sm["sW"], tt + 1)
                gp.dma_start(s_dram[tt * 128:(tt + 1) * 128, :],
                             s_st[tt % 2][:, :]).then_inc(dm["Sst0" if tt % 2 == 0 else "Sst1"], 16)
            gp.wait_ge(dm["Sst0"], 16 * (NT // 2))
            gp.wait_ge(dm["Sst1"], 16 * (NT // 2))

        @blk.tensor
        def _(t):
            t.wait_ge(dm["Swo"], 16)
            for tt in range(NT):
                if tt >= 1:
                    t.wait_ge(sm["sW"], tt)
                for ch in range(2):
                    for hv in range(ND):
                        t.matmul(pS[:, ch * 512:(ch + 1) * 512],
                                 coT[:, hv * L + tt * 128: hv * L + (tt + 1) * 128],
                                 woutn[:, hv * D + ch * 512: hv * D + ch * 512 + 512],
                                 start=(hv == 0), stop=False)
                    ins = t.matmul(pS[:, ch * 512:(ch + 1) * 512], ones_row[:, :],
                                   bout_bf[0:1, ch * 512:(ch + 1) * 512],
                                   start=False, stop=True)
                    if ch == 1:
                        ins.then_inc(sm["sR"], 1)

        @blk.scalar
        def _(s):
            for tt in range(NT):
                s.wait_ge(sm["sR"], tt + 1)
                if tt >= 2:
                    s.wait_ge(dm["Sst0" if tt % 2 == 0 else "Sst1"], 16 * (tt // 2))
                s.activation(s_st[tt % 2][:, :], pS[:, :], AF.Copy).then_inc(sm["sW"], 1)

    esS.close()
    es_bc.close()
    clear_block()
    if max_phase <= 9:
        if debug_dumps:
            dump_block([(0, s_dram[:, :], L * D)])
        es_persist.close()
        es_sem.close()
        return nc


    esD = ExitStack()
    wiT = esD.enter_context(nc.sbuf_tensor("wiT", [128, ND * F], BF16))
    woT = esD.enter_context(nc.sbuf_tensor("woT", [128, NF * D], BF16))
    xD_t = [esD.enter_context(nc.sbuf_tensor(f"xD_t{i}", [128, D], F32)) for i in range(2)]
    xnD = [esD.enter_context(nc.sbuf_tensor(f"xnD{i}", [128, D], BF16)) for i in range(2)]
    xnT_t = [esD.enter_context(nc.sbuf_tensor(f"xnT_t{i}", [128, D], BF16)) for i in range(2)]
    sD_t = [esD.enter_context(nc.sbuf_tensor(f"sD_t{i}", [128, D], BF16)) for i in range(2)]
    ipm_t = [esD.enter_context(nc.sbuf_tensor(f"ipm_t{i}", [32, E * 128], BF16)) for i in range(2)]
    btp_sb = [esD.enter_context(nc.sbuf_tensor(f"btp_sb{i}", [32, 1024], BF16)) for i in range(4)]
    rA = [esD.enter_context(nc.sbuf_tensor(f"rA{i}", [128, 512], BF16)) for i in range(2)]
    rB = [esD.enter_context(nc.sbuf_tensor(f"rB{i}", [128, 512], BF16)) for i in range(2)]
    mixn = esD.enter_context(nc.sbuf_tensor("mixn", [128, 1024], BF16))
    mixT_sb = esD.enter_context(nc.sbuf_tensor("mixT_sb", [128, 1024], BF16))
    gD = esD.enter_context(nc.sbuf_tensor("gD", [128, D], F32))
    tmpD = esD.enter_context(nc.sbuf_tensor("tmpD", [128, D], F32))
    outD = [esD.enter_context(nc.sbuf_tensor(f"outD{i}", [128, D], F32)) for i in range(2)]

    bi_bf = esD.enter_context(nc.sbuf_tensor("bi_bf", [1, F], BF16))
    bo_bf = esD.enter_context(nc.sbuf_tensor("bo_bf", [1, D], BF16))
    pD_z = esD.enter_context(nc.psum_tensor("pD_z", [128, 1024], F32))
    pD_mx = esD.enter_context(nc.psum_tensor("pD_mx", [128, 1024], F32))
    pD_mt = esD.enter_context(nc.psum_tensor("pD_mt", [128, 1024], BF16))
    pD_f = esD.enter_context(nc.psum_tensor("pD_f", [128, D], F32))
    pD_xn = esD.enter_context(nc.psum_tensor("pD_xn", [128, 1024], BF16))

    NQ = 4

    with nc.Block() as blk:
        @blk.gpsimd
        def _(gp):
            src = bass.AP(wiT_d, 0, [[F, 128], [128 * F, ND], [1, F]])
            gp.dma_start(bass.AP(wiT, 0, [[ND * F, 128], [F, ND], [1, F]]),
                         src).then_inc(dm["DwiT"], 16)
            gp.dma_start(bi_bf[:, :], bi_d[:, :]).then_inc(dm["DwiT"], 16)
            gp.dma_start(bo_bf[:, :], bo_d[:, :]).then_inc(dm["DwiT"], 16)
            nb = 0
            BTS = ["Dbt0", "Dbt1", "Dbt2", "Dbt3"]
            for tt in range(NT):
                for qtr in range(NQ):
                    for e in range(E):
                        if nb >= 4:
                            gp.wait_ge(sm["sM"], 2 * (nb - 3))
                        gp.dma_start(btp_sb[nb % 4][:, :],
                                     bass.AP(btp_d, e * 32 * F + qtr * 1024,
                                             [[F, 32], [1, 1024]])
                                     ).then_inc(dm[BTS[nb % 4]], 16)
                        nb += 1


        @blk.sync
        def _(sy):
            src = bass.AP(woT_d, 0, [[D, 128], [128 * D, NF], [1, D]])
            sy.dma_start(bass.AP(woT, 0, [[NF * D, 128], [D, NF], [1, D]]),
                         src).then_inc(dm["DwoT"], 16)
            for tt in range(NT):
                if tt >= 2:
                    sy.wait_ge(sm["sD"], tt - 1)
                sy.dma_start(xD_t[tt % 2][:, :],
                             x_d[tt * 128:(tt + 1) * 128, :]
                             ).then_inc(dm["Dxs0" if tt % 2 == 0 else "Dxs1"], 16)
                sy.dma_start(sD_t[tt % 2][:, :],
                             s_dram[tt * 128:(tt + 1) * 128, :]
                             ).then_inc(dm["Dxs0" if tt % 2 == 0 else "Dxs1"], 16)
                if tt >= 2:
                    sy.wait_ge(sm["sP"], tt - 1)
                sy.dma_start(bass.AP(ipm_t[tt % 2], 0, [[E * 128, 32], [128, E], [1, 128]]),
                             bass.AP(intpm_d, tt * 128,
                                     [[L, 32], [32 * L, E], [1, 128]])
                             ).then_inc(dm["Dip0" if tt % 2 == 0 else "Dip1"], 16)
                if tt >= 1:
                    sy.wait_ge(sm["sD"], tt)
                    sy.dma_start(out_d[(tt - 1) * 128:tt * 128, :],
                                 outD[(tt - 1) % 2][:, :]
                                 ).then_inc(dm["Dout0" if (tt - 1) % 2 == 0 else "Dout1"], 16)
            sy.wait_ge(sm["sD"], NT)
            sy.dma_start(out_d[(NT - 1) * 128:NT * 128, :],
                         outD[(NT - 1) % 2][:, :]
                         ).then_inc(dm["Dout0" if (NT - 1) % 2 == 0 else "Dout1"], 16)
            sy.wait_ge(dm["Dout0"], 16 * (NT // 2))
            sy.wait_ge(dm["Dout1"], 16 * (NT // 2))

        @blk.scalar
        def _(s):
            for tt in range(NT):
                s.wait_ge(dm["Dxs0" if tt % 2 == 0 else "Dxs1"], 32 * (tt // 2 + 1))
                if tt >= 2:
                    s.wait_ge(sm["sK"], tt - 1)
                nc.scalar.activation(xnD[tt % 2][:, :], xD_t[tt % 2][:, :], AF.Identity,
                                     bias=stats[:, NT + tt:NT + tt + 1],
                                     scale=stats[:, tt:tt + 1]).then_inc(sm["sL"], 1)
                if tt >= 1:
                    if tt >= 2:
                        s.wait_ge(sm["sD"], tt - 1)
                    s.wait_ge(sm["sP"], tt)
                    nc.scalar.activation(gD[:, :], pD_f[:, :], AF.Silu).then_inc(sm["sQ"], 1)
                for qtr in range(NQ):
                    for e in range(E):
                        idx = (tt * NQ + qtr) * E + e
                        s.wait_ge(sm["sM"], 2 * idx + 1)
                        if idx >= 2:
                            s.wait_ge(sm["sN"], idx - 1)
                        nc.scalar.activation(rA[idx % 2][:, :], pD_z[:, 0:512], AF.Relu,
                                             bias=0.0,
                                             scale=G_sb[:, tt * E + e: tt * E + e + 1]
                                             ).then_inc(sm["sO"], 1)
                if tt == NT - 1:
                    s.wait_ge(sm["sD"], tt)
                    s.wait_ge(sm["sP"], tt + 1)
                    nc.scalar.activation(gD[:, :], pD_f[:, :], AF.Silu).then_inc(sm["sQ"], 1)

        @blk.vector
        def _(v):
            for tt in range(NT):
                v.wait_ge(sm["sK"], tt + 1)
                if tt >= 2:
                    v.wait_ge(sm["sP"], tt - 1)
                v.tensor_copy(xnT_t[tt % 2][:, :], pD_xn[:, :]).then_inc(sm["sR"], 1)
                if tt >= 1:
                    ttp = tt - 1
                    v.wait_ge(sm["sQ"], ttp + 1)
                    v.wait_ge(dm["Dxs0" if ttp % 2 == 0 else "Dxs1"], 32 * (ttp // 2 + 1))
                    if ttp >= 2:
                        v.wait_ge(dm["Dout0" if ttp % 2 == 0 else "Dout1"], 16 * (ttp // 2))
                    ins = v.tensor_mul(tmpD[:, :], sD_t[ttp % 2][:, :], gD[:, :])
                    ins.then_inc(sm["vch"], 1)
                    v.wait_ge(sm["vch"], ttp + 1)
                    v.tensor_add(outD[ttp % 2][:, :], tmpD[:, :],
                                 xD_t[ttp % 2][:, :]).then_inc(sm["sD"], 1)
                for qtr in range(NQ):
                    qi = tt * NQ + qtr
                    for e in range(E):
                        idx = qi * E + e
                        v.wait_ge(sm["sM"], 2 * idx + 2)
                        if idx >= 2:
                            v.wait_ge(sm["sX"], idx - 1)
                        v.tensor_scalar(rB[idx % 2][:, :], pD_z[:, 512:1024],
                                        G_sb[:, tt * E + e: tt * E + e + 1], 0.0,
                                        ALU.mult, ALU.max).then_inc(sm["sS"], 1)
                    v.wait_ge(sm["sX"], 8 * (qi + 1))
                    v.tensor_copy(mixn[:, :], pD_mx[:, :]).then_inc(sm["sU"], 1)
                    v.wait_ge(sm["sV"], qi + 1)
                    v.tensor_copy(mixT_sb[:, :], pD_mt[:, :]).then_inc(sm["sW"], 1)
                if tt == NT - 1:
                    v.wait_ge(sm["sQ"], tt + 1)
                    v.wait_ge(dm["Dxs0" if tt % 2 == 0 else "Dxs1"], 32 * (tt // 2 + 1))
                    v.wait_ge(dm["Dout0" if tt % 2 == 0 else "Dout1"], 16 * (tt // 2))
                    ins = v.tensor_mul(tmpD[:, :], sD_t[tt % 2][:, :], gD[:, :])
                    ins.then_inc(sm["vch"], 1)
                    v.wait_ge(sm["vch"], tt + 1)
                    v.tensor_add(outD[tt % 2][:, :], tmpD[:, :],
                                 xD_t[tt % 2][:, :]).then_inc(sm["sD"], 1)

        @blk.tensor
        def _(t):
            t.wait_ge(dm["DwiT"], 48)
            t.wait_ge(dm["DwoT"], 16)
            pend = []

            def emit_down(k):
                for _ in range(k):
                    if not pend:
                        return
                    pend.pop(0)()

            def push_down(tt, qtr):
                qi = tt * NQ + qtr

                def first_waits(tt=tt, qtr=qtr, qi=qi):
                    t.wait_ge(sm["sW"], qi + 1)
                    if qtr == 0:
                        t.wait_ge(sm["sQ"], tt)
                first = [True]
                for ft in range(8):
                    for ch in range(2):
                        def job(tt=tt, qtr=qtr, ft=ft, ch=ch, first=first):
                            if first[0]:
                                first_waits()
                                first[0] = False
                            ftg = qtr * 8 + ft
                            t.matmul(pD_f[:, ch * 512:(ch + 1) * 512],
                                     mixT_sb[:, ft * 128:(ft + 1) * 128],
                                     woT[:, ftg * D + ch * 512: ftg * D + ch * 512 + 512],
                                     start=(qtr == 0 and ft == 0), stop=False)
                        pend.append(job)
                if qtr == NQ - 1:
                    def bo_job(tt=tt):
                        for ch in range(2):
                            ins = t.matmul(pD_f[:, ch * 512:(ch + 1) * 512],
                                           gsumrow[0:1, tt * 128:(tt + 1) * 128],
                                           bo_bf[0:1, ch * 512:(ch + 1) * 512],
                                           start=False, stop=True)
                            if ch == 1:
                                ins.then_inc(sm["sP"], 1)
                    pend.append(bo_job)

            def mix_transposes(tt, qtr):
                qi = tt * NQ + qtr
                t.wait_ge(sm["sU"], qi + 1)
                if qi >= 1:
                    t.wait_ge(sm["sW"], qi)
                for ft in range(8):
                    ins = nc.tensor.transpose(pD_mt[:, ft * 128:(ft + 1) * 128],
                                              mixn[:, ft * 128:(ft + 1) * 128],
                                              ident_bf[:, :])
                    if ft == 7:
                        ins.then_inc(sm["sV"], 1)
                push_down(tt, qtr)

            prev_q = [None]
            for tt in range(NT):
                t.wait_ge(sm["sL"], tt + 1)
                if tt >= 2:
                    t.wait_ge(sm["sR"], tt - 1)
                for ct in range(ND):
                    ins = nc.tensor.transpose(pD_xn[:, ct * 128:(ct + 1) * 128],
                                              xnD[tt % 2][:, ct * 128:(ct + 1) * 128],
                                              ident_bf[:, :])
                    if ct == ND - 1:
                        ins.then_inc(sm["sK"], 1)
                if prev_q[0] is not None:
                    mix_transposes(*prev_q[0])
                    prev_q[0] = None
                    emit_down(len(pend))
                t.wait_ge(sm["sR"], tt + 1)
                t.wait_ge(dm["Dip0" if tt % 2 == 0 else "Dip1"], 16 * (tt // 2 + 1))
                for qtr in range(NQ):
                    qi = tt * NQ + qtr
                    for hf in range(2):
                        for dt in range(ND):
                            t.matmul(pD_z[:, hf * 512:(hf + 1) * 512],
                                     xnT_t[tt % 2][:, dt * 128:(dt + 1) * 128],
                                     wiT[:, dt * F + qtr * 1024 + hf * 512:
                                         dt * F + qtr * 1024 + hf * 512 + 512],
                                     start=(dt == 0), stop=False)
                        t.matmul(pD_z[:, hf * 512:(hf + 1) * 512], ones_row[:, :],
                                 bi_bf[0:1, qtr * 1024 + hf * 512: qtr * 1024 + hf * 512 + 512],
                                 start=False, stop=True)
                    if prev_q[0] is not None:
                        mix_transposes(*prev_q[0])
                        prev_q[0] = None
                    for e in range(E):
                        idx = qi * E + e
                        t.wait_ge(dm[["Dbt0", "Dbt1", "Dbt2", "Dbt3"][idx % 4]],
                                  16 * (idx // 4 + 1))
                        for hf in range(2):
                            if e >= 1:
                                t.wait_ge(sm["sO" if hf == 0 else "sS"], idx)
                            t.matmul(pD_z[:, hf * 512:(hf + 1) * 512],
                                     ipm_t[tt % 2][:, e * 128:(e + 1) * 128],
                                     btp_sb[idx % 4][:, hf * 512:(hf + 1) * 512],
                                     start=False, stop=False, skip_group_check=True
                                     ).then_inc(sm["sM"], 1)
                        if e >= 1:
                            ip = idx - 1
                            if e == 1 and qi >= 1:
                                t.wait_ge(sm["sU"], qi)
                            t.matmul(pD_mx[:, 0:512], ident_bf[:, :], rA[ip % 2][:, :],
                                     start=(e == 1), stop=False).then_inc(sm["sN"], 1)
                            t.matmul(pD_mx[:, 512:1024], ident_bf[:, :], rB[ip % 2][:, :],
                                     start=(e == 1), stop=False).then_inc(sm["sX"], 1)
                        emit_down(2)
                    ip = qi * E + E - 1
                    t.wait_ge(sm["sO"], ip + 1)
                    t.wait_ge(sm["sS"], ip + 1)
                    t.matmul(pD_mx[:, 0:512], ident_bf[:, :], rA[ip % 2][:, :],
                             start=False, stop=True).then_inc(sm["sN"], 1)
                    t.matmul(pD_mx[:, 512:1024], ident_bf[:, :], rB[ip % 2][:, :],
                             start=False, stop=True).then_inc(sm["sX"], 1)
                    emit_down(3)
                    prev_q[0] = (tt, qtr)
            mix_transposes(*prev_q[0])
            emit_down(len(pend))

    esD.close()
    es_persist.close()
    es_sem.close()
    return nc


def kernel(**inputs):
    i = inputs
    x = np.ascontiguousarray(i["L0_attn_q"], dtype=np.float32)
    pos = positions_np()
    posT = np.zeros((192, 4096), dtype=ml_dtypes.bfloat16)
    posT[:, :4095] = pos.T.astype(ml_dtypes.bfloat16)

    common = {
        "w_qs": np.ascontiguousarray(i["L0_attn_w_qs"], np.float32),
        "w_ks": np.ascontiguousarray(i["L0_attn_w_ks"], np.float32),
        "w_vs": np.ascontiguousarray(i["L0_attn_w_vs"], np.float32),
        "w_rel": np.ascontiguousarray(i["L0_attn_w_rel_pos_k"], np.float32),
        "reb": np.ascontiguousarray(np.asarray(i["L0_attn_rel_embed_bias"]).reshape(H, DK), np.float32),
        "rpb": np.ascontiguousarray(np.asarray(i["L0_attn_rel_pos_bias"]).reshape(H, DK), np.float32),
        "w_out": np.ascontiguousarray(i["L0_attn_w_out"], np.float32),
        "b_out": np.ascontiguousarray(np.asarray(i["L0_attn_b_out"]).reshape(1, D), np.float32),
        "wi": np.ascontiguousarray(i["L0_ffn_wi"], np.float32),
        "bi": np.ascontiguousarray(np.asarray(i["L0_ffn_bi"]).reshape(1, F), np.float32),
        "wo": np.ascontiguousarray(i["L0_ffn_wo"], np.float32),
        "bo": np.ascontiguousarray(np.asarray(i["L0_ffn_bo"]).reshape(1, D), np.float32),
        "lora_A": np.ascontiguousarray(np.asarray(i["L0_ffn_lora_A"]).reshape(E * 16, D), np.float32),
        "lora_B": np.ascontiguousarray(np.asarray(i["L0_ffn_lora_B"]).reshape(E * F, 16), np.float32),
        "gate_w": np.ascontiguousarray(i["L0_ffn_gate_w"], np.float32),
        "gate_b": np.ascontiguousarray(np.asarray(i["L0_ffn_gate_b"]).reshape(1, E), np.float32),
        "posT": posT,
    }
    nc = build()
    in_maps = [dict(common, x=np.ascontiguousarray(x[b])) for b in range(8)]
    res = run_bass_kernel_spmd(nc, in_maps, core_ids=list(range(8)))
    return np.stack([np.asarray(res.results[b]["out"], np.float32) for b in range(8)], axis=0)


